# version 05
# baseline (speedup 1.0000x reference)
import contextlib
import os

import numpy as np

import concourse.bass as bass
import concourse.mybir as mybir
from concourse.bass_utils import run_bass_kernel_spmd

F32 = mybir.dt.float32
BF16 = mybir.dt.bfloat16
AF = mybir.ActivationFunctionType
ALU = mybir.AluOpType
AX = mybir.AxisListType

B, S, D = 16, 2048, 1024
NSEQ = 2
NCH = 8
NTC = 16
NTB = 4
TWO_PI = float(2.0 * np.pi)
NEG = -1.0e30
EPS = 1e-5

SUBLAYERS = tuple(os.environ.get("KSUB", "attn_a,ffn1,attn_b,ffn2").split(","))
KSTAGE = int(os.environ.get("KSTAGE", "4"))

ENGINES = ("tensor", "scalar", "vector", "gpsimd", "sync")
N_DMA_SEMS = 12


class Prog:
    def __init__(self, nc):
        self.nc = nc
        self.ops = {e: [] for e in ENGINES}
        self.cum = {e: 0 for e in ENGINES}
        self.known = {e: {} for e in ENGINES}
        self.last_w = {}
        self.readers = {}
        self.dma_cum = [0] * N_DMA_SEMS
        self.dma_rr = 0
        self.devnull = None
        self.n_waits = 0

    def _need_wait(self, eng, skey, val):
        k = self.known[eng]
        if k.get(skey, 0) >= val:
            return None
        k[skey] = val
        return (skey, val)

    def _deps(self, eng, reads, writes):
        waits = []
        me = ("e", eng)
        for r in reads:
            lw = self.last_w.get(r)
            if lw is not None and lw[0] != me:
                w = self._need_wait(eng, lw[0], lw[1])
                if w:
                    waits.append(w)
        for r in writes:
            lw = self.last_w.get(r)
            if lw is not None and lw[0] != me:
                w = self._need_wait(eng, lw[0], lw[1])
                if w:
                    waits.append(w)
            for skey, v in self.readers.get(r, {}).items():
                if skey != me:
                    w = self._need_wait(eng, skey, v)
                    if w:
                        waits.append(w)
        self.n_waits += len(waits)
        return waits

    def _commit(self, key, reads, writes, val):
        for r in reads:
            self.readers.setdefault(r, {})[key] = val
        for r in writes:
            self.last_w[r] = (key, val)
            self.readers[r] = {}

    def op(self, eng, emit, reads=(), writes=(), small=()):
        waits = self._deps(eng, reads, writes)
        self.cum[eng] += 1
        my = self.cum[eng]
        self.ops[eng].append((waits, emit, ("e", eng), 1))
        self._commit(("e", eng), reads, writes, my)
        if small:
            aps = list(small)
            self.cum[eng] += 1
            tv = self.cum[eng]

            def temit(E, _aps=aps, _eng=eng):
                ins = None
                for ap in _aps:
                    pdim, f = ap.shape[0], ap.free_size()
                    if f > 512:
                        ap = ap[:, 0:512] if ap.ndim == 2 else ap
                        f = 512
                    dn = self.devnull[_eng]
                    if _eng == "scalar":
                        ins = E.copy(dn[0:pdim, 0:f], ap)
                    else:
                        ins = E.tensor_copy(dn[0:pdim, 0:f], ap)
                return ins

            self.ops[eng].append(([], temit, ("e", eng), 1))
            for r in writes:
                self.last_w[r] = (("e", eng), tv)
        return my

    def dma(self, out_ap, in_ap, reads=(), writes=(), eng="sync", slow_ok=False):
        waits = self._deps(eng, reads, writes)
        si = self.dma_rr
        self.dma_rr = (self.dma_rr + 1) % N_DMA_SEMS
        prev = self.dma_cum[si]
        self.dma_cum[si] += 16
        val = self.dma_cum[si]
        key = ("d", si)
        if prev > 0:
            w = self._need_wait(eng, key, prev)
            if w:
                waits.append(w)

        def emit(E, _o=out_ap, _i=in_ap, _s=slow_ok):
            return E.dma_start(_o, _i, allow_slow_non_contiguous=_s)

        self.ops[eng].append((waits, emit, key, 16))
        self._commit(key, reads, writes, val)
        return key, val

    def barrier(self):
        targets = [(("e", e), self.cum[e]) for e in ENGINES if self.cum[e] > 0]
        targets += [(("d", i), self.dma_cum[i]) for i in range(N_DMA_SEMS)
                    if self.dma_cum[i] > 0]
        for e in ENGINES:
            waits = []
            for skey, v in targets:
                if skey == ("e", e):
                    continue
                w = self._need_wait(e, skey, v)
                if w:
                    waits.append(w)
            if waits:
                self.ops[e].append((waits, None, None, 0))

    def finalize_and_emit(self, stack):
        nc = self.nc
        esem = {e: stack.enter_context(nc.semaphore("sem_" + e)) for e in ENGINES}
        dsem = [stack.enter_context(nc.semaphore(f"dma{i}")) for i in range(N_DMA_SEMS)]

        def sem_of(skey):
            return esem[skey[1]] if skey[0] == "e" else dsem[skey[1]]

        waits = []
        for i in range(N_DMA_SEMS):
            if self.dma_cum[i] > 0:
                waits.append((("d", i), self.dma_cum[i]))
        for e in ENGINES:
            if e != "sync" and self.cum[e] > 0:
                waits.append((("e", e), self.cum[e]))
        self.ops["sync"].append((waits, None, None, 0))

        with nc.Block() as block:
            for ename in ENGINES:
                def section(E, _ops=self.ops[ename]):
                    for w, emit, skey, amt in _ops:
                        for wk, wv in w:
                            E.wait_ge(sem_of(wk), wv)
                        if emit is not None:
                            ins = emit(E)
                            if skey is not None:
                                ins.then_inc(sem_of(skey), amt)
                getattr(block, ename)(section)


class KB:
    def __init__(self):
        self.nc = bass.Bass("TRN2", target_bir_lowering=False)
        self.p = Prog(self.nc)
        self.stack = contextlib.ExitStack()
        self._uid = 0

    def sbuf(self, name, shape, dt=F32, stack=None):
        self._uid += 1
        return (stack or self.stack).enter_context(
            self.nc.sbuf_tensor(f"{name}_u{self._uid}", shape, dt))

    def psum(self, name, shape, dt=F32):
        return self.stack.enter_context(self.nc.psum_tensor(name, shape, dt))

    def scope(self):
        return contextlib.ExitStack()

    def gfence(self, items):
        if not hasattr(self, "_fence_f32"):
            self._fence_f32 = self.nc.dram_tensor("fence_f32", [128, 4096], F32)
            self._fence_b16 = self.nc.dram_tensor("fence_b16", [128, 8192], BF16)
        for ap, res in items:
            tgt = self._fence_b16 if ap.dtype == BF16 else self._fence_f32
            f = ap.free_size()
            pdim = ap.shape[0]
            self.p.dma(tgt[0:pdim, 0:f], ap, reads=[res], writes=[res], slow_ok=(f < 8))

    def dump(self, name, ap, reads):
        t = self.nc.dram_tensor("dbg_" + name, list(ap.shape), ap.dtype,
                                kind="ExternalOutput")
        self.p.dma(t[...], ap, reads=reads)

    def register_consts(self, vals):
        vals = sorted(set(float(v) for v in vals))
        self.cpool = self.sbuf("cpool", [128, max(1, len(vals))], F32)
        for j, v in enumerate(vals):
            self.nc.const_aps.register(F32, v, self.cpool[:, j:j + 1])
        n = len(vals)
        for j, v in enumerate(vals):
            self.p.op("vector",
                      lambda E, _v=v, _j=j: E.memset(self.cpool[:, _j:_j + 1], _v),
                      writes=[("cpool",)],
                      small=([self.cpool[:, 0:n]] if j == n - 1 else ()))


def build(kb):
    nc, p = kb.nc, kb.p

    x_in = nc.dram_tensor("x_in", [NSEQ, S, D], F32, kind="ExternalInput")
    out = nc.dram_tensor("out", [NSEQ, S, D], F32, kind="ExternalOutput")
    dr = {}
    dr["aw_qkv"] = nc.dram_tensor("aw_qkv", [D, 3 * D], F32, kind="ExternalInput")
    dr["ab_qkv"] = nc.dram_tensor("ab_qkv", [3 * D], F32, kind="ExternalInput")
    dr["aw_out"] = nc.dram_tensor("aw_out", [D, D], F32, kind="ExternalInput")
    dr["ab_out"] = nc.dram_tensor("ab_out", [D], F32, kind="ExternalInput")
    dr["f_wqkv"] = nc.dram_tensor("f_wqkv", [2, D, 1536], F32, kind="ExternalInput")
    dr["f_qs"] = nc.dram_tensor("f_qs", [2, 64], F32, kind="ExternalInput")
    dr["f_ks"] = nc.dram_tensor("f_ks", [2, 64], F32, kind="ExternalInput")
    dr["f_wo"] = nc.dram_tensor("f_wo", [2, 512, D], F32, kind="ExternalInput")
    dr["f_lg"] = nc.dram_tensor("f_lg", [2, D], F32, kind="ExternalInput")
    dr["f_lb"] = nc.dram_tensor("f_lb", [2, D], F32, kind="ExternalInput")
    dr["f_w1"] = nc.dram_tensor("f_w1", [2, D, 5460], F32, kind="ExternalInput")
    dr["f_b1"] = nc.dram_tensor("f_b1", [2, 5460], F32, kind="ExternalInput")
    dr["f_w2"] = nc.dram_tensor("f_w2", [2, 2730, D], F32, kind="ExternalInput")
    dr["f_b2"] = nc.dram_tensor("f_b2", [2, D], F32, kind="ExternalInput")
    h2dram = nc.dram_tensor("h2dram", [22, 128, S], BF16)
    dr["b_womT"] = nc.dram_tensor("b_womT", [D, D], F32, kind="ExternalInput")
    dr["b_bom"] = nc.dram_tensor("b_bom", [D], F32, kind="ExternalInput")
    dr["b_ls"] = nc.dram_tensor("b_ls", [D], F32, kind="ExternalInput")
    dr["b_lg"] = nc.dram_tensor("b_lg", [4096], F32, kind="ExternalInput")
    dr["b_lb"] = nc.dram_tensor("b_lb", [4096], F32, kind="ExternalInput")
    dr["b_w1T"] = nc.dram_tensor("b_w1T", [4096, 2048], F32, kind="ExternalInput")
    dr["b_b1"] = nc.dram_tensor("b_b1", [2048], F32, kind="ExternalInput")
    dr["b_w2T"] = nc.dram_tensor("b_w2T", [2048, D], F32, kind="ExternalInput")
    dr["b_b2"] = nc.dram_tensor("b_b2", [D], F32, kind="ExternalInput")
    ctxdram = nc.dram_tensor("ctxdram", [32, 128, S], BF16)
    rotdram = nc.dram_tensor("rotdram", [4, 128, NTC * 64], F32)
    bw1c = nc.dram_tensor("bw1c", [16, 128, 32 * 128], BF16)
    fw1c = nc.dram_tensor("fw1c", [2, 2, 22, 128, 1024], BF16)
    built_fw1c = set()

    xdram = nc.dram_tensor("xdram", [NSEQ, NCH, 128, S], F32)

    _dn = kb.sbuf("devnull", [128, 512], F32)
    p.devnull = {e: _dn for e in ("vector", "scalar", "gpsimd")}

    kb.register_consts([0.125, 1.0 / D, 1.0 / 4096.0, TWO_PI, 0.25,
                        -1.0, 0.5, 2.0, 8.0, EPS, 1e-12, -0.5,
                        float(-np.log(10000.0) / 32.0), float(np.log(8.0)),
                        float(1.0 / 256.0), 1e-24])

    identb = kb.sbuf("identb", [128, 128], BF16)
    identf = kb.sbuf("identf", [128, 128], F32)
    tri = kb.sbuf("tri", [128, 128], BF16)
    ones_col_b = kb.sbuf("ones_col_b", [128, 1], BF16)
    ones_row_f = kb.sbuf("ones_row_f", [1, 128], F32)
    ones2 = kb.sbuf("ones2", [2, 128], F32)
    rowsbuf = kb.sbuf("rowsbuf", [1, 3 * 512], F32)
    rs2_sb = kb.sbuf("rs2_sb", [2, 512], F32)
    rsbc_sb = kb.sbuf("rsbc_sb", [128, 512], F32)

    def g(emit, writes, reads=()):
        p.op("gpsimd", emit, reads=reads, writes=writes)

    g(lambda E: E.memset(identb[:], 0.0), [("identb",)])
    g(lambda E: E.affine_select(out=identb[:], in_=identb[:], compare_op=ALU.not_equal,
                                fill=1.0, base=0, pattern=[[-1, 128]],
                                channel_multiplier=1), [("identb",)])
    g(lambda E: E.memset(identf[:], 0.0), [("identf",)])
    g(lambda E: E.affine_select(out=identf[:], in_=identf[:], compare_op=ALU.not_equal,
                                fill=1.0, base=0, pattern=[[-1, 128]],
                                channel_multiplier=1), [("identf",)])
    g(lambda E: E.memset(tri[:], 0.0), [("tri",)])
    g(lambda E: E.affine_select(out=tri[:], in_=tri[:], compare_op=ALU.is_ge,
                                fill=NEG, base=0, pattern=[[-1, 128]],
                                channel_multiplier=1), [("tri",)])
    p.op("vector", lambda E: E.memset(ones_col_b[:], 1.0), writes=[("ones_col_b",)],
         small=[ones_col_b[:]])
    p.op("vector", lambda E: E.memset(ones_row_f[:], 1.0), writes=[("ones_row_f",)],
         small=[ones_row_f[:]])
    p.op("vector", lambda E: E.memset(ones2[:], 1.0), writes=[("ones2",)])
    g(lambda E: E.affine_select(out=ones2[:], in_=ones2[:], compare_op=ALU.is_ge,
                                fill=0.0, base=0, pattern=[[1, 128]],
                                channel_multiplier=-64), [("ones2",)])
    g(lambda E: E.affine_select(out=ones2[:], in_=ones2[:], compare_op=ALU.is_ge,
                                fill=0.0, base=63, pattern=[[-1, 128]],
                                channel_multiplier=64), [("ones2",)])

    kb.gfence([(identb[:], ("identb",)), (identf[:], ("identf",)),
               (tri[:], ("tri",)), (ones2[:], ("ones2",))])
    p.barrier()

    rotCq = rotSq = rotCk = rotSk = bandm = None

    def build_rotary():
        nonlocal bandm
        bandm = kb.sbuf("bandm", [128, 640], BF16)
        rotstk = kb.scope()
        rotCq = kb.sbuf("rotCq", [128, NTC * 64], F32, stack=rotstk)
        rotSq = kb.sbuf("rotSq", [128, NTC * 64], F32, stack=rotstk)
        rotCk = kb.sbuf("rotCk", [128, NTC * 64], F32, stack=rotstk)
        rotSk = kb.sbuf("rotSk", [128, NTC * 64], F32, stack=rotstk)
        tcols = kb.sbuf("tcols", [128, NTC], F32, stack=rotstk)
        jrow = kb.sbuf("jrow", [128, 64], F32, stack=rotstk)
        g(lambda E: E.iota(tcols[:], [[128, NTC]], channel_multiplier=1,
                           allow_small_or_imprecise_dtypes=True), [("tcols",)])
        g(lambda E: E.iota(jrow[:].rearrange("p (a b) -> p a b", a=2),
                           [[0, 2], [1, 32]], channel_multiplier=0,
                           allow_small_or_imprecise_dtypes=True), [("jrow",)])
        g(lambda E: E.memset(bandm[:], 0.0), [("bandm",)])
        g(lambda E: E.affine_select(out=bandm[:], in_=bandm[:], compare_op=ALU.is_ge,
                                    fill=NEG, base=0, pattern=[[1, 640]],
                                    channel_multiplier=-1), [("bandm",)])
        g(lambda E: E.affine_select(out=bandm[:], in_=bandm[:], compare_op=ALU.is_ge,
                                    fill=NEG, base=512, pattern=[[-1, 640]],
                                    channel_multiplier=1), [("bandm",)])
        kb.gfence([(tcols[:], ("tcols",)), (jrow[:], ("jrow",)),
                   (bandm[:], ("bandm",))])
        with kb.scope() as stk:
            t0b = kb.sbuf("rt0", [128, 64], F32, stack=stk)
            t1b = kb.sbuf("rt1", [128, 64], F32, stack=stk)
            sinb = kb.sbuf("rtsin", [128, 64], F32, stack=stk)
            cosb = kb.sbuf("rtcos", [128, 64], F32, stack=stk)
            tib = kb.sbuf("rti", [128, 64], mybir.dt.int32, stack=stk)
            invf = kb.sbuf("rinvf", [128, 64], F32, stack=stk)
            lnb = kb.sbuf("rlnb", [128, 64], F32, stack=stk)
            lnb1k = kb.sbuf("rlnb1k", [128, 64], F32, stack=stk)
            scn = kb.sbuf("rscn", [128, 64], F32, stack=stk)
            sc8 = kb.sbuf("rsc8", [128, 64], F32, stack=stk)
            rscn = kb.sbuf("rrscn", [128, 64], F32, stack=stk)
            p.op("scalar", lambda E: E.activation(invf[:], jrow[:], AF.Exp,
                                                  scale=float(-np.log(10000.0) / 32.0)),
                 reads=[("jrow",)], writes=[("invf",)])
            p.op("vector", lambda E: E.tensor_scalar(lnb[:], jrow[:], float(2.0 / 89.6),
                                                     float(25.6 / 89.6), ALU.mult,
                                                     ALU.add),
                 reads=[("jrow",)], writes=[("lnb",)])
            p.op("scalar", lambda E: E.activation(lnb[:], lnb[:], AF.Ln),
                 reads=[("lnb",)], writes=[("lnb",)])
            p.op("vector", lambda E: E.tensor_scalar_mul(lnb1k[:], lnb[:], 1024.0),
                 reads=[("lnb",)], writes=[("lnb1k",)])
            for tc in range(NTC):
                c0 = tc * 64
                tcol = tcols[:, tc:tc + 1]
                p.op("vector", lambda E, _t=tcol: E.tensor_scalar(
                        t0b[:], invf[:], _t, None, ALU.mult),
                     reads=[("invf",), ("tcols",)], writes=[("t0b",)])
                p.op("vector", lambda E: E.tensor_scalar_mul(t1b[:], t0b[:],
                                                             float(1.0 / TWO_PI)),
                     reads=[("t0b",)], writes=[("t1b",)])
                p.op("vector", lambda E: E.tensor_copy(tib[:], t1b[:]),
                     reads=[("t1b",)], writes=[("tib",)])
                p.op("vector", lambda E: E.tensor_copy(t0b[:], tib[:]),
                     reads=[("tib",)], writes=[("t0b",)])
                p.op("vector", lambda E: E.tensor_sub(t1b[:], t1b[:], t0b[:]),
                     reads=[("t0b",), ("t1b",)], writes=[("t1b",)])
                p.op("scalar", lambda E: E.activation(sinb[:], t1b[:], AF.Sin,
                                                      scale=TWO_PI),
                     reads=[("t1b",)], writes=[("sinb",)])
                p.op("vector", lambda E: E.tensor_scalar_add(t1b[:], t1b[:], 0.25),
                     reads=[("t1b",)], writes=[("t1b",)])
                p.op("vector", lambda E: E.tensor_scalar(t0b[:], t1b[:], 0.5, 1.0,
                                                         ALU.is_gt, ALU.mult),
                     reads=[("t1b",)], writes=[("t0b",)])
                p.op("vector", lambda E: E.scalar_tensor_tensor(
                        t1b[:], t0b[:], -1.0, t1b[:], ALU.mult, ALU.add),
                     reads=[("t0b",), ("t1b",)], writes=[("t1b",)])
                p.op("scalar", lambda E: E.activation(cosb[:], t1b[:], AF.Sin,
                                                      scale=TWO_PI),
                     reads=[("t1b",)], writes=[("cosb",)])
                p.op("vector", lambda E, _t=tcol: E.scalar_tensor_tensor(
                        t0b[:], lnb[:], _t, lnb1k[:], ALU.mult, ALU.subtract),
                     reads=[("lnb",), ("lnb1k",), ("tcols",)], writes=[("t0b",)])
                p.op("scalar", lambda E: E.activation(scn[:], t0b[:], AF.Exp,
                                                      scale=float(1.0 / 256.0)),
                     reads=[("t0b",)], writes=[("scn",)])
                p.op("scalar", lambda E: E.activation(sc8[:], t0b[:], AF.Exp,
                                                      scale=float(1.0 / 256.0),
                                                      bias=float(np.log(8.0))),
                     reads=[("t0b",)], writes=[("sc8",)])
                p.op("vector", lambda E: E.reciprocal(rscn[:], scn[:]),
                     reads=[("scn",)], writes=[("rscn",)])
                p.op("vector", lambda E, _c=c0: E.tensor_mul(
                        rotCq[:, _c:_c + 64], cosb[:], sc8[:]),
                     reads=[("cosb",), ("sc8",)], writes=[("rot",)])
                p.op("vector", lambda E, _c=c0: E.tensor_mul(
                        rotSq[:, _c:_c + 64], sinb[:], sc8[:]),
                     reads=[("sinb",), ("sc8",)], writes=[("rot",)])
                p.op("vector", lambda E, _c=c0: E.tensor_mul(
                        rotCk[:, _c:_c + 64], cosb[:], rscn[:]),
                     reads=[("cosb",), ("rscn",)], writes=[("rot",)])
                p.op("vector", lambda E, _c=c0: E.tensor_mul(
                        rotSk[:, _c:_c + 64], sinb[:], rscn[:]),
                     reads=[("sinb",), ("rscn",)], writes=[("rot",)])
        for i, t in enumerate((rotCq, rotSq, rotCk, rotSk)):
            p.dma(rotdram[i, :, :], t[:], reads=[("rot",)])
        p.barrier()
        rotstk.close()

    pscore = kb.psum("pscore", [128, 2048], F32)
    pfa = kb.psum("pfa", [128, 512], F32)
    pfb = kb.psum("pfb", [128, 512], F32)
    pfc = kb.psum("pfc", [128, 512], F32)
    ptr = kb.psum("ptr", [128, 1024], BF16)

    def load_col_table(stk, name, vec_ap, nch, scale=1.0):
        stage = kb.sbuf(name + "_st", [nch, 128], F32, stack=stk)
        tbl = kb.sbuf(name, [128, nch], F32, stack=stk)
        p.dma(stage[:], vec_ap.rearrange("(c p) -> c p", p=128), writes=[(name, "st")])
        p.op("tensor",
             lambda E: E.transpose(pfa[:, 0:nch], stage[:], identf[0:nch, 0:nch]),
             reads=[(name, "st"), ("identf",)], writes=[("pfa",)])
        if scale == 1.0:
            p.op("vector", lambda E: E.tensor_copy(tbl[:], pfa[:, 0:nch]),
                 reads=[("pfa",)], writes=[(name,)], small=[tbl[:]])
        else:
            p.op("vector",
                 lambda E, _s=scale: E.tensor_scalar_mul(tbl[:], pfa[:, 0:nch], _s),
                 reads=[("pfa",)], writes=[(name,)], small=[tbl[:]])
        return tbl

    ROWSLOT = {"sum": 0, "mu": 0, "sq": 1, "var": 1, "A": 2, "B": 0}

    def row(kind):
        idx = ROWSLOT[kind]
        return rowsbuf[0:1, idx * 512:(idx + 1) * 512]

    def rres(kind):
        return ("row", ROWSLOT[kind])

    def stage_ln(stk, s, xnb, first, ln_out=None, out_off=0, tmpf=None):
        if first:
            xtm = [kb.sbuf(f"ln_xtm{i}", [128, D], F32, stack=stk) for i in range(4)]
            xfmb = kb.sbuf("ln_xfmb", [128, 512], F32, stack=stk)
        sqb = [kb.sbuf(f"ln_sqb{i}", [128, 512], BF16, stack=stk) for i in range(4)]

        for tb in range(NTB):
            t0 = tb * 512
            if first:
                for t4 in range(4):
                    p.dma(xtm[t4][:],
                          x_in[s, t0 + t4 * 128: t0 + (t4 + 1) * 128, :],
                          writes=[("xtm", t4)])
                for dc in range(NCH):
                    pj = dc % 4
                    for t4 in range(4):
                        p.op("tensor",
                             lambda E, _dc=dc, _t4=t4, _pj=pj:
                             E.transpose(
                                 pscore[:, _pj * 512 + _t4 * 128: _pj * 512 + (_t4 + 1) * 128],
                                 xtm[_t4][:, _dc * 128:(_dc + 1) * 128],
                                 identf[:]),
                             reads=[("xtm", t4), ("identf",)],
                             writes=[("pscore", pj)])
                    p.op("vector",
                         lambda E, _pj=pj: E.tensor_copy(
                             xfmb[:], pscore[:, _pj * 512:(_pj + 1) * 512]),
                         reads=[("pscore", pj)], writes=[("xfmb",)])
                    p.dma(xdram[s, dc, :, t0:t0 + 512], xfmb[:], reads=[("xfmb",)])
                    p.op("scalar",
                         lambda E, _dc=dc, _t0=t0: E.copy(
                             xnb[_dc][:, _t0:_t0 + 512], xfmb[:]),
                         reads=[("xfmb",)], writes=[("xnb", dc, tb)])
            else:
                for dc in range(NCH):
                    p.dma(xnb[dc][:, t0:t0 + 512], xdram[s, dc, :, t0:t0 + 512],
                          writes=[("xnb", dc, tb)], eng="gpsimd")

            for dc in range(NCH):
                p.op("tensor",
                     lambda E, _dc=dc, _t0=t0: E.matmul(
                         pfa[0:1, :], ones_col_b[:], xnb[_dc][:, _t0:_t0 + 512],
                         start=(_dc == 0), stop=(_dc == NCH - 1)),
                     reads=[("xnb", dc, tb), ("ones_col_b",)], writes=[("pfa",)])
            for dc in range(NCH):
                p.op("scalar",
                     lambda E, _dc=dc, _t0=t0: E.activation(
                         sqb[_dc % 4][:], xnb[_dc][:, _t0:_t0 + 512], AF.Square),
                     reads=[("xnb", dc, tb)], writes=[("sqb", dc % 4)])
                p.op("tensor",
                     lambda E, _dc=dc: E.matmul(
                         pfb[0:1, :], ones_col_b[:], sqb[_dc % 4][:],
                         start=(_dc == 0), stop=(_dc == NCH - 1)),
                     reads=[("sqb", dc % 4), ("ones_col_b",)], writes=[("pfb",)])
            p.op("vector", lambda E: E.tensor_scalar_mul(row("mu"), pfa[0:1, :], 1.0 / D),
                 reads=[("pfa",)], writes=[rres("mu")])
            p.op("vector", lambda E: E.tensor_scalar_mul(row("var"), pfb[0:1, :], 1.0 / D),
                 reads=[("pfb",)], writes=[rres("var")])
            p.op("vector", lambda E: E.tensor_mul(row("A"), row("mu"), row("mu")),
                 reads=[rres("mu")], writes=[rres("A")])
            p.op("vector", lambda E: E.tensor_sub(row("var"), row("var"), row("A")),
                 reads=[rres("var"), rres("A")], writes=[rres("var")])
            p.op("vector", lambda E: E.tensor_scalar_add(row("var"), row("var"), EPS),
                 reads=[rres("var")], writes=[rres("var")])
            p.op("scalar", lambda E: E.activation(row("var"), row("var"), AF.Sqrt),
                 reads=[rres("var")], writes=[rres("var")], small=[row("var")])
            p.op("vector", lambda E: E.reciprocal(row("A"), row("var")),
                 reads=[rres("var")], writes=[rres("A")])
            p.op("vector", lambda E: E.tensor_mul(row("B"), row("mu"), row("A")),
                 reads=[rres("mu"), rres("A")], writes=[rres("B")])
            p.op("vector", lambda E: E.tensor_scalar_mul(row("B"), row("B"), -1.0),
                 reads=[rres("B")], writes=[rres("B")],
                 small=[row("B"), row("A")])
            p.op("tensor", lambda E: E.matmul(pfc[:], ones_row_f[:], row("A"),
                                              start=True, stop=True),
                 reads=[rres("A"), ("ones_row_f",)], writes=[("pfc",)])
            p.op("tensor", lambda E: E.matmul(pfa[:], ones_row_f[:], row("B"),
                                              start=True, stop=True),
                 reads=[rres("B"), ("ones_row_f",)], writes=[("pfa",)])
            for dc in range(NCH):
                if ln_out is None:
                    p.op("vector",
                         lambda E, _dc=dc, _t0=t0: E.tensor_tensor(
                             xnb[_dc][:, _t0:_t0 + 512], xnb[_dc][:, _t0:_t0 + 512],
                             pfc[:], ALU.mult),
                         reads=[("xnb", dc, tb), ("pfc",)], writes=[("xnb", dc, tb)])
                    p.op("vector",
                         lambda E, _dc=dc, _t0=t0: E.tensor_tensor(
                             xnb[_dc][:, _t0:_t0 + 512], xnb[_dc][:, _t0:_t0 + 512],
                             pfa[:], ALU.add),
                         reads=[("xnb", dc, tb), ("pfa",)], writes=[("xnb", dc, tb)])
                else:
                    p.op("vector",
                         lambda E, _dc=dc, _t0=t0, _f=tmpf: E.tensor_tensor(
                             _f[:], xnb[_dc][:, _t0:_t0 + 512], pfc[:], ALU.mult),
                         reads=[("xnb", dc, tb), ("pfc",)], writes=[("lntmp",)])
                    p.op("vector",
                         lambda E, _dc=dc, _t0=t0, _f=tmpf, _o=out_off: E.tensor_tensor(
                             ln_out[_dc][:, _o + _t0:_o + _t0 + 512], _f[:],
                             pfa[:], ALU.add),
                         reads=[("lntmp",), ("pfa",)], writes=[("xi", dc, tb)])

    def phase_attn_a(s, first):
        with kb.scope() as stk:
            qf = [kb.sbuf(f"a_qf{c}", [128, S], BF16, stack=stk) for c in range(NCH)]
            kf = [kb.sbuf(f"a_kf{c}", [128, S], BF16, stack=stk) for c in range(NCH)]
            vt = [kb.sbuf(f"a_vt{c}", [128, 1024], BF16, stack=stk) for c in range(NTC)]
            ctx = [kb.sbuf(f"a_ctx{c}", [128, S], BF16, stack=stk) for c in range(NCH)]

            with kb.scope() as stk2:
                xnb = [kb.sbuf(f"a_xnb{c}", [128, S], BF16, stack=stk2)
                       for c in range(NCH)]
                with kb.scope() as stk_ln:
                    stage_ln(stk_ln, s, xnb, first)
                p.barrier()
                if KSTAGE < 2:
                    p.barrier()
                    return
                bq_t = load_col_table(stk2, "bq_t", dr["ab_qkv"][0:D], NCH, scale=0.125)
                bk_t = load_col_table(stk2, "bk_t", dr["ab_qkv"][D:2 * D], NCH)
                bv_bc = kb.sbuf("bv_bc", [128, 1024], BF16, stack=stk2)
                p.dma(bv_bc[:],
                      dr["ab_qkv"][2 * D:3 * D].unsqueeze(0).broadcast_to((128, D)),
                      writes=[("bv_bc",)], eng="gpsimd")

                wstage = [kb.sbuf(f"a_wstage{i}", [128, 512], F32, stack=stk2)
                          for i in range(2)]
                wblk = kb.sbuf("a_wblk", [128, 8 * 512], BF16, stack=stk2)
                for wb in range(6):
                    c0 = wb * 512
                    for dc in range(NCH):
                        ws = wstage[dc % 2]
                        p.dma(ws[:], dr["aw_qkv"][dc * 128:(dc + 1) * 128, c0:c0 + 512],
                              writes=[("wstage", dc % 2)])
                        if wb < 2:
                            p.op("scalar",
                                 lambda E, _dc=dc, _w=ws: E.activation(
                                     wblk[:, _dc * 512:(_dc + 1) * 512], _w[:],
                                     AF.Identity, scale=0.125),
                                 reads=[("wstage", dc % 2)], writes=[("wblk", dc)])
                            if os.environ.get("KDBG") and s == 0 and wb == 0 and dc == 0:
                                kb.dump("wstage0", ws[:], [("wstage", 0)])
                                kb.dump("wblk0", wblk[:, 0:512], [("wblk", 0)])
                        else:
                            p.op("scalar",
                                 lambda E, _dc=dc, _w=ws: E.copy(
                                     wblk[:, _dc * 512:(_dc + 1) * 512], _w[:]),
                                 reads=[("wstage", dc % 2)], writes=[("wblk", dc)])
                    if os.environ.get("KDBG") and s == 0 and wb in (0, 2):
                        kb.dump(f"wblk_w{wb}", wblk[:],
                                [("wblk", d) for d in range(NCH)])
                        if wb == 0:
                            for d in range(NCH):
                                kb.dump(f"xnb_{d}", xnb[d][:, 0:512], [("xnb", d, 0)])
                    if wb < 4:
                        dst, dstn, btab, btn = ((qf, "qf", bq_t, "bq_t") if wb < 2
                                                else (kf, "kf", bk_t, "bk_t"))
                        for j in range(4):
                            qc = (wb % 2) * 4 + j
                            for tb in range(NTB):
                                pj = (j * NTB + tb) % 4
                                for dc in range(NCH):
                                    p.op("tensor",
                                         lambda E, _j=j, _tb=tb, _dc=dc, _pj=pj:
                                         E.matmul(
                                             pscore[:, _pj * 512:(_pj + 1) * 512],
                                             wblk[:, _dc * 512 + _j * 128: _dc * 512 + (_j + 1) * 128],
                                             xnb[_dc][:, _tb * 512:(_tb + 1) * 512],
                                             start=(_dc == 0), stop=(_dc == NCH - 1)),
                                         reads=[("wblk", dc), ("xnb", dc, tb)],
                                         writes=[("pscore", pj)])
                                p.op("scalar",
                                     lambda E, _qc=qc, _tb=tb, _pj=pj, _d=dst, _bt=btab:
                                     E.activation(
                                         _d[_qc][:, _tb * 512:(_tb + 1) * 512],
                                         pscore[:, _pj * 512:(_pj + 1) * 512],
                                         AF.Identity, bias=_bt[:, _qc:_qc + 1]),
                                     reads=[("pscore", pj), (btn,)],
                                     writes=[(dstn, qc)])
                    else:
                        v0 = (wb - 4) * 512
                        for tc in range(NTC):
                            pj = tc % 4
                            for dc in range(NCH):
                                p.op("tensor",
                                     lambda E, _tc=tc, _dc=dc, _pj=pj:
                                     E.matmul(
                                         pscore[:, _pj * 512:(_pj + 1) * 512],
                                         xnb[_dc][:, _tc * 128:(_tc + 1) * 128],
                                         wblk[:, _dc * 512:(_dc + 1) * 512],
                                         start=(_dc == 0), stop=(_dc == NCH - 1)),
                                     reads=[("wblk", dc), ("xnb", dc, tc // 4)],
                                     writes=[("pscore", pj)])
                            p.op("vector",
                                 lambda E, _tc=tc, _pj=pj, _v0=v0: E.tensor_tensor(
                                     vt[_tc][:, _v0:_v0 + 512],
                                     pscore[:, _pj * 512:(_pj + 1) * 512],
                                     bv_bc[:, _v0:_v0 + 512], ALU.add),
                                 reads=[("pscore", pj), ("bv_bc",)],
                                 writes=[("vt", tc, wb - 4)])
                p.barrier()

            if KSTAGE < 3:
                p.barrier()
                return
            with kb.scope() as stk3:
                asb = [kb.sbuf(f"a_asb{i}", [128, S], BF16, stack=stk3) for i in range(4)]
                aT = [[kb.sbuf(f"a_aT{h}_{k}", [128, 512], BF16, stack=stk3)
                       for k in range(NTC)] for h in range(2)]
                rs_pair = [kb.sbuf(f"a_rs{i}", [128, 2], F32, stack=stk3)
                           for i in range(4)]
                scd = kb.sbuf("a_scd", [128, 128], F32, stack=stk3)
                sums = kb.sbuf("a_sums", [128, 8], F32, stack=stk3)

                for qg in range(4):
                    nkg = 4 * qg + 4
                    for hp in range(8):
                        for h2 in range(2):
                            head = hp * 2 + h2
                            hc, hb = head // 2, (head % 2) * 64
                            for qr in range(4):
                                qabs = qg * 4 + qr
                                kcols = (qabs + 1) * 128
                                nblk = (kcols + 511) // 512
                                if qg == 0:
                                    pol = qr * 512
                                elif qg == 1:
                                    pol = (qr % 2) * 1024
                                else:
                                    pol = 0
                                pb = pol // 512
                                for j in range(nblk):
                                    w = min(512, kcols - j * 512)
                                    p.op("tensor",
                                         lambda E, _j=j, _w=w, _hc=hc, _hb=hb,
                                         _q=qabs, _po=pol:
                                         E.matmul(
                                             pscore[:, _po + _j * 512:_po + _j * 512 + _w],
                                             qf[_hc][_hb:_hb + 64, _q * 128:(_q + 1) * 128],
                                             kf[_hc][_hb:_hb + 64, _j * 512:_j * 512 + _w],
                                             start=True, stop=True),
                                         reads=[("qf", hc), ("kf", hc)],
                                         writes=[("pscore", pb + j)])
                                doff = kcols - 128 + pol
                                p.op("vector",
                                     lambda E, _do=doff: E.tensor_tensor(
                                         pscore[:, _do:_do + 128],
                                         pscore[:, _do:_do + 128],
                                         tri[:], ALU.add),
                                     reads=[("pscore", doff // 512), ("tri",)],
                                     writes=[("pscore", doff // 512)])
                                p.op("vector",
                                     lambda E, _n=kcols, _po=pol, _q=qr:
                                     E.reduce_max(
                                         sums[:, _q:_q + 1],
                                         pscore[:, _po:_po + _n], axis=AX.X,
                                         negate=True),
                                     reads=[("pscore", pb + j) for j in range(nblk)],
                                     writes=[("sums", qr)],
                                     small=[sums[:, qr:qr + 1]])
                                p.op("scalar",
                                     lambda E, _n=kcols, _qr=qr, _po=pol:
                                     E.activation(
                                         asb[_qr][:, 0:_n],
                                         pscore[:, _po:_po + _n], AF.Exp,
                                         bias=sums[:, _qr:_qr + 1],
                                         accum_out=sums[:, 4 + _qr:5 + _qr]),
                                     reads=[("pscore", pb + j) for j in range(nblk)]
                                     + [("sums", qr)],
                                     writes=[("asb", qr), ("sums", 4 + qr)],
                                     small=[sums[:, 4 + qr:5 + qr]])
                                p.op("vector",
                                     lambda E, _qr=qr, _h2=h2: E.reciprocal(
                                         rs_pair[_qr][:, _h2:_h2 + 1],
                                         sums[:, 4 + _qr:5 + _qr]),
                                     reads=[("sums", 4 + qr)],
                                     writes=[("rsp", qr, h2)],
                                     small=[rs_pair[qr][:, h2:h2 + 1]])
                                if (os.environ.get("KDBG") and s == 0 and qg == 0
                                        and hp == 0 and h2 == 0 and qr == 0):
                                    dbgs = kb.sbuf("dbgs", [128, 128], F32, stack=stk3)
                                    p.op("vector",
                                         lambda E: E.tensor_copy(dbgs[:], pscore[:, 0:128]),
                                         reads=[("pscore", 0)], writes=[("dbgsc",)])
                                    kb.dump("psc00", dbgs[:], [("dbgsc",)])
                                    kb.dump("sums00", sums[:],
                                            [("sums", 2), ("sums", 4)])
                            for kc in range(nkg):
                                half = kc % 2
                                fq = max(0, kc - 4 * qg)
                                for qr in range(fq, 4):
                                    p.op("tensor",
                                         lambda E, _kc=kc, _qr=qr, _h=half:
                                         E.transpose(
                                             ptr[:, _h * 512 + _qr * 128: _h * 512 + (_qr + 1) * 128],
                                             asb[_qr][:, _kc * 128:(_kc + 1) * 128],
                                             identb[:]),
                                         reads=[("asb", qr), ("identb",)],
                                         writes=[("ptr",)])
                                if fq > 0:
                                    p.op("vector",
                                         lambda E, _h2=h2, _kc=kc, _f=fq:
                                         E.memset(aT[_h2][_kc][:, 0:_f * 128], 0.0),
                                         writes=[("aT", h2, kc)])
                                if kc % 2 == 0:
                                    p.op("vector",
                                         lambda E, _h2=h2, _kc=kc, _h=half, _f=fq:
                                         E.tensor_copy(
                                             aT[_h2][_kc][:, _f * 128:512],
                                             ptr[:, _h * 512 + _f * 128: _h * 512 + 512]),
                                         reads=[("ptr",)], writes=[("aT", h2, kc)])
                                else:
                                    p.op("scalar",
                                         lambda E, _h2=h2, _kc=kc, _h=half, _f=fq:
                                         E.copy(
                                             aT[_h2][_kc][:, _f * 128:512],
                                             ptr[:, _h * 512 + _f * 128: _h * 512 + 512]),
                                         reads=[("ptr",)], writes=[("aT", h2, kc)])
                        for qr in range(4):
                            p.op("tensor",
                                 lambda E, _qr=qr: E.transpose(
                                     pfa[0:2, _qr * 128:(_qr + 1) * 128],
                                     rs_pair[_qr][:], identf[:]),
                                 reads=[("rsp", qr, 0), ("rsp", qr, 1), ("identf",)],
                                 writes=[("pfa",)])
                        p.op("vector", lambda E: E.tensor_copy(rs2_sb[:], pfa[0:2, :]),
                             reads=[("pfa",)], writes=[("rs2_sb",)], small=[rs2_sb[:]])
                        p.op("tensor", lambda E: E.matmul(pfb[:], ones2[:], rs2_sb[:],
                                                          start=True, stop=True),
                             reads=[("rs2_sb",), ("ones2",)], writes=[("pfb",)])
                        p.op("vector", lambda E: E.tensor_copy(rsbc_sb[:], pfb[:]),
                             reads=[("pfb",)], writes=[("rsbc_sb",)])
                        for h2, bank, bres in ((0, pfc, ("pfc",)), (1, pfa, ("pfa",))):
                            for kc in range(nkg):
                                p.op("tensor",
                                     lambda E, _kc=kc, _h2=h2, _hp=hp, _b=bank, _n=nkg:
                                     E.matmul(
                                         _b[0:64, :],
                                         vt[_kc][:, _hp * 128 + _h2 * 64: _hp * 128 + (_h2 + 1) * 64],
                                         aT[_h2][_kc][:],
                                         start=(_kc == 0), stop=(_kc == _n - 1)),
                                     reads=[("vt", kc, 0), ("vt", kc, 1),
                                            ("aT", h2, kc)],
                                     writes=[bres])
                        p.op("vector",
                             lambda E, _hp=hp, _qg=qg: E.tensor_tensor(
                                 ctx[_hp][0:64, _qg * 512:(_qg + 1) * 512],
                                 pfc[0:64, :], rsbc_sb[0:64, :], ALU.mult),
                             reads=[("pfc",), ("rsbc_sb",)], writes=[("ctx", hp, qg)])
                        p.op("vector",
                             lambda E, _hp=hp, _qg=qg: E.tensor_tensor(
                                 ctx[_hp][64:128, _qg * 512:(_qg + 1) * 512],
                                 pfa[0:64, :], rsbc_sb[64:128, :], ALU.mult),
                             reads=[("pfa",), ("rsbc_sb",)], writes=[("ctx", hp, qg)])
                        if os.environ.get("KDBG") and s == 0 and qg == 0 and hp == 0:
                            kb.dump("asb0", asb[0][:, 0:128], [("asb", 0)])
                            kb.dump("asb3", asb[3][:, 0:512], [("asb", 3)])
                            kb.dump("rs2", rs2_sb[:], [("rs2_sb",)])
                            kb.dump("rsbc", rsbc_sb[:], [("rsbc_sb",)])
                            kb.dump("ctx0", ctx[0][:, 0:512], [("ctx", 0, 0)])
                            kb.dump("aT00", aT[0][0][:], [("aT", 0, 0)])
                    if os.environ.get("KDBG") and s == 0 and qg == 0:
                        kb.dump("qf0", qf[0][:, 0:512], [("qf", 0)])
                        kb.dump("kf0", kf[0][:, 0:512], [("kf", 0)])
                        kb.dump("vt0", vt[0][:], [("vt", 0, 0), ("vt", 0, 1)])
                p.barrier()

            if KSTAGE < 4:
                p.barrier()
                return
            with kb.scope() as stk4:
                wstage_c = [kb.sbuf(f"c_wstage{i}", [128, 512], F32, stack=stk4)
                            for i in range(2)]
                wblk_c = kb.sbuf("c_wblk", [128, 8 * 512], BF16, stack=stk4)
                bo_t = load_col_table(stk4, "bo_t", dr["ab_out"][:], NCH)
                xblk = [kb.sbuf(f"c_xblk{i}", [128, 512], F32, stack=stk4)
                        for i in range(4)]
                tmp = [kb.sbuf(f"c_tmp{i}", [128, 512], F32, stack=stk4)
                       for i in range(2)]
                for ob in range(2):
                    c0 = ob * 512
                    for dc in range(NCH):
                        ws = wstage_c[dc % 2]
                        p.dma(ws[:], dr["aw_out"][dc * 128:(dc + 1) * 128, c0:c0 + 512],
                              writes=[("cwstage", dc % 2)])
                        p.op("scalar",
                             lambda E, _dc=dc, _w=ws: E.copy(
                                 wblk_c[:, _dc * 512:(_dc + 1) * 512], _w[:]),
                             reads=[("cwstage", dc % 2)], writes=[("cwblk", dc)])
                    for tb in range(NTB):
                        for j in range(4):
                            oc = ob * 4 + j
                            pj = (tb * 4 + j) % 4
                            for dc in range(NCH):
                                p.op("tensor",
                                     lambda E, _j=j, _tb=tb, _dc=dc, _pj=pj:
                                     E.matmul(
                                         pscore[:, _pj * 512:(_pj + 1) * 512],
                                         wblk_c[:, _dc * 512 + _j * 128: _dc * 512 + (_j + 1) * 128],
                                         ctx[_dc][:, _tb * 512:(_tb + 1) * 512],
                                         start=(_dc == 0), stop=(_dc == NCH - 1)),
                                     reads=[("cwblk", dc), ("ctx", dc, tb)],
                                     writes=[("pscore", pj)])
                            p.dma(xblk[j][:], xdram[s, oc, :, tb * 512:(tb + 1) * 512],
                                  writes=[("xblk", j)])
                            tp = tmp[j % 2]
                            p.op("scalar",
                                 lambda E, _pj=pj, _oc=oc, _tp=tp: E.activation(
                                     _tp[:], pscore[:, _pj * 512:(_pj + 1) * 512],
                                     AF.Identity, bias=bo_t[:, _oc:_oc + 1]),
                                 reads=[("pscore", pj), ("bo_t",)],
                                 writes=[("ctmp", j % 2)])
                            p.op("vector",
                                 lambda E, _j=j, _tp=tp: E.tensor_add(
                                     xblk[_j][:], xblk[_j][:], _tp[:]),
                                 reads=[("ctmp", j % 2), ("xblk", j)],
                                 writes=[("xblk", j)])
                            p.dma(xdram[s, oc, :, tb * 512:(tb + 1) * 512], xblk[j][:],
                                  reads=[("xblk", j)])
        p.barrier()

    def ffn_qkv_rotary(stka, s, l, xi, vt, qf, kf, rott):
        rotCq, rotSq, rotCk, rotSk = rott
        wq = kb.sbuf("f_wq", [128, NCH * 1536], BF16, stack=stka)
        wqs = kb.sbuf("f_wqs", [128, 768], F32, stack=stka)
        for dc in range(NCH):
            for hh in range(2):
                p.dma(wqs[:], dr["f_wqkv"][l, dc * 128:(dc + 1) * 128,
                                           hh * 768:(hh + 1) * 768],
                      writes=[("fwqs",)])
                p.op("scalar",
                     lambda E, _dc=dc, _h=hh, _w=wqs: E.copy(
                         wq[:, _dc * 1536 + _h * 768: _dc * 1536 + (_h + 1) * 768],
                         _w[:]),
                     reads=[("fwqs",)], writes=[("fwq", dc)])
        qsc = kb.sbuf("f_qsc", [128, 64], F32, stack=stka)
        ksc = kb.sbuf("f_ksc", [128, 64], F32, stack=stka)
        p.dma(qsc[:], dr["f_qs"][l].unsqueeze(0).broadcast_to((128, 64)),
              writes=[("fqsc",)])
        p.dma(ksc[:], dr["f_ks"][l].unsqueeze(0).broadcast_to((128, 64)),
              writes=[("fksc",)])
        xbt = [kb.sbuf(f"f_xbt{c}", [128, 513], BF16, stack=stka)
               for c in range(NCH)]
        qt_ = kb.sbuf("f_qt", [128, 512], F32, stack=stka)
        kt_ = kb.sbuf("f_kt", [128, 512], F32, stack=stka)
        rsm = kb.sbuf("f_rsm", [128, 8], F32, stack=stka)
        rsm2 = kb.sbuf("f_rsm2", [128, 8], F32, stack=stka)
        rsn = kb.sbuf("f_rsn", [128, 8], F32, stack=stka)
        scl = kb.sbuf("f_scl", [128, 512], F32, stack=stka)
        rot2 = kb.sbuf("f_rot2", [128, 512], F32, stack=stka)
        t1_ = kb.sbuf("f_t1", [128, 512], F32, stack=stka)
        rqb = kb.sbuf("f_rqb", [128, 512], BF16, stack=stka)

        def v3(t):
            return t[:].rearrange("p (h d) -> p h d", h=8)

        for tb in range(NTB):
            for dc in range(NCH):
                p.op("scalar",
                     lambda E, _dc=dc, _t0=tb * 512: E.copy(
                         xbt[_dc][:], xi[_dc][:, _t0:_t0 + 513]),
                     reads=[("xi", dc, tb), ("xi", dc, tb - 1), ("xi", dc, "pad")],
                     writes=[("xbt", dc)])
            for tc4 in range(4):
                tc = tb * 4 + tc4
                for role, off, psj in (("q", 0, tc4 % 2), ("k", 512, 2 + tc4 % 2),
                                       ("v", 1024, tc4 % 2)):
                    for dc in range(NCH):
                        coff = (1 if dc < 4 else 0) + tc4 * 128
                        p.op("tensor",
                             lambda E, _dc=dc, _co=coff, _o=off, _pj=psj:
                             E.matmul(pscore[:, _pj * 512:(_pj + 1) * 512],
                                      xbt[_dc][:, _co:_co + 128],
                                      wq[:, _dc * 1536 + _o: _dc * 1536 + _o + 512],
                                      start=(_dc == 0), stop=(_dc == NCH - 1)),
                             reads=[("xbt", dc), ("fwq", dc)],
                             writes=[("pscore", psj)])
                    if role == "v":
                        p.op("vector",
                             lambda E, _tc=tc, _pj=psj: E.tensor_copy(
                                 vt[_tc][:], pscore[:, _pj * 512:(_pj + 1) * 512]),
                             reads=[("pscore", psj)], writes=[("vt", tc)])
                    else:
                        dst = qt_ if role == "q" else kt_
                        p.op("vector",
                             lambda E, _d=dst, _pj=psj: E.tensor_copy(
                                 _d[:], pscore[:, _pj * 512:(_pj + 1) * 512]),
                             reads=[("pscore", psj)], writes=[(role + "t",)])
                for role, buf, sct, Ct, St, dstf in (
                        ("q", qt_, qsc, rotCq, rotSq, qf),
                        ("k", kt_, ksc, rotCk, rotSk, kf)):
                    p.op("scalar",
                         lambda E, _b=buf: E.activation(t1_[:], _b[:], AF.Square),
                         reads=[(role + "t",)], writes=[("t1_",)])
                    p.op("vector",
                         lambda E: E.tensor_reduce(
                             rsm[:].rearrange("p (h o) -> p h o", o=1),
                             v3(t1_), axis=AX.X, op=ALU.add),
                         reads=[("t1_",)], writes=[("rsm",)], small=[rsm[:]])
                    p.op("scalar", lambda E: E.activation(rsm2[:], rsm[:], AF.Sqrt,
                                                          bias=1e-24),
                         reads=[("rsm",)], writes=[("rsm2",)], small=[rsm2[:]])
                    p.op("vector", lambda E: E.reciprocal(rsn[:], rsm2[:]),
                         reads=[("rsm2",)], writes=[("rsn",)], small=[rsn[:]])
                    p.op("vector",
                         lambda E, _s=sct: E.tensor_tensor(
                             v3(scl),
                             rsn[:].unsqueeze(2).broadcast_to((128, 8, 64)),
                             _s[:].unsqueeze(1).broadcast_to((128, 8, 64)),
                             ALU.mult),
                         reads=[("rsn",), ("f" + role + "sc",)], writes=[("scl",)])
                    p.op("vector",
                         lambda E, _b=buf: E.tensor_tensor(_b[:], _b[:], scl[:],
                                                           ALU.mult),
                         reads=[(role + "t",), ("scl",)], writes=[(role + "t",)])
                    p.op("vector",
                         lambda E, _b=buf: E.tensor_scalar_mul(
                             rot2[:].rearrange("p (h u d) -> p h u d", h=8, u=2)[:, :, 0, :],
                             _b[:].rearrange("p (h u d) -> p h u d", h=8, u=2)[:, :, 1, :],
                             -1.0),
                         reads=[(role + "t",)], writes=[("rot2",)])
                    p.op("vector",
                         lambda E, _b=buf: E.tensor_copy(
                             rot2[:].rearrange("p (h u d) -> p h u d", h=8, u=2)[:, :, 1, :],
                             _b[:].rearrange("p (h u d) -> p h u d", h=8, u=2)[:, :, 0, :]),
                         reads=[(role + "t",)], writes=[("rot2",)])
                    p.op("vector",
                         lambda E, _b=buf, _C=Ct, _c=tc * 64: E.tensor_tensor(
                             v3(t1_), v3(_b),
                             _C[:, _c:_c + 64].unsqueeze(1).broadcast_to((128, 8, 64)),
                             ALU.mult),
                         reads=[(role + "t",), ("rot",)], writes=[("t1_",)])
                    p.op("vector",
                         lambda E, _S=St, _c=tc * 64: E.tensor_tensor(
                             v3(rot2), v3(rot2),
                             _S[:, _c:_c + 64].unsqueeze(1).broadcast_to((128, 8, 64)),
                             ALU.mult),
                         reads=[("rot2",), ("rot",)], writes=[("rot2",)])
                    p.op("vector",
                         lambda E: E.tensor_add(rqb[:], t1_[:], rot2[:]),
                         reads=[("t1_",), ("rot2",)], writes=[("rqb",)])
                    for fc in range(4):
                        p.op("tensor",
                             lambda E, _fc=fc: E.transpose(
                                 ptr[:, _fc * 128:(_fc + 1) * 128],
                                 rqb[:, _fc * 128:(_fc + 1) * 128], identb[:]),
                             reads=[("rqb",), ("identb",)], writes=[("ptr",)])
                    for fc in range(4):
                        p.op("vector",
                             lambda E, _fc=fc, _d=dstf, _tc=tc: E.tensor_copy(
                                 _d[_fc][:, _tc * 128:(_tc + 1) * 128],
                                 ptr[:, _fc * 128:(_fc + 1) * 128]),
                             reads=[("ptr",)], writes=[(role + "f", fc)])

    def ffn_attn(stka, s, l, vt, qf, kf, of):
        if os.environ.get("KDBG") and s == 0 and l == 0:
            kb.dump("fqf0", qf[0][:], [("qf", 0)])
            kb.dump("fkf0", kf[0][:], [("kf", 0)])
            kb.dump("fvt0", vt[0][:], [("vt", 0)])
        asbf = [kb.sbuf(f"f_asb{i}", [128, 640], BF16, stack=stka) for i in range(4)]
        aTf = [[kb.sbuf(f"f_aT{h}_{k}", [128, 512], BF16, stack=stka)
                for k in range(8)] for h in range(2)]
        rsp = [kb.sbuf(f"f_rsp{i}", [128, 2], F32, stack=stka) for i in range(4)]
        sums = kb.sbuf("f_sums", [128, 8], F32, stack=stka)
        for w in range(4):
            for hp in range(4):
                for h2 in range(2):
                    head = 2 * hp + h2
                    hb = (head % 2) * 64
                    hc = head // 2
                    for m in range(4):
                        if w == 0:
                            kcols = 128 * (m + 1)
                            off = (m % 2) * 1024
                            nsl = (kcols + 511) // 512
                            slres = [("pscore", off // 512 + j) for j in range(nsl)]
                            for j in range(nsl):
                                wdt = min(512, kcols - j * 512)
                                p.op("tensor",
                                     lambda E, _j=j, _w=wdt, _hc=hc, _hb=hb,
                                            _m=m, _o=off:
                                     E.matmul(
                                         pscore[:, _o + _j * 512:_o + _j * 512 + _w],
                                         qf[_hc][_hb:_hb + 64,
                                                 _m * 128:(_m + 1) * 128],
                                         kf[_hc][_hb:_hb + 64, _j * 512:_j * 512 + _w],
                                         start=True, stop=True),
                                     reads=[("qf", hc), ("kf", hc)],
                                     writes=[("pscore", off // 512 + j)])
                            doff = off + kcols - 128
                            p.op("vector",
                                 lambda E, _do=doff: E.tensor_tensor(
                                     pscore[:, _do:_do + 128],
                                     pscore[:, _do:_do + 128], tri[:], ALU.add),
                                 reads=[("pscore", doff // 512), ("tri",)],
                                 writes=[("pscore", doff // 512)])
                        else:
                            kcols = 640
                            off = (m % 2) * 1024
                            kb0 = (w - 1) * 512 + m * 128
                            slres = [("pscore", off // 512), ("pscore", off // 512 + 1)]
                            if (os.environ.get("KDBG") and s == 0 and l == 0
                                    and w == 1 and hp == 0 and h2 == 0 and m == 0):
                                dsc4 = kb.sbuf("f_dsc4", [128, 640], F32, stack=stka)
                                p.op("vector",
                                     lambda E: E.tensor_copy(dsc4[:], pscore[:, 0:640]),
                                     reads=slres, writes=[("fdsc4",)])
                                kb.dump("fscw1prior", dsc4[:], [("fdsc4",)])
                            q0 = (m + 4 * w) * 128
                            p.op("tensor",
                                 lambda E, _hc=hc, _hb=hb, _q0=q0, _o=off, _k=kb0:
                                 E.matmul(pscore[:, _o:_o + 512],
                                          qf[_hc][_hb:_hb + 64, _q0:_q0 + 128],
                                          kf[_hc][_hb:_hb + 64, _k:_k + 512],
                                          start=True, stop=True),
                                 reads=[("qf", hc), ("kf", hc)], writes=[slres[0]])
                            p.op("tensor",
                                 lambda E, _hc=hc, _hb=hb, _q0=q0, _o=off, _k=kb0:
                                 E.matmul(pscore[:, _o + 512:_o + 640],
                                          qf[_hc][_hb:_hb + 64, _q0:_q0 + 128],
                                          kf[_hc][_hb:_hb + 64, _k + 512:_k + 640],
                                          start=True, stop=True),
                                 reads=[("qf", hc), ("kf", hc)], writes=[slres[1]])
                            if (os.environ.get("KDBG") and s == 0 and l == 0
                                    and w == 1 and hp == 0 and h2 == 0 and m == 0):
                                kb.dump("fqf0_at_w1", qf[0][:], [("qf", 0)])
                                kb.dump("fkf0_at_w1", kf[0][:], [("kf", 0)])
                                dsc2 = kb.sbuf("f_dsc2", [128, 640], F32, stack=stka)
                                p.op("vector",
                                     lambda E: E.tensor_copy(dsc2[:], pscore[:, 0:640]),
                                     reads=slres, writes=[("fdsc2",)])
                                kb.dump("fscw1pre", dsc2[:], [("fdsc2",)])
                            p.op("vector",
                                 lambda E, _o=off: E.tensor_tensor(
                                     pscore[:, _o:_o + 512], pscore[:, _o:_o + 512],
                                     bandm[:, 0:512], ALU.add),
                                 reads=[slres[0], ("bandm",)], writes=[slres[0]])
                            p.op("vector",
                                 lambda E, _o=off: E.tensor_tensor(
                                     pscore[:, _o + 512:_o + 640],
                                     pscore[:, _o + 512:_o + 640],
                                     bandm[:, 512:640], ALU.add),
                                 reads=[slres[1], ("bandm",)], writes=[slres[1]])
                            if (os.environ.get("KDBG") and s == 0 and l == 0
                                    and w == 1 and hp == 0 and h2 == 0 and m == 0):
                                dsc3 = kb.sbuf("f_dsc3", [128, 640], F32, stack=stka)
                                p.op("vector",
                                     lambda E: E.tensor_copy(dsc3[:], pscore[:, 0:640]),
                                     reads=slres, writes=[("fdsc3",)])
                                kb.dump("fscw1post", dsc3[:], [("fdsc3",)])
                        p.op("vector",
                             lambda E, _o=off, _n=kcols:
                             E.reduce_max(sums[:, 2:3], pscore[:, _o:_o + _n],
                                          axis=AX.X, negate=True),
                             reads=slres, writes=[("sums", 2)], small=[sums[:, 2:3]])
                        p.op("scalar",
                             lambda E, _o=off, _n=kcols, _m=m: E.activation(
                                 asbf[_m][:, 0:_n], pscore[:, _o:_o + _n], AF.Exp,
                                 bias=sums[:, 2:3], accum_out=sums[:, 4:5]),
                             reads=slres + [("sums", 2)],
                             writes=[("asbf", m), ("sums", 4)], small=[sums[:, 4:5]])
                        p.op("vector",
                             lambda E, _m=m, _h2=h2: E.reciprocal(
                                 rsp[_m][:, _h2:_h2 + 1], sums[:, 4:5]),
                             reads=[("sums", 4)], writes=[("frsp", m, h2)],
                             small=[rsp[m][:, h2:h2 + 1]])
                        if (os.environ.get("KDBG") and s == 0 and l == 0 and w == 0
                                and hp == 0 and h2 == 0 and m == 1):
                            kb.dump("fasb1", asbf[1][:, 0:256], [("asbf", 1)])
                            kb.dump("fsums1", sums[:], [("sums", 2), ("sums", 4)])
                            dsc = kb.sbuf("f_dsc", [128, 256], F32, stack=stka)
                            p.op("vector",
                                 lambda E: E.tensor_copy(dsc[:], pscore[:, 1024:1280]),
                                 reads=[("pscore", 2)], writes=[("fdsc",)])
                            kb.dump("fscm1", dsc[:], [("fdsc",)])
                        if (os.environ.get("KDBG") and s == 0 and l == 0 and w == 1
                                and hp == 0 and h2 == 0 and m == 0):
                            kb.dump("fasbw1", asbf[0][:, 0:640], [("asbf", 0)])
                    kcr_range = range(8) if w > 0 else range(4, 8)
                    for kcr in kcr_range:
                        half = kcr % 2
                        if w > 0:
                            ms = range(max(0, kcr - 4), min(3, kcr) + 1)
                        else:
                            kcg = kcr - 4
                            ms = range(kcg, 4)
                        ms = list(ms)
                        for m in ms:
                            j = (kcr - m) if w > 0 else (kcr - 4)
                            p.op("tensor",
                                 lambda E, _m=m, _j=j, _h=half: E.transpose(
                                     ptr[:, _h * 512 + _m * 128: _h * 512 + (_m + 1) * 128],
                                     asbf[_m][:, _j * 128:(_j + 1) * 128], identb[:]),
                                 reads=[("asbf", m), ("identb",)], writes=[("ptr",)])
                        f0, f1 = ms[0], ms[-1] + 1
                        if f0 > 0:
                            p.op("vector",
                                 lambda E, _h2=h2, _k=kcr, _f=f0: E.memset(
                                     aTf[_h2][_k][:, 0:_f * 128], 0.0),
                                 writes=[("aTf", h2, kcr)])
                        if f1 < 4:
                            p.op("vector",
                                 lambda E, _h2=h2, _k=kcr, _f=f1: E.memset(
                                     aTf[_h2][_k][:, _f * 128:512], 0.0),
                                 writes=[("aTf", h2, kcr)])
                        if kcr % 2 == 0:
                            p.op("vector",
                                 lambda E, _h2=h2, _k=kcr, _h=half, _a=f0, _b=f1:
                                 E.tensor_copy(aTf[_h2][_k][:, _a * 128:_b * 128],
                                               ptr[:, _h * 512 + _a * 128:_h * 512 + _b * 128]),
                                 reads=[("ptr",)], writes=[("aTf", h2, kcr)])
                        else:
                            p.op("scalar",
                                 lambda E, _h2=h2, _k=kcr, _h=half, _a=f0, _b=f1:
                                 E.copy(aTf[_h2][_k][:, _a * 128:_b * 128],
                                        ptr[:, _h * 512 + _a * 128:_h * 512 + _b * 128]),
                                 reads=[("ptr",)], writes=[("aTf", h2, kcr)])
                for m in range(4):
                    p.op("tensor",
                         lambda E, _m=m: E.transpose(
                             pfa[0:2, _m * 128:(_m + 1) * 128], rsp[_m][:],
                             identf[:]),
                         reads=[("frsp", m, 0), ("frsp", m, 1), ("identf",)],
                         writes=[("pfa",)])
                p.op("vector", lambda E: E.tensor_copy(rs2_sb[:], pfa[0:2, :]),
                     reads=[("pfa",)], writes=[("rs2_sb",)], small=[rs2_sb[:]])
                p.op("tensor", lambda E: E.matmul(pfb[:], ones2[:], rs2_sb[:],
                                                  start=True, stop=True),
                     reads=[("rs2_sb",), ("ones2",)], writes=[("pfb",)])
                p.op("vector", lambda E: E.tensor_copy(rsbc_sb[:], pfb[:]),
                     reads=[("pfb",)], writes=[("rsbc_sb",)])
                kcr_list = list(range(8)) if w > 0 else list(range(4, 8))
                for h2, bank, bres in ((0, pfc, ("pfc",)), (1, pfa, ("pfa",))):
                    for i, kcr in enumerate(kcr_list):
                        tck = 4 * (w - 1) + kcr
                        p.op("tensor",
                             lambda E, _t=tck, _h2=h2, _hp=hp, _b=bank,
                                    _st=(i == 0), _sp=(i == len(kcr_list) - 1),
                                    _k=kcr:
                             E.matmul(_b[0:64, :],
                                      vt[_t][:, _hp * 128 + _h2 * 64:
                                             _hp * 128 + (_h2 + 1) * 64],
                                      aTf[_h2][_k][:], start=_st, stop=_sp),
                             reads=[("vt", tck), ("aTf", h2, kcr)], writes=[bres])
                p.op("vector",
                     lambda E, _hp=hp, _w=w: E.tensor_tensor(
                         of[_hp][0:64, _w * 512:(_w + 1) * 512], pfc[0:64, :],
                         rsbc_sb[0:64, :], ALU.mult),
                     reads=[("pfc",), ("rsbc_sb",)], writes=[("off", hp, w)])
                p.op("vector",
                     lambda E, _hp=hp, _w=w: E.tensor_tensor(
                         of[_hp][64:128, _w * 512:(_w + 1) * 512], pfa[0:64, :],
                         rsbc_sb[64:128, :], ALU.mult),
                     reads=[("pfa",), ("rsbc_sb",)], writes=[("off", hp, w)])

    def ffn_wo(stka, s, l, xi, of):
        if os.environ.get("KDBG") and s == 0 and l == 0:
            kb.dump("fof0", of[0][:], [("off", 0, w) for w in range(4)])
            kb.dump("fof3", of[3][:], [("off", 3, w) for w in range(4)])
        wob = kb.sbuf("f_wob", [128, 4 * 1024], BF16, stack=stka)
        wos = kb.sbuf("f_wos", [128, 1024], F32, stack=stka)
        for dc in range(4):
            p.dma(wos[:], dr["f_wo"][l, dc * 128:(dc + 1) * 128, :],
                  writes=[("fwos",)])
            p.op("scalar",
                 lambda E, _dc=dc: E.copy(wob[:, _dc * 1024:(_dc + 1) * 1024],
                                          wos[:]),
                 reads=[("fwos",)], writes=[("fwob", dc)])
        for oc in range(NCH):
            for tb in range(NTB):
                pj = (oc * NTB + tb) % 4
                for dc in range(4):
                    p.op("tensor",
                         lambda E, _oc=oc, _tb=tb, _dc=dc, _pj=pj:
                         E.matmul(pscore[:, _pj * 512:(_pj + 1) * 512],
                                  wob[:, _dc * 1024 + _oc * 128:
                                      _dc * 1024 + (_oc + 1) * 128],
                                  of[_dc][:, _tb * 512:(_tb + 1) * 512],
                                  start=(_dc == 0), stop=(_dc == 3)),
                         reads=[("fwob", dc), ("off", dc, tb)],
                         writes=[("pscore", pj)])
                p.op("vector",
                     lambda E, _oc=oc, _t0=tb * 512, _pj=pj: E.tensor_tensor(
                         xi[_oc][:, 1 + _t0:1 + _t0 + 512],
                         xi[_oc][:, 1 + _t0:1 + _t0 + 512],
                         pscore[:, _pj * 512:(_pj + 1) * 512], ALU.add),
                     reads=[("pscore", pj), ("xi", oc, tb)],
                     writes=[("xi", oc, tb)])

    def ffn_mlp_a(stkm, s, l, xi):
        xn2 = [kb.sbuf(f"f_xn2_{c}", [128, S], BF16, stack=stkm)
               for c in range(NCH)]
        xbt = [kb.sbuf(f"f_mxbt{c}", [128, 513], BF16, stack=stkm)
               for c in range(NCH)]
        sqb = [kb.sbuf(f"f_msq{i}", [128, 512], BF16, stack=stkm) for i in range(4)]
        lnt = kb.sbuf("f_lnt", [128, 512], F32, stack=stkm)
        lgt = load_col_table(stkm, "f_lgt", dr["f_lg"][l], NCH)
        lbt = load_col_table(stkm, "f_lbt", dr["f_lb"][l], NCH)
        for tb in range(NTB):
            t0 = tb * 512
            for dc in range(NCH):
                p.op("scalar",
                     lambda E, _dc=dc, _t0=t0: E.copy(
                         xbt[_dc][:], xi[_dc][:, _t0:_t0 + 513]),
                     reads=[("xi", dc, tb), ("xi", dc, tb - 1), ("xi", dc, "pad")],
                     writes=[("mxbt", dc)])
            for dc in range(NCH):
                p.op("tensor",
                     lambda E, _dc=dc, _co=(1 if dc < 4 else 0): E.matmul(
                         pfa[0:1, :], ones_col_b[:],
                         xbt[_dc][:, _co:_co + 512],
                         start=(_dc == 0), stop=(_dc == NCH - 1)),
                     reads=[("mxbt", dc), ("ones_col_b",)], writes=[("pfa",)])
            for dc in range(NCH):
                p.op("scalar",
                     lambda E, _dc=dc, _co=(1 if dc < 4 else 0): E.activation(
                         sqb[_dc % 4][:], xbt[_dc][:, _co:_co + 512], AF.Square),
                     reads=[("mxbt", dc)], writes=[("msqb", dc % 4)])
                p.op("tensor",
                     lambda E, _dc=dc: E.matmul(
                         pfb[0:1, :], ones_col_b[:], sqb[_dc % 4][:],
                         start=(_dc == 0), stop=(_dc == NCH - 1)),
                     reads=[("msqb", dc % 4), ("ones_col_b",)], writes=[("pfb",)])
            p.op("vector", lambda E: E.tensor_scalar_mul(row("mu"), pfa[0:1, :], 1.0 / D),
                 reads=[("pfa",)], writes=[rres("mu")])
            p.op("vector", lambda E: E.tensor_scalar_mul(row("var"), pfb[0:1, :], 1.0 / D),
                 reads=[("pfb",)], writes=[rres("var")])
            p.op("vector", lambda E: E.tensor_mul(row("A"), row("mu"), row("mu")),
                 reads=[rres("mu")], writes=[rres("A")])
            p.op("vector", lambda E: E.tensor_sub(row("var"), row("var"), row("A")),
                 reads=[rres("var"), rres("A")], writes=[rres("var")])
            p.op("vector", lambda E: E.tensor_scalar_add(row("var"), row("var"), EPS),
                 reads=[rres("var")], writes=[rres("var")])
            p.op("scalar", lambda E: E.activation(row("var"), row("var"), AF.Sqrt),
                 reads=[rres("var")], writes=[rres("var")], small=[row("var")])
            p.op("vector", lambda E: E.reciprocal(row("A"), row("var")),
                 reads=[rres("var")], writes=[rres("A")])
            p.op("vector", lambda E: E.tensor_mul(row("B"), row("mu"), row("A")),
                 reads=[rres("mu"), rres("A")], writes=[rres("B")])
            p.op("vector", lambda E: E.tensor_scalar_mul(row("B"), row("B"), -1.0),
                 reads=[rres("B")], writes=[rres("B")], small=[row("B"), row("A")])
            p.op("tensor", lambda E: E.matmul(pfc[:], ones_row_f[:], row("A"),
                                              start=True, stop=True),
                 reads=[rres("A"), ("ones_row_f",)], writes=[("pfc",)])
            p.op("tensor", lambda E: E.matmul(pfa[:], ones_row_f[:], row("B"),
                                              start=True, stop=True),
                 reads=[rres("B"), ("ones_row_f",)], writes=[("pfa",)])
            for dc in range(NCH):
                co = 1 if dc < 4 else 0
                p.op("vector",
                     lambda E, _dc=dc, _co=co: E.tensor_tensor(
                         lnt[:], xbt[_dc][:, _co:_co + 512], pfc[:], ALU.mult),
                     reads=[("mxbt", dc), ("pfc",)], writes=[("flnt",)])
                p.op("vector",
                     lambda E: E.tensor_tensor(lnt[:], lnt[:], pfa[:], ALU.add),
                     reads=[("flnt",), ("pfa",)], writes=[("flnt",)])
                p.op("scalar",
                     lambda E, _dc=dc, _t0=t0: E.activation(
                         xn2[_dc][:, _t0:_t0 + 512], lnt[:], AF.Identity,
                         scale=lgt[:, _dc:_dc + 1], bias=lbt[:, _dc:_dc + 1]),
                     reads=[("flnt",), ("f_lgt",), ("f_lbt",)],
                     writes=[("xn2", dc, tb)])
        def load_b1(stk2, name, base):
            stage = kb.sbuf(name + "_st", [22, 128], F32, stack=stk2)
            tbl = kb.sbuf(name, [128, 22], F32, stack=stk2)
            p.op("vector", lambda E: E.memset(stage[:], 0.0), writes=[(name, "st")])
            p.dma(stage[0:21, :],
                  dr["f_b1"][l, base:base + 2688].rearrange("(c p) -> c p", p=128),
                  writes=[(name, "st")])
            p.dma(stage[21:22, 0:42], dr["f_b1"][l, base + 2688:base + 2730]
                  .unsqueeze(0), writes=[(name, "st")], slow_ok=True)
            p.op("tensor",
                 lambda E: E.transpose(pfb[:, 0:22], stage[:], identf[0:22, 0:22]),
                 reads=[(name, "st"), ("identf",)], writes=[("pfb",)])
            p.op("vector", lambda E: E.tensor_copy(tbl[:], pfb[:, 0:22]),
                 reads=[("pfb",)], writes=[(name,)], small=[tbl[:]])
            return tbl
        if os.environ.get("KDBG") and s == 0 and l == 0:
            kb.dump("fxn2_0", xn2[0][:], [("xn2", 0, t) for t in range(NTB)])
            kb.dump("fxiw0", xi[0][:, 0:1024], [("xi", 0, 0), ("xi", 0, 1)])
        b1a = load_b1(stkm, "f_b1a", 0)
        b1g = load_b1(stkm, "f_b1g", 2730)
        build_w1c = l not in built_fw1c
        built_fw1c.add(l)
        w1s = [kb.sbuf(f"f_w1s{i}", [128, 1024], F32, stack=stkm) for i in range(2)]
        w1ab = [kb.sbuf(f"f_w1ab{i}", [128, 1024], BF16, stack=stkm)
                for i in range(2)]
        w1gb = [kb.sbuf(f"f_w1gb{i}", [128, 1024], BF16, stack=stkm)
                for i in range(2)]
        ggb = kb.sbuf("f_ggb", [128, 512], BF16, stack=stkm)
        atb = kb.sbuf("f_atb", [128, 512], F32, stack=stkm)
        h2st = [kb.sbuf(f"f_h2st{i}", [128, S], BF16, stack=stkm) for i in range(2)]
        for c in range(22):
            wc = 128 if c < 21 else 42
            pr = c % 2
            for pi, part in enumerate(("a", "g")):
                wbuf = (w1ab if part == "a" else w1gb)[pr]
                wres = ("w1" + part, pr)
                if build_w1c:
                    base = c * 128 if part == "a" else 2730 + c * 128
                    st_ = w1s[pi]
                    p.dma(st_[:, 0:8 * wc].rearrange("p (c n) -> p c n", c=8),
                          dr["f_w1"][l, :, base:base + wc]
                          .rearrange("(c p) n -> p c n", p=128),
                          writes=[("w1s", part)])
                    p.op("scalar",
                         lambda E, _w=wbuf, _s=st_, _n=8 * wc: E.copy(
                             _w[:, 0:_n], _s[:, 0:_n]),
                         reads=[("w1s", part)], writes=[wres])
                    p.dma(fw1c[l, pi, c, :, 0:8 * wc], wbuf[:, 0:8 * wc],
                          reads=[wres])
                else:
                    p.dma(wbuf[:, 0:8 * wc], fw1c[l, pi, c, :, 0:8 * wc],
                          writes=[wres])
            for tb in range(NTB):
                pja, pjg = (tb * 2) % 4, (tb * 2 + 1) % 4
                for dc in range(NCH):
                    p.op("tensor",
                         lambda E, _dc=dc, _tb=tb, _pj=pjg, _wc=wc,
                         _w=w1gb[pr]: E.matmul(
                             pscore[0:_wc, _pj * 512:(_pj + 1) * 512],
                             _w[:, _dc * _wc:(_dc + 1) * _wc],
                             xn2[_dc][:, _tb * 512:(_tb + 1) * 512],
                             start=(_dc == 0), stop=(_dc == NCH - 1)),
                         reads=[("w1g", pr), ("xn2", dc, tb)],
                         writes=[("pscore", pjg)])
                for dc in range(NCH):
                    p.op("tensor",
                         lambda E, _dc=dc, _tb=tb, _pj=pja, _wc=wc,
                         _w=w1ab[pr]: E.matmul(
                             pscore[0:_wc, _pj * 512:(_pj + 1) * 512],
                             _w[:, _dc * _wc:(_dc + 1) * _wc],
                             xn2[_dc][:, _tb * 512:(_tb + 1) * 512],
                             start=(_dc == 0), stop=(_dc == NCH - 1)),
                         reads=[("w1a", pr), ("xn2", dc, tb)],
                         writes=[("pscore", pja)])
                p.op("scalar",
                     lambda E, _pj=pjg, _wc=wc, _c=c: E.activation(
                         ggb[0:_wc, :], pscore[0:_wc, _pj * 512:(_pj + 1) * 512],
                         AF.Gelu, bias=b1g[0:_wc, _c:_c + 1]),
                     reads=[("pscore", pjg), ("f_b1g",)], writes=[("ggb",)])
                p.op("scalar",
                     lambda E, _pj=pja, _wc=wc, _c=c: E.activation(
                         atb[0:_wc, :], pscore[0:_wc, _pj * 512:(_pj + 1) * 512],
                         AF.Identity, bias=b1a[0:_wc, _c:_c + 1]),
                     reads=[("pscore", pja), ("f_b1a",)], writes=[("atb",)])
                p.op("vector",
                     lambda E, _c=c, _tb=tb, _wc=wc: E.tensor_tensor(
                         h2st[_c % 2][0:_wc, _tb * 512:(_tb + 1) * 512],
                         atb[0:_wc, :], ggb[0:_wc, :], ALU.mult),
                     reads=[("atb",), ("ggb",)], writes=[("h2st", c % 2)])
            p.dma(h2dram[c, 0:wc, :], h2st[c % 2][0:wc, :],
                  reads=[("h2st", c % 2)])

    def ffn_mlp_b(stkb, s, l, xi):
        dodump = os.environ.get("KDBG") and s == 0 and l == 0
        w2b = kb.sbuf("f_w2b", [128, 22 * 1024], BF16, stack=stkb)
        w2s = [kb.sbuf(f"f_w2s{i}", [128, 1024], F32, stack=stkb) for i in range(2)]
        b2t = load_col_table(stkb, "f_b2t", dr["f_b2"][l], NCH)
        h2l = [kb.sbuf(f"f_h2l{i}", [128, 22 * 512], BF16, stack=stkb)
               for i in range(2)]
        tmpo = kb.sbuf("f_tmpo", [128, 512], F32, stack=stkb)
        for c in range(22):
            wc = 128 if c < 21 else 42
            st_ = w2s[c % 2]
            p.dma(st_[0:wc, :], dr["f_w2"][l, c * 128:c * 128 + wc, :],
                  writes=[("w2s", c % 2)])
            p.op("scalar",
                 lambda E, _c=c, _s=st_, _wc=wc: E.copy(
                     w2b[0:_wc, _c * 1024:(_c + 1) * 1024], _s[0:_wc, :]),
                 reads=[("w2s", c % 2)], writes=[("w2b", c)])
        for tb in range(NTB):
            t0 = tb * 512
            hb = h2l[tb % 2]
            for c in range(22):
                wc = 128 if c < 21 else 42
                p.dma(hb[0:wc, c * 512:(c + 1) * 512],
                      h2dram[c, 0:wc, t0:t0 + 512], writes=[("h2l", tb % 2, c)])
            for oc in range(NCH):
                pj = (tb * 2 + oc) % 4
                for c in range(22):
                    wc = 128 if c < 21 else 42
                    p.op("tensor",
                         lambda E, _c=c, _oc=oc, _pj=pj, _wc=wc, _hb=hb: E.matmul(
                             pscore[:, _pj * 512:(_pj + 1) * 512],
                             w2b[0:_wc, _c * 1024 + _oc * 128:
                                 _c * 1024 + (_oc + 1) * 128],
                             _hb[0:_wc, _c * 512:(_c + 1) * 512],
                             start=(_c == 0), stop=(_c == 21)),
                         reads=[("w2b", c), ("h2l", tb % 2, c)],
                         writes=[("pscore", pj)])
                p.op("scalar",
                     lambda E, _pj=pj, _oc=oc: E.activation(
                         tmpo[:], pscore[:, _pj * 512:(_pj + 1) * 512],
                         AF.Identity, bias=b2t[:, _oc:_oc + 1]),
                     reads=[("pscore", pj), ("f_b2t",)], writes=[("ftmpo",)])
                p.op("vector",
                     lambda E, _oc=oc, _t0=t0: E.tensor_tensor(
                         xi[_oc][:, 1 + _t0:1 + _t0 + 512],
                         xi[_oc][:, 1 + _t0:1 + _t0 + 512], tmpo[:], ALU.add),
                     reads=[("ftmpo",), ("xi", oc, tb)], writes=[("xi", oc, tb)])
        if dodump:
            kb.dump("fxim0", xi[0][:, 0:1024], [("xi", 0, 0), ("xi", 0, 1)])
            kb.dump("fh2l0", h2l[1][:, 0:1024], [("h2l", 1, 0), ("h2l", 1, 1)])

    def phase_ffn(s, inv):
        with kb.scope() as stk:
            xi = [kb.sbuf(f"f_xi{c}", [128, S + 1], F32, stack=stk)
                  for c in range(NCH)]
            rott = [kb.sbuf(f"f_rot{i}", [128, NTC * 64], F32, stack=stk)
                    for i in range(4)]
            for i in range(4):
                p.dma(rott[i][:], rotdram[i, :, :], writes=[("rot",)])
            with kb.scope() as stk0:
                xnb = [kb.sbuf(f"f_xnb{c}", [128, S], BF16, stack=stk0)
                       for c in range(NCH)]
                lntmp = kb.sbuf("f_lntmp", [128, 512], F32, stack=stk0)
                for dc in range(NCH):
                    p.op("vector", lambda E, _dc=dc: E.memset(xi[_dc][:, 0:1], 0.0),
                         writes=[("xi", dc, "pad")])
                stage_ln(stk0, s, xnb, first=False, ln_out=xi, out_off=1,
                         tmpf=lntmp)
            p.barrier()
            for l in range(2):
                with kb.scope() as stko:
                    of = [kb.sbuf(f"f_of{i}", [128, S], BF16, stack=stko)
                          for i in range(4)]
                    vt = [kb.sbuf(f"f_vt{i}", [128, 512], BF16, stack=stko)
                          for i in range(NTC)]
                    qff = [kb.sbuf(f"f_qf{i}", [128, S], BF16, stack=stko)
                           for i in range(4)]
                    kff = [kb.sbuf(f"f_kf{i}", [128, S], BF16, stack=stko)
                           for i in range(4)]
                    with kb.scope() as stka:
                        ffn_qkv_rotary(stka, s, l, xi, vt, qff, kff, rott)
                    p.barrier()
                    with kb.scope() as stka:
                        ffn_attn(stka, s, l, vt, qff, kff, of)
                    p.barrier()
                    with kb.scope() as stka:
                        ffn_wo(stka, s, l, xi, of)
                    p.barrier()
                with kb.scope() as stkm:
                    ffn_mlp_a(stkm, s, l, xi)
                    p.barrier()
                with kb.scope() as stkb:
                    ffn_mlp_b(stkb, s, l, xi)
                p.barrier()
            with kb.scope() as stk9:
                xblk = [kb.sbuf(f"f_oxb{i}", [128, 512], F32, stack=stk9)
                        for i in range(2)]
                for tb in range(NTB):
                    for dc in range(NCH):
                        bx = xblk[dc % 2]
                        p.dma(bx[:], xdram[s, dc, :, tb * 512:(tb + 1) * 512],
                              writes=[("oxblk", dc % 2)])
                        p.op("vector",
                             lambda E, _b=bx, _dc=dc, _t0=tb * 512: E.tensor_tensor(
                                 _b[:], _b[:], xi[_dc][:, 1 + _t0:1 + _t0 + 512],
                                 ALU.add),
                             reads=[("oxblk", dc % 2), ("xi", dc, tb)],
                             writes=[("oxblk", dc % 2)])
                        p.dma(xdram[s, dc, :, tb * 512:(tb + 1) * 512], bx[:],
                              reads=[("oxblk", dc % 2)])
        p.barrier()

    def phase_attn_b(s):
        with kb.scope() as stk:
            xnf = [kb.sbuf(f"b_xnf{c}", [128, S], F32, stack=stk)
                   for c in range(NCH)]
            with kb.scope() as stk0:
                xnb0 = [kb.sbuf(f"b_xnb0_{c}", [128, S], BF16, stack=stk0)
                        for c in range(NCH)]
                lntmp = kb.sbuf("b_lntmp", [128, 512], F32, stack=stk0)
                stage_ln(stk0, s, xnb0, first=False, ln_out=xnf, out_off=0,
                         tmpf=lntmp)
            p.barrier()
            XNF = [("xi", dc, tb) for dc in range(NCH) for tb in range(NTB)]

            def xnfr(dc):
                return [("xi", dc, t) for t in range(NTB)]

            with kb.scope() as stke:
                xnb8 = [kb.sbuf(f"b_xnb8_{c}", [128, S], BF16, stack=stke)
                        for c in range(NCH)]
                for dc in range(NCH):
                    p.op("scalar",
                         lambda E, _dc=dc: E.copy(xnb8[_dc][:], xnf[_dc][:]),
                         reads=xnfr(dc), writes=[("bxnb", dc)])
                posc = kb.sbuf("b_posc", [128, S], BF16, stack=stke)
                rpos = kb.sbuf("b_rpos", [128, S], BF16, stack=stke)
                with kb.scope() as stkt:
                    tia = kb.sbuf("b_tia", [128, S], F32, stack=stkt)
                    tia2 = kb.sbuf("b_tia2", [128, S], F32, stack=stkt)
                    g(lambda E: E.iota(tia[:], [[1, S]], base=1,
                                       channel_multiplier=0,
                                       allow_small_or_imprecise_dtypes=True),
                      [("btia",)])
                    kb.gfence([(tia[:], ("btia",))])
                    p.op("vector", lambda E: E.reciprocal(tia2[:], tia[:]),
                         reads=[("btia",)], writes=[("btia2",)])
                    p.op("scalar", lambda E: E.copy(rpos[:], tia2[:]),
                         reads=[("btia2",)], writes=[("brpos",)])
                    p.op("scalar", lambda E: E.activation(tia[:], tia[:], AF.Sqrt),
                         reads=[("btia",)], writes=[("btia",)])
                    p.op("vector", lambda E: E.reciprocal(tia2[:], tia[:]),
                         reads=[("btia",)], writes=[("btia2",)])
                    p.op("scalar", lambda E: E.copy(posc[:], tia2[:]),
                         reads=[("btia2",)], writes=[("bposc",)])
                p.barrier()
                womb = kb.sbuf("b_womb", [128, NCH * 1024], BF16, stack=stke)
                woms = kb.sbuf("b_woms", [128, 1024], F32, stack=stke)
                for dc in range(NCH):
                    p.dma(woms[:], dr["b_womT"][dc * 128:(dc + 1) * 128, :],
                          writes=[("bwoms",)])
                    p.op("scalar",
                         lambda E, _dc=dc: E.copy(
                             womb[:, _dc * 1024:(_dc + 1) * 1024], woms[:]),
                         reads=[("bwoms",)], writes=[("bwomb", dc)])
                els_t = load_col_table(stke, "b_els", dr["b_ls"][:], NCH)
                bom_t = load_col_table(stke, "b_bom", dr["b_bom"][:], NCH)
                p.op("scalar", lambda E: E.activation(els_t[:], els_t[:], AF.Exp),
                     reads=[("b_els",)], writes=[("b_els",)], small=[els_t[:]])
                ebom_t = kb.sbuf("b_ebom", [128, NCH], F32, stack=stke)
                p.op("vector", lambda E: E.tensor_mul(ebom_t[:], els_t[:], bom_t[:]),
                     reads=[("b_els",), ("b_bom",)], writes=[("b_ebom",)],
                     small=[ebom_t[:]])
                og = kb.sbuf("b_og", [128, S], F32, stack=stke)
                u_ = kb.sbuf("b_u", [128, S], F32, stack=stke)
                ki_ = kb.sbuf("b_ki", [128, S], mybir.dt.int32, stack=stke)
                phi = kb.sbuf("b_phi", [128, S], F32, stack=stke)
                kf_ = phi
                sinb = kb.sbuf("b_sin", [128, S], BF16, stack=stke)
                cosb = kb.sbuf("b_cos", [128, S], BF16, stack=stke)
                cmr = kb.sbuf("b_cmr", [128, S], BF16, stack=stke)
                cmi = kb.sbuf("b_cmi", [128, S], BF16, stack=stke)
                memr = kb.sbuf("b_memr", [128, S], F32, stack=stke)
                memi = kb.sbuf("b_memi", [128, S], F32, stack=stke)
                ones1 = kb.sbuf("b_ones", [128, 1], BF16, stack=stke)
                p.op("vector", lambda E: E.memset(ones1[:], 1.0),
                     writes=[("bones",)], small=[ones1[:]])
                onesb_f = None
                for ec in range(NCH):
                    for tb in range(NTB):
                        pj = tb % 2
                        for dc in range(NCH):
                            p.op("tensor",
                                 lambda E, _ec=ec, _tb=tb, _dc=dc, _pj=pj:
                                 E.matmul(pscore[:, _pj * 512:(_pj + 1) * 512],
                                          womb[:, _dc * 1024 + _ec * 128:
                                               _dc * 1024 + (_ec + 1) * 128],
                                          xnb8[_dc][:, _tb * 512:(_tb + 1) * 512],
                                          start=(_dc == 0), stop=(_dc == NCH - 1)),
                                 reads=[("bwomb", dc), ("bxnb", dc)],
                                 writes=[("pscore", pj)])
                        p.op("scalar",
                             lambda E, _ec=ec, _tb=tb, _pj=pj: E.activation(
                                 og[:, _tb * 512:(_tb + 1) * 512],
                                 pscore[:, _pj * 512:(_pj + 1) * 512], AF.Identity,
                                 scale=els_t[:, _ec:_ec + 1],
                                 bias=ebom_t[:, _ec:_ec + 1]),
                             reads=[("pscore", pj), ("b_els",), ("b_ebom",)],
                             writes=[("bog",)])
                    p.op("vector",
                         lambda E: E.tensor_tensor(og[:], og[:], posc[:], ALU.mult),
                         reads=[("bog",), ("bposc",)], writes=[("bog",)])
                    p.op("vector",
                         lambda E: E.tensor_tensor_scan(phi[:], ones1[:].broadcast_to((128, S)), og[:],
                                                        0.0, ALU.mult, ALU.add),
                         reads=[("bog",), ("bones",)], writes=[("bphi",)])
                    p.op("vector",
                         lambda E: E.tensor_scalar_mul(u_[:], phi[:],
                                                       float(1.0 / TWO_PI)),
                         reads=[("bphi",)], writes=[("bu",)])
                    p.op("vector", lambda E: E.tensor_copy(ki_[:], u_[:]),
                         reads=[("bu",)], writes=[("bki",)])
                    p.op("vector", lambda E: E.tensor_copy(kf_[:], ki_[:]),
                         reads=[("bki",)], writes=[("bphi",)])
                    p.op("vector", lambda E: E.tensor_sub(u_[:], u_[:], kf_[:]),
                         reads=[("bu",), ("bphi",)], writes=[("bu",)])
                    p.op("scalar",
                         lambda E: E.activation(sinb[:], u_[:], AF.Sin,
                                                scale=TWO_PI),
                         reads=[("bu",)], writes=[("bsin",)])
                    p.op("vector", lambda E: E.tensor_scalar_add(u_[:], u_[:], 0.25),
                         reads=[("bu",)], writes=[("bu",)])
                    p.op("vector",
                         lambda E: E.tensor_scalar(kf_[:], u_[:], 0.5, 1.0,
                                                   ALU.is_gt, ALU.mult),
                         reads=[("bu",)], writes=[("bphi",)])
                    p.op("vector",
                         lambda E: E.scalar_tensor_tensor(u_[:], kf_[:], -1.0,
                                                          u_[:], ALU.mult, ALU.add),
                         reads=[("bphi",), ("bu",)], writes=[("bu",)])
                    p.op("scalar",
                         lambda E: E.activation(cosb[:], u_[:], AF.Sin,
                                                scale=TWO_PI),
                         reads=[("bu",)], writes=[("bcos",)])
                    p.op("vector",
                         lambda E, _ec=ec: E.tensor_tensor(
                             cmr[:], xnf[_ec][:], cosb[:], ALU.mult),
                         reads=xnfr(ec) + [("bcos",)], writes=[("bcmr",)])
                    p.op("vector",
                         lambda E, _ec=ec: E.tensor_tensor(
                             cmi[:], xnf[_ec][:], sinb[:], ALU.mult),
                         reads=xnfr(ec) + [("bsin",)], writes=[("bcmi",)])
                    p.op("vector",
                         lambda E: E.tensor_tensor_scan(memr[:], ones1[:].broadcast_to((128, S)),
                                                        cmr[:], 0.0, ALU.mult,
                                                        ALU.add),
                         reads=[("bcmr",), ("bones",)], writes=[("bmemr",)])
                    p.op("vector",
                         lambda E: E.tensor_tensor(memr[:], memr[:], rpos[:],
                                                   ALU.mult),
                         reads=[("bmemr",), ("brpos",)], writes=[("bmemr",)])
                    p.op("vector",
                         lambda E: E.tensor_tensor_scan(memi[:], ones1[:].broadcast_to((128, S)),
                                                        cmi[:], 0.0, ALU.mult,
                                                        ALU.add),
                         reads=[("bcmi",), ("bones",)], writes=[("bmemi",)])
                    p.op("vector",
                         lambda E: E.tensor_tensor(memi[:], memi[:], rpos[:],
                                                   ALU.mult),
                         reads=[("bmemi",), ("brpos",)], writes=[("bmemi",)])
                    if os.environ.get("KDBG") and s == 0 and ec == 0:
                        kb.dump("bog0", og[:], [("bog",)])
                        kb.dump("bphi0", phi[:], [("bphi",)])
                        kb.dump("bcos0", cosb[:], [("bcos",)])
                        kb.dump("bsin0", sinb[:], [("bsin",)])
                        kb.dump("bcmr0", cmr[:], [("bcmr",)])
                        kb.dump("bmemr0", memr[:], [("bmemr",)])
                    p.dma(ctxdram[ec, :, :], cmr[:], reads=[("bcmr",)])
                    p.dma(ctxdram[8 + ec, :, :], cmi[:], reads=[("bcmi",)])
                    p.op("vector",
                         lambda E: E.tensor_tensor(og[:], memr[:], cosb[:],
                                                   ALU.mult),
                         reads=[("bmemr",), ("bcos",)], writes=[("bog",)])
                    p.op("vector",
                         lambda E: E.tensor_tensor(u_[:], memi[:], sinb[:],
                                                   ALU.mult),
                         reads=[("bmemi",), ("bsin",)], writes=[("bu",)])
                    p.op("vector",
                         lambda E: E.tensor_tensor(cmr[:], og[:], u_[:], ALU.add),
                         reads=[("bog",), ("bu",)], writes=[("bcmr",)])
                    p.op("vector",
                         lambda E: E.tensor_tensor(og[:], memi[:], cosb[:],
                                                   ALU.mult),
                         reads=[("bmemi",), ("bcos",)], writes=[("bog",)])
                    p.op("vector",
                         lambda E: E.tensor_tensor(u_[:], memr[:], sinb[:],
                                                   ALU.mult),
                         reads=[("bmemr",), ("bsin",)], writes=[("bu",)])
                    p.op("vector",
                         lambda E: E.tensor_tensor(cmi[:], og[:], u_[:],
                                                   ALU.subtract),
                         reads=[("bog",), ("bu",)], writes=[("bcmi",)])
                    if os.environ.get("KDBG") and s == 0 and ec == 0:
                        kb.dump("bretr0", cmr[:], [("bcmr",)])
                    p.dma(ctxdram[16 + ec, :, :], cmr[:], reads=[("bcmr",)])
                    p.dma(ctxdram[24 + ec, :, :], cmi[:], reads=[("bcmi",)])
            p.barrier()

            with kb.scope() as stkl:
                rsum = kb.sbuf("b_rsum", [1, S], F32, stack=stkl)
                rsq = kb.sbuf("b_rsq", [1, S], F32, stack=stkl)
                cblk = [kb.sbuf(f"b_cblk{i}", [128, S], BF16, stack=stkl)
                        for i in range(2)]
                csq = kb.sbuf("b_csq", [128, S], BF16, stack=stkl)
                gam_t = load_col_table(stkl, "b_gam", dr["b_lg"][:], 32)
                bet_t = load_col_table(stkl, "b_bet", dr["b_lb"][:], 32)
                p.op("vector", lambda E: E.memset(rsum[:], 0.0), writes=[("brsum",)])
                p.op("vector", lambda E: E.memset(rsq[:], 0.0), writes=[("brsq",)])
                for cc in range(32):
                    b = cblk[cc % 2]
                    p.dma(b[:], ctxdram[cc, :, :], writes=[("bcblk", cc % 2)])
                    p.op("scalar",
                         lambda E, _b=b: E.activation(csq[:], _b[:], AF.Square),
                         reads=[("bcblk", cc % 2)], writes=[("bcsq",)])
                    for tb in range(NTB):
                        bank, bres = (pfa, ("pfa",)) if tb % 2 == 0 else (pfb, ("pfb",))
                        p.op("tensor",
                             lambda E, _tb=tb, _b=b, _bk=bank:
                             E.matmul(_bk[0:1, :], ones_col_b[:],
                                      _b[:, _tb * 512:(_tb + 1) * 512],
                                      start=True, stop=True),
                             reads=[("bcblk", cc % 2), ("ones_col_b",)],
                             writes=[bres])
                        p.op("vector",
                             lambda E, _tb=tb, _bk=bank: E.tensor_tensor(
                                 rsum[0:1, _tb * 512:(_tb + 1) * 512],
                                 rsum[0:1, _tb * 512:(_tb + 1) * 512],
                                 _bk[0:1, :], ALU.add),
                             reads=[bres, ("brsum",)], writes=[("brsum",)])
                    for tb in range(NTB):
                        bank, bres = (pfa, ("pfa",)) if tb % 2 == 0 else (pfb, ("pfb",))
                        p.op("tensor",
                             lambda E, _tb=tb, _bk=bank:
                             E.matmul(_bk[0:1, :], ones_col_b[:],
                                      csq[:, _tb * 512:(_tb + 1) * 512],
                                      start=True, stop=True),
                             reads=[("bcsq",), ("ones_col_b",)], writes=[bres])
                        p.op("vector",
                             lambda E, _tb=tb, _bk=bank: E.tensor_tensor(
                                 rsq[0:1, _tb * 512:(_tb + 1) * 512],
                                 rsq[0:1, _tb * 512:(_tb + 1) * 512],
                                 _bk[0:1, :], ALU.add),
                             reads=[bres, ("brsq",)], writes=[("brsq",)])
                rt = kb.sbuf("b_rt", [1, S], F32, stack=stkl)
                p.op("vector", lambda E: E.tensor_scalar_mul(rsum[:], rsum[:],
                                                             1.0 / 4096.0),
                     reads=[("brsum",)], writes=[("brsum",)])
                p.op("vector", lambda E: E.tensor_scalar_mul(rsq[:], rsq[:],
                                                             1.0 / 4096.0),
                     reads=[("brsq",)], writes=[("brsq",)])
                p.op("vector", lambda E: E.tensor_mul(rt[:], rsum[:], rsum[:]),
                     reads=[("brsum",)], writes=[("brt",)])
                p.op("vector", lambda E: E.tensor_sub(rsq[:], rsq[:], rt[:]),
                     reads=[("brsq",), ("brt",)], writes=[("brsq",)])
                p.op("vector", lambda E: E.tensor_scalar_add(rsq[:], rsq[:], EPS),
                     reads=[("brsq",)], writes=[("brsq",)])
                p.op("scalar", lambda E: E.activation(rsq[:], rsq[:], AF.Sqrt),
                     reads=[("brsq",)], writes=[("brsq",)], small=[rsq[:]])
                p.op("vector", lambda E: E.reciprocal(rt[:], rsq[:]),
                     reads=[("brsq",)], writes=[("brt",)])
                p.op("vector", lambda E: E.tensor_mul(rsum[:], rsum[:], rt[:]),
                     reads=[("brsum",), ("brt",)], writes=[("brsum",)])
                p.op("vector", lambda E: E.tensor_scalar_mul(rsum[:], rsum[:], -1.0),
                     reads=[("brsum",)], writes=[("brsum",)],
                     small=[rsum[:], rt[:]])
                if os.environ.get("KDBG") and s == 0:
                    kb.dump("brows_mu", rsum[:], [("brsum",)])
                    kb.dump("brows_rt", rt[:], [("brt",)])
                happ = [kb.sbuf(f"b_happ{i}", [128, 512], F32, stack=stkl)
                        for i in range(4)]
                hblk2 = [kb.sbuf(f"b_hap{i}", [128, 512], BF16, stack=stkl)
                         for i in range(4)]
                for tb in range(NTB):
                    t0 = tb * 512
                    p.op("tensor",
                         lambda E, _t0=t0: E.matmul(pfc[:], ones_row_f[:],
                                                    rt[0:1, _t0:_t0 + 512],
                                                    start=True, stop=True),
                         reads=[("brt",), ("ones_row_f",)], writes=[("pfc",)])
                    p.op("tensor",
                         lambda E, _t0=t0: E.matmul(pscore[:, 1536:2048],
                                                    ones_row_f[:],
                                                    rsum[0:1, _t0:_t0 + 512],
                                                    start=True, stop=True),
                         reads=[("brsum",), ("ones_row_f",)],
                         writes=[("pscore", 3)])
                    for cc in range(32):
                        hb2 = hblk2[cc % 4]
                        ha = happ[cc % 4]
                        p.dma(hb2[:], ctxdram[cc, :, t0:t0 + 512],
                              writes=[("bhap", cc % 4)])
                        p.op("vector",
                             lambda E, _b=hb2, _a=ha: E.tensor_tensor(
                                 _a[:], _b[:], pfc[:], ALU.mult),
                             reads=[("bhap", cc % 4), ("pfc",)],
                             writes=[("bhapp", cc % 4)])
                        p.op("vector",
                             lambda E, _a=ha: E.tensor_tensor(
                                 _a[:], _a[:], pscore[:, 1536:2048], ALU.add),
                             reads=[("bhapp", cc % 4), ("pscore", 3)],
                             writes=[("bhapp", cc % 4)])
                        p.op("scalar",
                             lambda E, _cc=cc, _b=hb2, _a=ha: E.activation(
                                 _b[:], _a[:], AF.Identity,
                                 scale=gam_t[:, _cc:_cc + 1],
                                 bias=bet_t[:, _cc:_cc + 1]),
                             reads=[("bhapp", cc % 4), ("b_gam",), ("b_bet",)],
                             writes=[("bhap", cc % 4)])
                        if (os.environ.get("KDBG") and s == 0 and tb == 0
                                and cc == 0):
                            kb.dump("bh0", hb2[:], [("bhap", 0)])
                        p.dma(ctxdram[cc, :, t0:t0 + 512], hb2[:],
                              reads=[("bhap", cc % 4)])
            p.barrier()

            if s == 0:
                with kb.scope() as stkc:
                    cw1s = [kb.sbuf(f"b_cw1s{i}", [128, 32 * 128], F32,
                                    stack=stkc) for i in range(4)]
                    cw1b = [kb.sbuf(f"b_cw1b{i}", [128, 32 * 128], BF16,
                                    stack=stkc) for i in range(4)]
                    for oc in range(16):
                        pr = oc % 4
                        p.dma(cw1s[pr][:].rearrange("p (c n) -> p c n", c=32),
                              dr["b_w1T"][:, oc * 128:(oc + 1) * 128]
                              .rearrange("(c p) n -> p c n", p=128),
                              writes=[("bcw1s", pr)])
                        p.op("scalar",
                             lambda E, _p=pr: E.copy(cw1b[_p][:], cw1s[_p][:]),
                             reads=[("bcw1s", pr)], writes=[("bcw1b", pr)])
                        p.dma(bw1c[oc], cw1b[pr][:], reads=[("bcw1b", pr)])
                p.barrier()

            with kb.scope() as stkw:
                w2b = kb.sbuf("b_w2b", [128, 16 * 1024], BF16, stack=stkw)
                w2s = [kb.sbuf(f"b_w2s{i}", [128, 1024], F32, stack=stkw)
                       for i in range(2)]
                for c2 in range(16):
                    st_ = w2s[c2 % 2]
                    p.dma(st_[:], dr["b_w2T"][c2 * 128:(c2 + 1) * 128, :],
                          writes=[("bw2s", c2 % 2)])
                    p.op("scalar",
                         lambda E, _c=c2, _s=st_: E.copy(
                             w2b[:, _c * 1024:(_c + 1) * 1024], _s[:]),
                         reads=[("bw2s", c2 % 2)], writes=[("bw2b", c2)])
                b1t = load_col_table(stkw, "b_b1t", dr["b_b1"][:], 16)
                b2t = load_col_table(stkw, "b_b2t", dr["b_b2"][:], NCH)
                w1b = [kb.sbuf(f"b_w1b{i}", [128, 32 * 128], BF16, stack=stkw)
                       for i in range(2)]
                hblk = [kb.sbuf(f"b_hblk{c}", [128, 512], BF16, stack=stkw)
                        for c in range(32)]
                o1g = [kb.sbuf(f"b_o1g{c}", [128, 512], BF16, stack=stkw)
                       for c in range(16)]
                tmpo = kb.sbuf("b_tmpo", [128, 512], F32, stack=stkw)
                xblk = [kb.sbuf(f"b_xblk{i}", [128, 512], F32, stack=stkw)
                        for i in range(2)]
                for tb in range(NTB):
                    t0 = tb * 512
                    for cc in range(32):
                        p.dma(hblk[cc][:], ctxdram[cc, :, t0:t0 + 512],
                              writes=[("bhblk", cc)])
                    for oc in range(16):
                        wb = w1b[oc % 2]
                        p.dma(wb[:], bw1c[oc], writes=[("bw1b", oc % 2)])
                        pj = oc % 2
                        for cc in range(32):
                            p.op("tensor",
                                 lambda E, _cc=cc, _pj=pj, _w=wb: E.matmul(
                                     pscore[:, _pj * 512:(_pj + 1) * 512],
                                     _w[:, _cc * 128:(_cc + 1) * 128],
                                     hblk[_cc][:],
                                     start=(_cc == 0), stop=(_cc == 31)),
                                 reads=[("bw1b", oc % 2), ("bhblk", cc)],
                                 writes=[("pscore", pj)])
                        p.op("scalar",
                             lambda E, _oc=oc, _pj=pj: E.activation(
                                 o1g[_oc][:], pscore[:, _pj * 512:(_pj + 1) * 512],
                                 AF.Gelu, bias=b1t[:, _oc:_oc + 1]),
                             reads=[("pscore", pj), ("b_b1t",)],
                             writes=[("bo1g", oc)])
                        if os.environ.get("KDBG") and s == 0 and tb == 0:
                            kb.dump(f"bo1g{oc}", o1g[oc][:], [("bo1g", oc)])
                    for oc2 in range(NCH):
                        pj = 2 + oc2 % 2
                        for c2 in range(16):
                            p.op("tensor",
                                 lambda E, _c=c2, _oc=oc2, _pj=pj: E.matmul(
                                     pscore[:, _pj * 512:(_pj + 1) * 512],
                                     w2b[:, _c * 1024 + _oc * 128:
                                         _c * 1024 + (_oc + 1) * 128],
                                     o1g[_c][:],
                                     start=(_c == 0), stop=(_c == 15)),
                                 reads=[("bw2b", c2), ("bo1g", c2)],
                                 writes=[("pscore", pj)])
                        p.op("scalar",
                             lambda E, _oc=oc2, _pj=pj: E.activation(
                                 tmpo[:], pscore[:, _pj * 512:(_pj + 1) * 512],
                                 AF.Identity, bias=b2t[:, _oc:_oc + 1]),
                             reads=[("pscore", pj), ("b_b2t",)],
                             writes=[("btmpo",)])
                        if (os.environ.get("KDBG") and s == 0 and tb == 0
                                and oc2 == 0):
                            kb.dump("bout2_0", tmpo[:], [("btmpo",)])
                        p.op("vector",
                             lambda E, _oc=oc2, _t0=t0: E.tensor_tensor(
                                 tmpo[:], tmpo[:], xnf[_oc][:, _t0:_t0 + 512],
                                 ALU.add),
                             reads=[("btmpo",), ("xi", oc2, tb)],
                             writes=[("btmpo",)])
                        bx = xblk[oc2 % 2]
                        p.dma(bx[:], xdram[s, oc2, :, t0:t0 + 512],
                              writes=[("bxblk", oc2 % 2)])
                        p.op("vector",
                             lambda E, _b=bx: E.tensor_tensor(
                                 _b[:], _b[:], tmpo[:], ALU.add),
                             reads=[("bxblk", oc2 % 2), ("btmpo",)],
                             writes=[("bxblk", oc2 % 2)])
                        p.dma(xdram[s, oc2, :, t0:t0 + 512], bx[:],
                              reads=[("bxblk", oc2 % 2)])
        p.barrier()

    def phase_output(s):
        with kb.scope() as stk:
            xfm = [kb.sbuf(f"o_xfm{i}", [128, 512], F32, stack=stk) for i in range(NCH)]
            otm = kb.sbuf("o_otm", [128, D], F32, stack=stk)
            for tb in range(NTB):
                t0 = tb * 512
                for dc in range(NCH):
                    p.dma(xfm[dc][:], xdram[s, dc, :, t0:t0 + 512],
                          writes=[("oxfm", dc)])
                for t4 in range(4):
                    pj = t4 % 2
                    for dc in range(NCH):
                        p.op("tensor",
                             lambda E, _t4=t4, _dc=dc, _pj=pj:
                             E.transpose(
                                 pscore[:, _pj * 1024 + _dc * 128: _pj * 1024 + (_dc + 1) * 128],
                                 xfm[_dc][:, _t4 * 128:(_t4 + 1) * 128],
                                 identf[:]),
                             reads=[("oxfm", dc), ("identf",)],
                             writes=[("pscore", pj * 2), ("pscore", pj * 2 + 1)])
                    p.op("vector",
                         lambda E, _pj=pj: E.tensor_copy(
                             otm[:], pscore[:, _pj * 1024:(_pj + 1) * 1024]),
                         reads=[("pscore", pj * 2), ("pscore", pj * 2 + 1)],
                         writes=[("otm",)])
                    p.dma(out[s, t0 + t4 * 128: t0 + (t4 + 1) * 128, :], otm[:],
                          reads=[("otm",)])
        p.barrier()

    for s in range(NSEQ):
        if "attn_a" in SUBLAYERS:
            phase_attn_a(s, first=True)
    if "ffn1" in SUBLAYERS or "ffn2" in SUBLAYERS:
        build_rotary()
    if "ffn1" in SUBLAYERS:
        for s in range(NSEQ):
            phase_ffn(s, 1)
    if "attn_b" in SUBLAYERS:
        for s in range(NSEQ):
            phase_attn_b(s)
    if "ffn2" in SUBLAYERS:
        for s in range(NSEQ):
            phase_ffn(s, 2)
    for s in range(NSEQ):
        phase_output(s)


_CACHE = {}


def _get_built():
    if "nc" not in _CACHE:
        kb = KB()
        kb.stack.__enter__()
        try:
            build(kb)
            kb.p.finalize_and_emit(kb.stack)
        finally:
            pass
        _CACHE["kb"] = kb
        _CACHE["nc"] = kb.nc
        _CACHE["stats"] = (dict(kb.p.cum), kb.p.n_waits)
    return _CACHE["nc"]


def kernel(L0_attn_a_x, L0_attn_a_w_qkv, L0_attn_a_b_qkv, L0_attn_a_w_out,
           L0_attn_a_b_out, L0_attn_b_W_omega, L0_attn_b_b_omega,
           L0_attn_b_log_scale, L0_attn_b_ln_gamma, L0_attn_b_ln_beta,
           L0_attn_b_W1, L0_attn_b_b1, L0_attn_b_W2, L0_attn_b_b2,
           L0_ffn_wqkv, L0_ffn_q_scale, L0_ffn_k_scale, L0_ffn_wo,
           L0_ffn_ln_g, L0_ffn_ln_b, L0_ffn_w1, L0_ffn_b1, L0_ffn_w2,
           L0_ffn_b2):
    nc = _get_built()
    x = np.ascontiguousarray(np.asarray(L0_attn_a_x, dtype=np.float32))
    com = {
        "aw_qkv": np.ascontiguousarray(L0_attn_a_w_qkv, dtype=np.float32),
        "ab_qkv": np.ascontiguousarray(L0_attn_a_b_qkv, dtype=np.float32),
        "aw_out": np.ascontiguousarray(L0_attn_a_w_out, dtype=np.float32),
        "ab_out": np.ascontiguousarray(L0_attn_a_b_out, dtype=np.float32),
        "f_wqkv": np.ascontiguousarray(L0_ffn_wqkv, dtype=np.float32),
        "f_qs": np.ascontiguousarray(L0_ffn_q_scale, dtype=np.float32),
        "f_ks": np.ascontiguousarray(L0_ffn_k_scale, dtype=np.float32),
        "f_wo": np.ascontiguousarray(L0_ffn_wo, dtype=np.float32),
        "f_lg": np.ascontiguousarray(L0_ffn_ln_g, dtype=np.float32),
        "f_lb": np.ascontiguousarray(L0_ffn_ln_b, dtype=np.float32),
        "f_w1": np.ascontiguousarray(L0_ffn_w1, dtype=np.float32),
        "f_b1": np.ascontiguousarray(L0_ffn_b1, dtype=np.float32),
        "f_w2": np.ascontiguousarray(L0_ffn_w2, dtype=np.float32),
        "f_b2": np.ascontiguousarray(L0_ffn_b2, dtype=np.float32),
        "b_womT": np.ascontiguousarray(np.asarray(L0_attn_b_W_omega).T,
                                       dtype=np.float32),
        "b_bom": np.ascontiguousarray(L0_attn_b_b_omega, dtype=np.float32),
        "b_ls": np.ascontiguousarray(L0_attn_b_log_scale, dtype=np.float32),
        "b_lg": np.ascontiguousarray(L0_attn_b_ln_gamma, dtype=np.float32),
        "b_lb": np.ascontiguousarray(L0_attn_b_ln_beta, dtype=np.float32),
        "b_w1T": np.ascontiguousarray(np.asarray(L0_attn_b_W1).T,
                                      dtype=np.float32),
        "b_b1": np.ascontiguousarray(L0_attn_b_b1, dtype=np.float32),
        "b_w2T": np.ascontiguousarray(np.asarray(L0_attn_b_W2).T,
                                      dtype=np.float32),
        "b_b2": np.ascontiguousarray(L0_attn_b_b2, dtype=np.float32),
    }
    in_maps = []
    for c in range(8):
        m = dict(com)
        m["x_in"] = x[2 * c:2 * c + 2]
        in_maps.append(m)
    res = run_bass_kernel_spmd(nc, in_maps, core_ids=list(range(8)))
    return np.concatenate([r["out"] for r in res.results], axis=0)


# version 06
# speedup vs baseline: 1.0149x; 1.0149x over previous
import contextlib
import os

import numpy as np

import concourse.bass as bass
import concourse.mybir as mybir
from concourse.bass_utils import run_bass_kernel_spmd

F32 = mybir.dt.float32
BF16 = mybir.dt.bfloat16
AF = mybir.ActivationFunctionType
ALU = mybir.AluOpType
AX = mybir.AxisListType

B, S, D = 16, 2048, 1024
NSEQ = 2
NCH = 8
NTC = 16
NTB = 4
TWO_PI = float(2.0 * np.pi)
NEG = -1.0e30
EPS = 1e-5

SUBLAYERS = tuple(os.environ.get("KSUB", "attn_a,ffn1,attn_b,ffn2").split(","))
KSTAGE = int(os.environ.get("KSTAGE", "4"))

ENGINES = ("tensor", "scalar", "vector", "gpsimd", "sync")
N_DMA_SEMS = 16


class Prog:
    def __init__(self, nc):
        self.nc = nc
        self.ops = {e: [] for e in ENGINES}
        self.cum = {e: 0 for e in ENGINES}
        self.known = {e: {} for e in ENGINES}
        self.last_w = {}
        self.readers = {}
        self.dma_cum = [0] * N_DMA_SEMS
        self.dma_rr = 0
        self.devnull = None
        self.n_waits = 0

    def _need_wait(self, eng, skey, val):
        k = self.known[eng]
        if k.get(skey, 0) >= val:
            return None
        k[skey] = val
        return (skey, val)

    def _deps(self, eng, reads, writes):
        waits = []
        me = ("e", eng)
        for r in reads:
            lw = self.last_w.get(r)
            if lw is not None and lw[0] != me:
                w = self._need_wait(eng, lw[0], lw[1])
                if w:
                    waits.append(w)
        for r in writes:
            lw = self.last_w.get(r)
            if lw is not None and lw[0] != me:
                w = self._need_wait(eng, lw[0], lw[1])
                if w:
                    waits.append(w)
            for skey, v in self.readers.get(r, {}).items():
                if skey != me:
                    w = self._need_wait(eng, skey, v)
                    if w:
                        waits.append(w)
        self.n_waits += len(waits)
        return waits

    def _commit(self, key, reads, writes, val):
        for r in reads:
            self.readers.setdefault(r, {})[key] = val
        for r in writes:
            self.last_w[r] = (key, val)
            self.readers[r] = {}

    def op(self, eng, emit, reads=(), writes=(), small=()):
        waits = self._deps(eng, reads, writes)
        self.cum[eng] += 1
        my = self.cum[eng]
        self.ops[eng].append((waits, emit, ("e", eng), 1))
        self._commit(("e", eng), reads, writes, my)
        if small:
            aps = list(small)
            self.cum[eng] += 1
            tv = self.cum[eng]

            def temit(E, _aps=aps, _eng=eng):
                ins = None
                for ap in _aps:
                    pdim, f = ap.shape[0], ap.free_size()
                    if f > 512:
                        ap = ap[:, 0:512] if ap.ndim == 2 else ap
                        f = 512
                    dn = self.devnull[_eng]
                    if _eng == "scalar":
                        ins = E.copy(dn[0:pdim, 0:f], ap)
                    else:
                        ins = E.tensor_copy(dn[0:pdim, 0:f], ap)
                return ins

            self.ops[eng].append(([], temit, ("e", eng), 1))
            for r in writes:
                self.last_w[r] = (("e", eng), tv)
        return my

    def dma(self, out_ap, in_ap, reads=(), writes=(), eng="sync", slow_ok=False):
        waits = self._deps(eng, reads, writes)
        si = self.dma_rr
        self.dma_rr = (self.dma_rr + 1) % N_DMA_SEMS
        prev = self.dma_cum[si]
        self.dma_cum[si] += 16
        val = self.dma_cum[si]
        key = ("d", si)
        if prev > 0:
            w = self._need_wait(eng, key, prev)
            if w:
                waits.append(w)

        def emit(E, _o=out_ap, _i=in_ap, _s=slow_ok):
            return E.dma_start(_o, _i, allow_slow_non_contiguous=_s)

        self.ops[eng].append((waits, emit, key, 16))
        self._commit(key, reads, writes, val)
        return key, val

    def barrier(self):
        targets = [(("e", e), self.cum[e]) for e in ENGINES if self.cum[e] > 0]
        targets += [(("d", i), self.dma_cum[i]) for i in range(N_DMA_SEMS)
                    if self.dma_cum[i] > 0]
        for e in ENGINES:
            waits = []
            for skey, v in targets:
                if skey == ("e", e):
                    continue
                w = self._need_wait(e, skey, v)
                if w:
                    waits.append(w)
            if waits:
                self.ops[e].append((waits, None, None, 0))

    def finalize_and_emit(self, stack):
        nc = self.nc
        esem = {e: stack.enter_context(nc.semaphore("sem_" + e)) for e in ENGINES}
        dsem = [stack.enter_context(nc.semaphore(f"dma{i}")) for i in range(N_DMA_SEMS)]

        def sem_of(skey):
            return esem[skey[1]] if skey[0] == "e" else dsem[skey[1]]

        waits = []
        for i in range(N_DMA_SEMS):
            if self.dma_cum[i] > 0:
                waits.append((("d", i), self.dma_cum[i]))
        for e in ENGINES:
            if e != "sync" and self.cum[e] > 0:
                waits.append((("e", e), self.cum[e]))
        self.ops["sync"].append((waits, None, None, 0))

        with nc.Block() as block:
            for ename in ENGINES:
                def section(E, _ops=self.ops[ename]):
                    for w, emit, skey, amt in _ops:
                        for wk, wv in w:
                            E.wait_ge(sem_of(wk), wv)
                        if emit is not None:
                            ins = emit(E)
                            if skey is not None:
                                ins.then_inc(sem_of(skey), amt)
                getattr(block, ename)(section)


class KB:
    def __init__(self):
        self.nc = bass.Bass("TRN2", target_bir_lowering=False)
        self.p = Prog(self.nc)
        self.stack = contextlib.ExitStack()
        self._uid = 0

    def sbuf(self, name, shape, dt=F32, stack=None):
        self._uid += 1
        return (stack or self.stack).enter_context(
            self.nc.sbuf_tensor(f"{name}_u{self._uid}", shape, dt))

    def psum(self, name, shape, dt=F32):
        return self.stack.enter_context(self.nc.psum_tensor(name, shape, dt))

    def scope(self):
        return contextlib.ExitStack()

    def gfence(self, items):
        if not hasattr(self, "_fence_f32"):
            self._fence_f32 = self.nc.dram_tensor("fence_f32", [128, 4096], F32)
            self._fence_b16 = self.nc.dram_tensor("fence_b16", [128, 8192], BF16)
        for ap, res in items:
            tgt = self._fence_b16 if ap.dtype == BF16 else self._fence_f32
            f = ap.free_size()
            pdim = ap.shape[0]
            self.p.dma(tgt[0:pdim, 0:f], ap, reads=[res], writes=[res], slow_ok=(f < 8))

    def dump(self, name, ap, reads):
        t = self.nc.dram_tensor("dbg_" + name, list(ap.shape), ap.dtype,
                                kind="ExternalOutput")
        self.p.dma(t[...], ap, reads=reads)

    def register_consts(self, vals):
        vals = sorted(set(float(v) for v in vals))
        self.cpool = self.sbuf("cpool", [128, max(1, len(vals))], F32)
        for j, v in enumerate(vals):
            self.nc.const_aps.register(F32, v, self.cpool[:, j:j + 1])
        n = len(vals)
        for j, v in enumerate(vals):
            self.p.op("vector",
                      lambda E, _v=v, _j=j: E.memset(self.cpool[:, _j:_j + 1], _v),
                      writes=[("cpool",)],
                      small=([self.cpool[:, 0:n]] if j == n - 1 else ()))


def build(kb):
    nc, p = kb.nc, kb.p

    x_in = nc.dram_tensor("x_in", [NSEQ, S, D], F32, kind="ExternalInput")
    out = nc.dram_tensor("out", [NSEQ, S, D], F32, kind="ExternalOutput")
    dr = {}
    dr["aw_qkv"] = nc.dram_tensor("aw_qkv", [D, 3 * D], F32, kind="ExternalInput")
    dr["ab_qkv"] = nc.dram_tensor("ab_qkv", [3 * D], F32, kind="ExternalInput")
    dr["aw_out"] = nc.dram_tensor("aw_out", [D, D], F32, kind="ExternalInput")
    dr["ab_out"] = nc.dram_tensor("ab_out", [D], F32, kind="ExternalInput")
    dr["f_wqkv"] = nc.dram_tensor("f_wqkv", [2, D, 1536], F32, kind="ExternalInput")
    dr["f_qs"] = nc.dram_tensor("f_qs", [2, 64], F32, kind="ExternalInput")
    dr["f_ks"] = nc.dram_tensor("f_ks", [2, 64], F32, kind="ExternalInput")
    dr["f_wo"] = nc.dram_tensor("f_wo", [2, 512, D], F32, kind="ExternalInput")
    dr["f_lg"] = nc.dram_tensor("f_lg", [2, D], F32, kind="ExternalInput")
    dr["f_lb"] = nc.dram_tensor("f_lb", [2, D], F32, kind="ExternalInput")
    dr["f_w1"] = nc.dram_tensor("f_w1", [2, D, 5460], F32, kind="ExternalInput")
    dr["f_b1"] = nc.dram_tensor("f_b1", [2, 5460], F32, kind="ExternalInput")
    dr["f_w2"] = nc.dram_tensor("f_w2", [2, 2730, D], F32, kind="ExternalInput")
    dr["f_b2"] = nc.dram_tensor("f_b2", [2, D], F32, kind="ExternalInput")
    h2dram = nc.dram_tensor("h2dram", [22, 128, S], BF16)
    dr["b_womT"] = nc.dram_tensor("b_womT", [D, D], F32, kind="ExternalInput")
    dr["b_bom"] = nc.dram_tensor("b_bom", [D], F32, kind="ExternalInput")
    dr["b_ls"] = nc.dram_tensor("b_ls", [D], F32, kind="ExternalInput")
    dr["b_lg"] = nc.dram_tensor("b_lg", [4096], F32, kind="ExternalInput")
    dr["b_lb"] = nc.dram_tensor("b_lb", [4096], F32, kind="ExternalInput")
    dr["b_w1T"] = nc.dram_tensor("b_w1T", [4096, 2048], F32, kind="ExternalInput")
    dr["b_b1"] = nc.dram_tensor("b_b1", [2048], F32, kind="ExternalInput")
    dr["b_w2T"] = nc.dram_tensor("b_w2T", [2048, D], F32, kind="ExternalInput")
    dr["b_b2"] = nc.dram_tensor("b_b2", [D], F32, kind="ExternalInput")
    ctxdram = nc.dram_tensor("ctxdram", [32, 128, S], BF16)
    rotdram = nc.dram_tensor("rotdram", [4, 128, NTC * 64], F32)
    bw1c = nc.dram_tensor("bw1c", [16, 128, 32 * 128], BF16)
    fw1c = nc.dram_tensor("fw1c", [2, 2, 22, 128, 1024], BF16)
    built_fw1c = set()

    xdram = nc.dram_tensor("xdram", [NSEQ, NCH, 128, S], F32)

    _dn = kb.sbuf("devnull", [128, 512], F32)
    p.devnull = {e: _dn for e in ("vector", "scalar", "gpsimd")}

    kb.register_consts([0.125, 1.0 / D, 1.0 / 4096.0, TWO_PI, 0.25,
                        -1.0, 0.5, 2.0, 8.0, EPS, 1e-12, -0.5,
                        float(-np.log(10000.0) / 32.0), float(np.log(8.0)),
                        float(1.0 / 256.0), 1e-24])

    identb = kb.sbuf("identb", [128, 128], BF16)
    identf = kb.sbuf("identf", [128, 128], F32)
    tri = kb.sbuf("tri", [128, 128], BF16)
    ones_col_b = kb.sbuf("ones_col_b", [128, 1], BF16)
    ones_row_f = kb.sbuf("ones_row_f", [1, 128], F32)
    ones2 = kb.sbuf("ones2", [2, 128], F32)
    rowsbuf = kb.sbuf("rowsbuf", [1, 3 * 512], F32)
    rs2_sb = kb.sbuf("rs2_sb", [2, 512], F32)
    rsbc_sb = kb.sbuf("rsbc_sb", [128, 512], F32)

    def g(emit, writes, reads=()):
        p.op("gpsimd", emit, reads=reads, writes=writes)

    g(lambda E: E.memset(identb[:], 0.0), [("identb",)])
    g(lambda E: E.affine_select(out=identb[:], in_=identb[:], compare_op=ALU.not_equal,
                                fill=1.0, base=0, pattern=[[-1, 128]],
                                channel_multiplier=1), [("identb",)])
    g(lambda E: E.memset(identf[:], 0.0), [("identf",)])
    g(lambda E: E.affine_select(out=identf[:], in_=identf[:], compare_op=ALU.not_equal,
                                fill=1.0, base=0, pattern=[[-1, 128]],
                                channel_multiplier=1), [("identf",)])
    g(lambda E: E.memset(tri[:], 0.0), [("tri",)])
    g(lambda E: E.affine_select(out=tri[:], in_=tri[:], compare_op=ALU.is_ge,
                                fill=NEG, base=0, pattern=[[-1, 128]],
                                channel_multiplier=1), [("tri",)])
    p.op("vector", lambda E: E.memset(ones_col_b[:], 1.0), writes=[("ones_col_b",)],
         small=[ones_col_b[:]])
    p.op("vector", lambda E: E.memset(ones_row_f[:], 1.0), writes=[("ones_row_f",)],
         small=[ones_row_f[:]])
    p.op("vector", lambda E: E.memset(ones2[:], 1.0), writes=[("ones2",)])
    g(lambda E: E.affine_select(out=ones2[:], in_=ones2[:], compare_op=ALU.is_ge,
                                fill=0.0, base=0, pattern=[[1, 128]],
                                channel_multiplier=-64), [("ones2",)])
    g(lambda E: E.affine_select(out=ones2[:], in_=ones2[:], compare_op=ALU.is_ge,
                                fill=0.0, base=63, pattern=[[-1, 128]],
                                channel_multiplier=64), [("ones2",)])

    kb.gfence([(identb[:], ("identb",)), (identf[:], ("identf",)),
               (tri[:], ("tri",)), (ones2[:], ("ones2",))])
    p.barrier()

    rotCq = rotSq = rotCk = rotSk = bandm = None

    def build_rotary():
        nonlocal bandm
        bandm = kb.sbuf("bandm", [128, 640], BF16)
        rotstk = kb.scope()
        rotCq = kb.sbuf("rotCq", [128, NTC * 64], F32, stack=rotstk)
        rotSq = kb.sbuf("rotSq", [128, NTC * 64], F32, stack=rotstk)
        rotCk = kb.sbuf("rotCk", [128, NTC * 64], F32, stack=rotstk)
        rotSk = kb.sbuf("rotSk", [128, NTC * 64], F32, stack=rotstk)
        tcols = kb.sbuf("tcols", [128, NTC], F32, stack=rotstk)
        jrow = kb.sbuf("jrow", [128, 64], F32, stack=rotstk)
        g(lambda E: E.iota(tcols[:], [[128, NTC]], channel_multiplier=1,
                           allow_small_or_imprecise_dtypes=True), [("tcols",)])
        g(lambda E: E.iota(jrow[:].rearrange("p (a b) -> p a b", a=2),
                           [[0, 2], [1, 32]], channel_multiplier=0,
                           allow_small_or_imprecise_dtypes=True), [("jrow",)])
        g(lambda E: E.memset(bandm[:], 0.0), [("bandm",)])
        g(lambda E: E.affine_select(out=bandm[:], in_=bandm[:], compare_op=ALU.is_ge,
                                    fill=NEG, base=0, pattern=[[1, 640]],
                                    channel_multiplier=-1), [("bandm",)])
        g(lambda E: E.affine_select(out=bandm[:], in_=bandm[:], compare_op=ALU.is_ge,
                                    fill=NEG, base=512, pattern=[[-1, 640]],
                                    channel_multiplier=1), [("bandm",)])
        kb.gfence([(tcols[:], ("tcols",)), (jrow[:], ("jrow",)),
                   (bandm[:], ("bandm",))])
        with kb.scope() as stk:
            t0b = kb.sbuf("rt0", [128, 64], F32, stack=stk)
            t1b = kb.sbuf("rt1", [128, 64], F32, stack=stk)
            sinb = kb.sbuf("rtsin", [128, 64], F32, stack=stk)
            cosb = kb.sbuf("rtcos", [128, 64], F32, stack=stk)
            tib = kb.sbuf("rti", [128, 64], mybir.dt.int32, stack=stk)
            invf = kb.sbuf("rinvf", [128, 64], F32, stack=stk)
            lnb = kb.sbuf("rlnb", [128, 64], F32, stack=stk)
            lnb1k = kb.sbuf("rlnb1k", [128, 64], F32, stack=stk)
            scn = kb.sbuf("rscn", [128, 64], F32, stack=stk)
            sc8 = kb.sbuf("rsc8", [128, 64], F32, stack=stk)
            rscn = kb.sbuf("rrscn", [128, 64], F32, stack=stk)
            p.op("scalar", lambda E: E.activation(invf[:], jrow[:], AF.Exp,
                                                  scale=float(-np.log(10000.0) / 32.0)),
                 reads=[("jrow",)], writes=[("invf",)])
            p.op("vector", lambda E: E.tensor_scalar(lnb[:], jrow[:], float(2.0 / 89.6),
                                                     float(25.6 / 89.6), ALU.mult,
                                                     ALU.add),
                 reads=[("jrow",)], writes=[("lnb",)])
            p.op("scalar", lambda E: E.activation(lnb[:], lnb[:], AF.Ln),
                 reads=[("lnb",)], writes=[("lnb",)])
            p.op("vector", lambda E: E.tensor_scalar_mul(lnb1k[:], lnb[:], 1024.0),
                 reads=[("lnb",)], writes=[("lnb1k",)])
            for tc in range(NTC):
                c0 = tc * 64
                tcol = tcols[:, tc:tc + 1]
                p.op("vector", lambda E, _t=tcol: E.tensor_scalar(
                        t0b[:], invf[:], _t, None, ALU.mult),
                     reads=[("invf",), ("tcols",)], writes=[("t0b",)])
                p.op("vector", lambda E: E.tensor_scalar_mul(t1b[:], t0b[:],
                                                             float(1.0 / TWO_PI)),
                     reads=[("t0b",)], writes=[("t1b",)])
                p.op("vector", lambda E: E.tensor_copy(tib[:], t1b[:]),
                     reads=[("t1b",)], writes=[("tib",)])
                p.op("vector", lambda E: E.tensor_copy(t0b[:], tib[:]),
                     reads=[("tib",)], writes=[("t0b",)])
                p.op("vector", lambda E: E.tensor_sub(t1b[:], t1b[:], t0b[:]),
                     reads=[("t0b",), ("t1b",)], writes=[("t1b",)])
                p.op("scalar", lambda E: E.activation(sinb[:], t1b[:], AF.Sin,
                                                      scale=TWO_PI),
                     reads=[("t1b",)], writes=[("sinb",)])
                p.op("vector", lambda E: E.tensor_scalar_add(t1b[:], t1b[:], 0.25),
                     reads=[("t1b",)], writes=[("t1b",)])
                p.op("vector", lambda E: E.tensor_scalar(t0b[:], t1b[:], 0.5, 1.0,
                                                         ALU.is_gt, ALU.mult),
                     reads=[("t1b",)], writes=[("t0b",)])
                p.op("vector", lambda E: E.scalar_tensor_tensor(
                        t1b[:], t0b[:], -1.0, t1b[:], ALU.mult, ALU.add),
                     reads=[("t0b",), ("t1b",)], writes=[("t1b",)])
                p.op("scalar", lambda E: E.activation(cosb[:], t1b[:], AF.Sin,
                                                      scale=TWO_PI),
                     reads=[("t1b",)], writes=[("cosb",)])
                p.op("vector", lambda E, _t=tcol: E.scalar_tensor_tensor(
                        t0b[:], lnb[:], _t, lnb1k[:], ALU.mult, ALU.subtract),
                     reads=[("lnb",), ("lnb1k",), ("tcols",)], writes=[("t0b",)])
                p.op("scalar", lambda E: E.activation(scn[:], t0b[:], AF.Exp,
                                                      scale=float(1.0 / 256.0)),
                     reads=[("t0b",)], writes=[("scn",)])
                p.op("scalar", lambda E: E.activation(sc8[:], t0b[:], AF.Exp,
                                                      scale=float(1.0 / 256.0),
                                                      bias=float(np.log(8.0))),
                     reads=[("t0b",)], writes=[("sc8",)])
                p.op("vector", lambda E: E.reciprocal(rscn[:], scn[:]),
                     reads=[("scn",)], writes=[("rscn",)])
                p.op("vector", lambda E, _c=c0: E.tensor_mul(
                        rotCq[:, _c:_c + 64], cosb[:], sc8[:]),
                     reads=[("cosb",), ("sc8",)], writes=[("rot",)])
                p.op("vector", lambda E, _c=c0: E.tensor_mul(
                        rotSq[:, _c:_c + 64], sinb[:], sc8[:]),
                     reads=[("sinb",), ("sc8",)], writes=[("rot",)])
                p.op("vector", lambda E, _c=c0: E.tensor_mul(
                        rotCk[:, _c:_c + 64], cosb[:], rscn[:]),
                     reads=[("cosb",), ("rscn",)], writes=[("rot",)])
                p.op("vector", lambda E, _c=c0: E.tensor_mul(
                        rotSk[:, _c:_c + 64], sinb[:], rscn[:]),
                     reads=[("sinb",), ("rscn",)], writes=[("rot",)])
        for i, t in enumerate((rotCq, rotSq, rotCk, rotSk)):
            p.dma(rotdram[i, :, :], t[:], reads=[("rot",)])
        p.barrier()
        rotstk.close()

    pscore = kb.psum("pscore", [128, 2048], F32)
    pfa = kb.psum("pfa", [128, 512], F32)
    pfb = kb.psum("pfb", [128, 512], F32)
    pfc = kb.psum("pfc", [128, 512], F32)
    ptr = kb.psum("ptr", [128, 1024], BF16)

    def load_col_table(stk, name, vec_ap, nch, scale=1.0):
        stage = kb.sbuf(name + "_st", [nch, 128], F32, stack=stk)
        tbl = kb.sbuf(name, [128, nch], F32, stack=stk)
        p.dma(stage[:], vec_ap.rearrange("(c p) -> c p", p=128), writes=[(name, "st")])
        p.op("tensor",
             lambda E: E.transpose(pfa[:, 0:nch], stage[:], identf[0:nch, 0:nch]),
             reads=[(name, "st"), ("identf",)], writes=[("pfa",)])
        if scale == 1.0:
            p.op("vector", lambda E: E.tensor_copy(tbl[:], pfa[:, 0:nch]),
                 reads=[("pfa",)], writes=[(name,)], small=[tbl[:]])
        else:
            p.op("vector",
                 lambda E, _s=scale: E.tensor_scalar_mul(tbl[:], pfa[:, 0:nch], _s),
                 reads=[("pfa",)], writes=[(name,)], small=[tbl[:]])
        return tbl

    ROWSLOT = {"sum": 0, "mu": 0, "sq": 1, "var": 1, "A": 2, "B": 0}

    def row(kind):
        idx = ROWSLOT[kind]
        return rowsbuf[0:1, idx * 512:(idx + 1) * 512]

    def rres(kind):
        return ("row", ROWSLOT[kind])

    def stage_ln(stk, s, xnb, first, ln_out=None, out_off=0, tmpf=None):
        if first:
            xtm = [kb.sbuf(f"ln_xtm{i}", [128, D], F32, stack=stk) for i in range(4)]
            xfmb = kb.sbuf("ln_xfmb", [128, 512], F32, stack=stk)
        sqb = [kb.sbuf(f"ln_sqb{i}", [128, 512], BF16, stack=stk) for i in range(4)]

        for tb in range(NTB):
            t0 = tb * 512
            if first:
                for t4 in range(4):
                    p.dma(xtm[t4][:],
                          x_in[s, t0 + t4 * 128: t0 + (t4 + 1) * 128, :],
                          writes=[("xtm", t4)])
                for dc in range(NCH):
                    pj = dc % 4
                    for t4 in range(4):
                        p.op("tensor",
                             lambda E, _dc=dc, _t4=t4, _pj=pj:
                             E.transpose(
                                 pscore[:, _pj * 512 + _t4 * 128: _pj * 512 + (_t4 + 1) * 128],
                                 xtm[_t4][:, _dc * 128:(_dc + 1) * 128],
                                 identf[:]),
                             reads=[("xtm", t4), ("identf",)],
                             writes=[("pscore", pj)])
                    p.op("vector",
                         lambda E, _pj=pj: E.tensor_copy(
                             xfmb[:], pscore[:, _pj * 512:(_pj + 1) * 512]),
                         reads=[("pscore", pj)], writes=[("xfmb",)])
                    p.dma(xdram[s, dc, :, t0:t0 + 512], xfmb[:], reads=[("xfmb",)])
                    p.op("scalar",
                         lambda E, _dc=dc, _t0=t0: E.copy(
                             xnb[_dc][:, _t0:_t0 + 512], xfmb[:]),
                         reads=[("xfmb",)], writes=[("xnb", dc, tb)])
            else:
                for dc in range(NCH):
                    p.dma(xnb[dc][:, t0:t0 + 512], xdram[s, dc, :, t0:t0 + 512],
                          writes=[("xnb", dc, tb)], eng="gpsimd")

            for dc in range(NCH):
                p.op("tensor",
                     lambda E, _dc=dc, _t0=t0: E.matmul(
                         pfa[0:1, :], ones_col_b[:], xnb[_dc][:, _t0:_t0 + 512],
                         start=(_dc == 0), stop=(_dc == NCH - 1)),
                     reads=[("xnb", dc, tb), ("ones_col_b",)], writes=[("pfa",)])
            for dc in range(NCH):
                p.op("scalar",
                     lambda E, _dc=dc, _t0=t0: E.activation(
                         sqb[_dc % 4][:], xnb[_dc][:, _t0:_t0 + 512], AF.Square),
                     reads=[("xnb", dc, tb)], writes=[("sqb", dc % 4)])
                p.op("tensor",
                     lambda E, _dc=dc: E.matmul(
                         pfb[0:1, :], ones_col_b[:], sqb[_dc % 4][:],
                         start=(_dc == 0), stop=(_dc == NCH - 1)),
                     reads=[("sqb", dc % 4), ("ones_col_b",)], writes=[("pfb",)])
            p.op("vector", lambda E: E.tensor_scalar_mul(row("mu"), pfa[0:1, :], 1.0 / D),
                 reads=[("pfa",)], writes=[rres("mu")])
            p.op("vector", lambda E: E.tensor_scalar_mul(row("var"), pfb[0:1, :], 1.0 / D),
                 reads=[("pfb",)], writes=[rres("var")])
            p.op("vector", lambda E: E.tensor_mul(row("A"), row("mu"), row("mu")),
                 reads=[rres("mu")], writes=[rres("A")])
            p.op("vector", lambda E: E.tensor_sub(row("var"), row("var"), row("A")),
                 reads=[rres("var"), rres("A")], writes=[rres("var")])
            p.op("vector", lambda E: E.tensor_scalar_add(row("var"), row("var"), EPS),
                 reads=[rres("var")], writes=[rres("var")])
            p.op("scalar", lambda E: E.activation(row("var"), row("var"), AF.Sqrt),
                 reads=[rres("var")], writes=[rres("var")], small=[row("var")])
            p.op("vector", lambda E: E.reciprocal(row("A"), row("var")),
                 reads=[rres("var")], writes=[rres("A")])
            p.op("vector", lambda E: E.tensor_mul(row("B"), row("mu"), row("A")),
                 reads=[rres("mu"), rres("A")], writes=[rres("B")])
            p.op("vector", lambda E: E.tensor_scalar_mul(row("B"), row("B"), -1.0),
                 reads=[rres("B")], writes=[rres("B")],
                 small=[row("B"), row("A")])
            p.op("tensor", lambda E: E.matmul(pfc[:], ones_row_f[:], row("A"),
                                              start=True, stop=True),
                 reads=[rres("A"), ("ones_row_f",)], writes=[("pfc",)])
            p.op("tensor", lambda E: E.matmul(pfa[:], ones_row_f[:], row("B"),
                                              start=True, stop=True),
                 reads=[rres("B"), ("ones_row_f",)], writes=[("pfa",)])
            for dc in range(NCH):
                if ln_out is None:
                    p.op("vector",
                         lambda E, _dc=dc, _t0=t0: E.tensor_tensor(
                             xnb[_dc][:, _t0:_t0 + 512], xnb[_dc][:, _t0:_t0 + 512],
                             pfc[:], ALU.mult),
                         reads=[("xnb", dc, tb), ("pfc",)], writes=[("xnb", dc, tb)])
                    p.op("vector",
                         lambda E, _dc=dc, _t0=t0: E.tensor_tensor(
                             xnb[_dc][:, _t0:_t0 + 512], xnb[_dc][:, _t0:_t0 + 512],
                             pfa[:], ALU.add),
                         reads=[("xnb", dc, tb), ("pfa",)], writes=[("xnb", dc, tb)])
                else:
                    p.op("vector",
                         lambda E, _dc=dc, _t0=t0, _f=tmpf: E.tensor_tensor(
                             _f[:], xnb[_dc][:, _t0:_t0 + 512], pfc[:], ALU.mult),
                         reads=[("xnb", dc, tb), ("pfc",)], writes=[("lntmp",)])
                    p.op("vector",
                         lambda E, _dc=dc, _t0=t0, _f=tmpf, _o=out_off: E.tensor_tensor(
                             ln_out[_dc][:, _o + _t0:_o + _t0 + 512], _f[:],
                             pfa[:], ALU.add),
                         reads=[("lntmp",), ("pfa",)], writes=[("xi", dc, tb)])

    def phase_attn_a(s, first):
        with kb.scope() as stk:
            qf = [kb.sbuf(f"a_qf{c}", [128, S], BF16, stack=stk) for c in range(NCH)]
            kf = [kb.sbuf(f"a_kf{c}", [128, S], BF16, stack=stk) for c in range(NCH)]
            vt = [kb.sbuf(f"a_vt{c}", [128, 1024], BF16, stack=stk) for c in range(NTC)]
            ctx = [kb.sbuf(f"a_ctx{c}", [128, S], BF16, stack=stk) for c in range(NCH)]

            with kb.scope() as stk2:
                xnb = [kb.sbuf(f"a_xnb{c}", [128, S], BF16, stack=stk2)
                       for c in range(NCH)]
                with kb.scope() as stk_ln:
                    stage_ln(stk_ln, s, xnb, first)
                p.barrier()
                if KSTAGE < 2:
                    p.barrier()
                    return
                bq_t = load_col_table(stk2, "bq_t", dr["ab_qkv"][0:D], NCH, scale=0.125)
                bk_t = load_col_table(stk2, "bk_t", dr["ab_qkv"][D:2 * D], NCH)
                bv_bc = kb.sbuf("bv_bc", [128, 1024], BF16, stack=stk2)
                p.dma(bv_bc[:],
                      dr["ab_qkv"][2 * D:3 * D].unsqueeze(0).broadcast_to((128, D)),
                      writes=[("bv_bc",)], eng="gpsimd")

                wstage = [kb.sbuf(f"a_wstage{i}", [128, 512], F32, stack=stk2)
                          for i in range(2)]
                wblk = kb.sbuf("a_wblk", [128, 8 * 512], BF16, stack=stk2)
                for wb in range(6):
                    c0 = wb * 512
                    for dc in range(NCH):
                        ws = wstage[dc % 2]
                        p.dma(ws[:], dr["aw_qkv"][dc * 128:(dc + 1) * 128, c0:c0 + 512],
                              writes=[("wstage", dc % 2)])
                        if wb < 2:
                            p.op("scalar",
                                 lambda E, _dc=dc, _w=ws: E.activation(
                                     wblk[:, _dc * 512:(_dc + 1) * 512], _w[:],
                                     AF.Identity, scale=0.125),
                                 reads=[("wstage", dc % 2)], writes=[("wblk", dc)])
                            if os.environ.get("KDBG") and s == 0 and wb == 0 and dc == 0:
                                kb.dump("wstage0", ws[:], [("wstage", 0)])
                                kb.dump("wblk0", wblk[:, 0:512], [("wblk", 0)])
                        else:
                            p.op("scalar",
                                 lambda E, _dc=dc, _w=ws: E.copy(
                                     wblk[:, _dc * 512:(_dc + 1) * 512], _w[:]),
                                 reads=[("wstage", dc % 2)], writes=[("wblk", dc)])
                    if os.environ.get("KDBG") and s == 0 and wb in (0, 2):
                        kb.dump(f"wblk_w{wb}", wblk[:],
                                [("wblk", d) for d in range(NCH)])
                        if wb == 0:
                            for d in range(NCH):
                                kb.dump(f"xnb_{d}", xnb[d][:, 0:512], [("xnb", d, 0)])
                    if wb < 4:
                        dst, dstn, btab, btn = ((qf, "qf", bq_t, "bq_t") if wb < 2
                                                else (kf, "kf", bk_t, "bk_t"))
                        for j in range(4):
                            qc = (wb % 2) * 4 + j
                            for tb in range(NTB):
                                pj = (j * NTB + tb) % 4
                                for dc in range(NCH):
                                    p.op("tensor",
                                         lambda E, _j=j, _tb=tb, _dc=dc, _pj=pj:
                                         E.matmul(
                                             pscore[:, _pj * 512:(_pj + 1) * 512],
                                             wblk[:, _dc * 512 + _j * 128: _dc * 512 + (_j + 1) * 128],
                                             xnb[_dc][:, _tb * 512:(_tb + 1) * 512],
                                             start=(_dc == 0), stop=(_dc == NCH - 1)),
                                         reads=[("wblk", dc), ("xnb", dc, tb)],
                                         writes=[("pscore", pj)])
                                p.op("scalar",
                                     lambda E, _qc=qc, _tb=tb, _pj=pj, _d=dst, _bt=btab:
                                     E.activation(
                                         _d[_qc][:, _tb * 512:(_tb + 1) * 512],
                                         pscore[:, _pj * 512:(_pj + 1) * 512],
                                         AF.Identity, bias=_bt[:, _qc:_qc + 1]),
                                     reads=[("pscore", pj), (btn,)],
                                     writes=[(dstn, qc)])
                    else:
                        v0 = (wb - 4) * 512
                        for tc in range(NTC):
                            pj = tc % 4
                            for dc in range(NCH):
                                p.op("tensor",
                                     lambda E, _tc=tc, _dc=dc, _pj=pj:
                                     E.matmul(
                                         pscore[:, _pj * 512:(_pj + 1) * 512],
                                         xnb[_dc][:, _tc * 128:(_tc + 1) * 128],
                                         wblk[:, _dc * 512:(_dc + 1) * 512],
                                         start=(_dc == 0), stop=(_dc == NCH - 1)),
                                     reads=[("wblk", dc), ("xnb", dc, tc // 4)],
                                     writes=[("pscore", pj)])
                            p.op("vector",
                                 lambda E, _tc=tc, _pj=pj, _v0=v0: E.tensor_tensor(
                                     vt[_tc][:, _v0:_v0 + 512],
                                     pscore[:, _pj * 512:(_pj + 1) * 512],
                                     bv_bc[:, _v0:_v0 + 512], ALU.add),
                                 reads=[("pscore", pj), ("bv_bc",)],
                                 writes=[("vt", tc, wb - 4)])
                p.barrier()

            if KSTAGE < 3:
                p.barrier()
                return
            with kb.scope() as stk3:
                asb = [kb.sbuf(f"a_asb{i}", [128, S], BF16, stack=stk3) for i in range(4)]
                aT = [[kb.sbuf(f"a_aT{h}_{k}", [128, 512], BF16, stack=stk3)
                       for k in range(NTC)] for h in range(2)]
                rs_pair = [kb.sbuf(f"a_rs{i}", [128, 2], F32, stack=stk3)
                           for i in range(4)]
                scd = kb.sbuf("a_scd", [128, 128], F32, stack=stk3)
                sums = kb.sbuf("a_sums", [128, 8], F32, stack=stk3)

                for qg in range(4):
                    nkg = 4 * qg + 4
                    for hp in range(8):
                        for h2 in range(2):
                            head = hp * 2 + h2
                            hc, hb = head // 2, (head % 2) * 64
                            for qr in range(4):
                                qabs = qg * 4 + qr
                                kcols = (qabs + 1) * 128
                                nblk = (kcols + 511) // 512
                                for j in range(nblk):
                                    w = min(512, kcols - j * 512)
                                    p.op("tensor",
                                         lambda E, _j=j, _w=w, _hc=hc, _hb=hb, _q=qabs:
                                         E.matmul(
                                             pscore[:, _j * 512:_j * 512 + _w],
                                             qf[_hc][_hb:_hb + 64, _q * 128:(_q + 1) * 128],
                                             kf[_hc][_hb:_hb + 64, _j * 512:_j * 512 + _w],
                                             start=True, stop=True),
                                         reads=[("qf", hc), ("kf", hc)],
                                         writes=[("pscore", j)])
                                doff = kcols - 128
                                p.op("vector",
                                     lambda E, _do=doff: E.tensor_tensor(
                                         pscore[:, _do:_do + 128],
                                         pscore[:, _do:_do + 128],
                                         tri[:], ALU.add),
                                     reads=[("pscore", doff // 512), ("tri",)],
                                     writes=[("pscore", doff // 512)])
                                p.op("vector",
                                     lambda E, _n=kcols: E.reduce_max(
                                         sums[:, 2:3], pscore[:, 0:_n], axis=AX.X,
                                         negate=True),
                                     reads=[("pscore", j) for j in range(nblk)],
                                     writes=[("sums", 2)], small=[sums[:, 2:3]])
                                p.op("scalar",
                                     lambda E, _n=kcols, _qr=qr: E.activation(
                                         asb[_qr][:, 0:_n], pscore[:, 0:_n], AF.Exp,
                                         bias=sums[:, 2:3], accum_out=sums[:, 4:5]),
                                     reads=[("pscore", j) for j in range(nblk)]
                                     + [("sums", 2)],
                                     writes=[("asb", qr), ("sums", 4)],
                                     small=[sums[:, 4:5]])
                                p.op("vector",
                                     lambda E, _qr=qr, _h2=h2: E.reciprocal(
                                         rs_pair[_qr][:, _h2:_h2 + 1], sums[:, 4:5]),
                                     reads=[("sums", 4)], writes=[("rsp", qr, h2)],
                                     small=[rs_pair[qr][:, h2:h2 + 1]])
                                if (os.environ.get("KDBG") and s == 0 and qg == 0
                                        and hp == 0 and h2 == 0 and qr == 0):
                                    dbgs = kb.sbuf("dbgs", [128, 128], F32, stack=stk3)
                                    p.op("vector",
                                         lambda E: E.tensor_copy(dbgs[:], pscore[:, 0:128]),
                                         reads=[("pscore", 0)], writes=[("dbgsc",)])
                                    kb.dump("psc00", dbgs[:], [("dbgsc",)])
                                    kb.dump("sums00", sums[:],
                                            [("sums", 2), ("sums", 4)])
                            for kc in range(nkg):
                                half = kc % 2
                                fq = max(0, kc - 4 * qg)
                                for qr in range(fq, 4):
                                    p.op("tensor",
                                         lambda E, _kc=kc, _qr=qr, _h=half:
                                         E.transpose(
                                             ptr[:, _h * 512 + _qr * 128: _h * 512 + (_qr + 1) * 128],
                                             asb[_qr][:, _kc * 128:(_kc + 1) * 128],
                                             identb[:]),
                                         reads=[("asb", qr), ("identb",)],
                                         writes=[("ptr",)])
                                if fq > 0:
                                    p.op("vector",
                                         lambda E, _h2=h2, _kc=kc, _f=fq:
                                         E.memset(aT[_h2][_kc][:, 0:_f * 128], 0.0),
                                         writes=[("aT", h2, kc)])
                                p.op("vector",
                                     lambda E, _h2=h2, _kc=kc, _h=half, _f=fq:
                                     E.tensor_copy(
                                         aT[_h2][_kc][:, _f * 128:512],
                                         ptr[:, _h * 512 + _f * 128: _h * 512 + 512]),
                                     reads=[("ptr",)], writes=[("aT", h2, kc)])
                        for qr in range(4):
                            p.op("tensor",
                                 lambda E, _qr=qr: E.transpose(
                                     pfa[0:2, _qr * 128:(_qr + 1) * 128],
                                     rs_pair[_qr][:], identf[:]),
                                 reads=[("rsp", qr, 0), ("rsp", qr, 1), ("identf",)],
                                 writes=[("pfa",)])
                        p.op("vector", lambda E: E.tensor_copy(rs2_sb[:], pfa[0:2, :]),
                             reads=[("pfa",)], writes=[("rs2_sb",)], small=[rs2_sb[:]])
                        p.op("tensor", lambda E: E.matmul(pfb[:], ones2[:], rs2_sb[:],
                                                          start=True, stop=True),
                             reads=[("rs2_sb",), ("ones2",)], writes=[("pfb",)])
                        p.op("vector", lambda E: E.tensor_copy(rsbc_sb[:], pfb[:]),
                             reads=[("pfb",)], writes=[("rsbc_sb",)])
                        for h2, bank, bres in ((0, pfc, ("pfc",)), (1, pfa, ("pfa",))):
                            for kc in range(nkg):
                                p.op("tensor",
                                     lambda E, _kc=kc, _h2=h2, _hp=hp, _b=bank, _n=nkg:
                                     E.matmul(
                                         _b[0:64, :],
                                         vt[_kc][:, _hp * 128 + _h2 * 64: _hp * 128 + (_h2 + 1) * 64],
                                         aT[_h2][_kc][:],
                                         start=(_kc == 0), stop=(_kc == _n - 1)),
                                     reads=[("vt", kc, 0), ("vt", kc, 1),
                                            ("aT", h2, kc)],
                                     writes=[bres])
                        p.op("vector",
                             lambda E, _hp=hp, _qg=qg: E.tensor_tensor(
                                 ctx[_hp][0:64, _qg * 512:(_qg + 1) * 512],
                                 pfc[0:64, :], rsbc_sb[0:64, :], ALU.mult),
                             reads=[("pfc",), ("rsbc_sb",)], writes=[("ctx", hp, qg)])
                        p.op("vector",
                             lambda E, _hp=hp, _qg=qg: E.tensor_tensor(
                                 ctx[_hp][64:128, _qg * 512:(_qg + 1) * 512],
                                 pfa[0:64, :], rsbc_sb[64:128, :], ALU.mult),
                             reads=[("pfa",), ("rsbc_sb",)], writes=[("ctx", hp, qg)])
                        if os.environ.get("KDBG") and s == 0 and qg == 0 and hp == 0:
                            kb.dump("asb0", asb[0][:, 0:128], [("asb", 0)])
                            kb.dump("asb3", asb[3][:, 0:512], [("asb", 3)])
                            kb.dump("rs2", rs2_sb[:], [("rs2_sb",)])
                            kb.dump("rsbc", rsbc_sb[:], [("rsbc_sb",)])
                            kb.dump("ctx0", ctx[0][:, 0:512], [("ctx", 0, 0)])
                            kb.dump("aT00", aT[0][0][:], [("aT", 0, 0)])
                    if os.environ.get("KDBG") and s == 0 and qg == 0:
                        kb.dump("qf0", qf[0][:, 0:512], [("qf", 0)])
                        kb.dump("kf0", kf[0][:, 0:512], [("kf", 0)])
                        kb.dump("vt0", vt[0][:], [("vt", 0, 0), ("vt", 0, 1)])
                p.barrier()

            if KSTAGE < 4:
                p.barrier()
                return
            with kb.scope() as stk4:
                wstage_c = [kb.sbuf(f"c_wstage{i}", [128, 512], F32, stack=stk4)
                            for i in range(2)]
                wblk_c = kb.sbuf("c_wblk", [128, 8 * 512], BF16, stack=stk4)
                bo_t = load_col_table(stk4, "bo_t", dr["ab_out"][:], NCH)
                xblk = [kb.sbuf(f"c_xblk{i}", [128, 512], F32, stack=stk4)
                        for i in range(4)]
                tmp = [kb.sbuf(f"c_tmp{i}", [128, 512], F32, stack=stk4)
                       for i in range(2)]
                for ob in range(2):
                    c0 = ob * 512
                    for dc in range(NCH):
                        ws = wstage_c[dc % 2]
                        p.dma(ws[:], dr["aw_out"][dc * 128:(dc + 1) * 128, c0:c0 + 512],
                              writes=[("cwstage", dc % 2)])
                        p.op("scalar",
                             lambda E, _dc=dc, _w=ws: E.copy(
                                 wblk_c[:, _dc * 512:(_dc + 1) * 512], _w[:]),
                             reads=[("cwstage", dc % 2)], writes=[("cwblk", dc)])
                    for tb in range(NTB):
                        for j in range(4):
                            oc = ob * 4 + j
                            pj = (tb * 4 + j) % 4
                            for dc in range(NCH):
                                p.op("tensor",
                                     lambda E, _j=j, _tb=tb, _dc=dc, _pj=pj:
                                     E.matmul(
                                         pscore[:, _pj * 512:(_pj + 1) * 512],
                                         wblk_c[:, _dc * 512 + _j * 128: _dc * 512 + (_j + 1) * 128],
                                         ctx[_dc][:, _tb * 512:(_tb + 1) * 512],
                                         start=(_dc == 0), stop=(_dc == NCH - 1)),
                                     reads=[("cwblk", dc), ("ctx", dc, tb)],
                                     writes=[("pscore", pj)])
                            p.dma(xblk[j][:], xdram[s, oc, :, tb * 512:(tb + 1) * 512],
                                  writes=[("xblk", j)])
                            tp = tmp[j % 2]
                            p.op("scalar",
                                 lambda E, _pj=pj, _oc=oc, _tp=tp: E.activation(
                                     _tp[:], pscore[:, _pj * 512:(_pj + 1) * 512],
                                     AF.Identity, bias=bo_t[:, _oc:_oc + 1]),
                                 reads=[("pscore", pj), ("bo_t",)],
                                 writes=[("ctmp", j % 2)])
                            p.op("vector",
                                 lambda E, _j=j, _tp=tp: E.tensor_add(
                                     xblk[_j][:], xblk[_j][:], _tp[:]),
                                 reads=[("ctmp", j % 2), ("xblk", j)],
                                 writes=[("xblk", j)])
                            p.dma(xdram[s, oc, :, tb * 512:(tb + 1) * 512], xblk[j][:],
                                  reads=[("xblk", j)])
        p.barrier()

    def ffn_qkv_rotary(stka, s, l, xi, vt, qf, kf, rott):
        rotCq, rotSq, rotCk, rotSk = rott
        wq = kb.sbuf("f_wq", [128, NCH * 1536], BF16, stack=stka)
        wqs = kb.sbuf("f_wqs", [128, 768], F32, stack=stka)
        for dc in range(NCH):
            for hh in range(2):
                p.dma(wqs[:], dr["f_wqkv"][l, dc * 128:(dc + 1) * 128,
                                           hh * 768:(hh + 1) * 768],
                      writes=[("fwqs",)])
                p.op("scalar",
                     lambda E, _dc=dc, _h=hh, _w=wqs: E.copy(
                         wq[:, _dc * 1536 + _h * 768: _dc * 1536 + (_h + 1) * 768],
                         _w[:]),
                     reads=[("fwqs",)], writes=[("fwq", dc)])
        qsc = kb.sbuf("f_qsc", [128, 64], F32, stack=stka)
        ksc = kb.sbuf("f_ksc", [128, 64], F32, stack=stka)
        p.dma(qsc[:], dr["f_qs"][l].unsqueeze(0).broadcast_to((128, 64)),
              writes=[("fqsc",)])
        p.dma(ksc[:], dr["f_ks"][l].unsqueeze(0).broadcast_to((128, 64)),
              writes=[("fksc",)])
        xbt = [kb.sbuf(f"f_xbt{c}", [128, 513], BF16, stack=stka)
               for c in range(NCH)]
        qt_ = kb.sbuf("f_qt", [128, 512], F32, stack=stka)
        kt_ = kb.sbuf("f_kt", [128, 512], F32, stack=stka)
        rsm = kb.sbuf("f_rsm", [128, 8], F32, stack=stka)
        rsm2 = kb.sbuf("f_rsm2", [128, 8], F32, stack=stka)
        rsn = kb.sbuf("f_rsn", [128, 8], F32, stack=stka)
        scl = kb.sbuf("f_scl", [128, 512], F32, stack=stka)
        rot2 = kb.sbuf("f_rot2", [128, 512], F32, stack=stka)
        t1_ = kb.sbuf("f_t1", [128, 512], F32, stack=stka)
        rqb = kb.sbuf("f_rqb", [128, 512], BF16, stack=stka)

        def v3(t):
            return t[:].rearrange("p (h d) -> p h d", h=8)

        for tb in range(NTB):
            for dc in range(NCH):
                p.op("scalar",
                     lambda E, _dc=dc, _t0=tb * 512: E.copy(
                         xbt[_dc][:], xi[_dc][:, _t0:_t0 + 513]),
                     reads=[("xi", dc, tb), ("xi", dc, tb - 1), ("xi", dc, "pad")],
                     writes=[("xbt", dc)])
            for tc4 in range(4):
                tc = tb * 4 + tc4
                for role, off, psj in (("q", 0, tc4 % 2), ("k", 512, 2 + tc4 % 2),
                                       ("v", 1024, tc4 % 2)):
                    for dc in range(NCH):
                        coff = (1 if dc < 4 else 0) + tc4 * 128
                        p.op("tensor",
                             lambda E, _dc=dc, _co=coff, _o=off, _pj=psj:
                             E.matmul(pscore[:, _pj * 512:(_pj + 1) * 512],
                                      xbt[_dc][:, _co:_co + 128],
                                      wq[:, _dc * 1536 + _o: _dc * 1536 + _o + 512],
                                      start=(_dc == 0), stop=(_dc == NCH - 1)),
                             reads=[("xbt", dc), ("fwq", dc)],
                             writes=[("pscore", psj)])
                    if role == "v":
                        p.op("vector",
                             lambda E, _tc=tc, _pj=psj: E.tensor_copy(
                                 vt[_tc][:], pscore[:, _pj * 512:(_pj + 1) * 512]),
                             reads=[("pscore", psj)], writes=[("vt", tc)])
                    else:
                        dst = qt_ if role == "q" else kt_
                        p.op("vector",
                             lambda E, _d=dst, _pj=psj: E.tensor_copy(
                                 _d[:], pscore[:, _pj * 512:(_pj + 1) * 512]),
                             reads=[("pscore", psj)], writes=[(role + "t",)])
                for role, buf, sct, Ct, St, dstf in (
                        ("q", qt_, qsc, rotCq, rotSq, qf),
                        ("k", kt_, ksc, rotCk, rotSk, kf)):
                    p.op("scalar",
                         lambda E, _b=buf: E.activation(t1_[:], _b[:], AF.Square),
                         reads=[(role + "t",)], writes=[("t1_",)])
                    p.op("vector",
                         lambda E: E.tensor_reduce(
                             rsm[:].rearrange("p (h o) -> p h o", o=1),
                             v3(t1_), axis=AX.X, op=ALU.add),
                         reads=[("t1_",)], writes=[("rsm",)], small=[rsm[:]])
                    p.op("scalar", lambda E: E.activation(rsm2[:], rsm[:], AF.Sqrt,
                                                          bias=1e-24),
                         reads=[("rsm",)], writes=[("rsm2",)], small=[rsm2[:]])
                    p.op("vector", lambda E: E.reciprocal(rsn[:], rsm2[:]),
                         reads=[("rsm2",)], writes=[("rsn",)], small=[rsn[:]])
                    p.op("vector",
                         lambda E, _s=sct: E.tensor_tensor(
                             v3(scl),
                             rsn[:].unsqueeze(2).broadcast_to((128, 8, 64)),
                             _s[:].unsqueeze(1).broadcast_to((128, 8, 64)),
                             ALU.mult),
                         reads=[("rsn",), ("f" + role + "sc",)], writes=[("scl",)])
                    p.op("vector",
                         lambda E, _b=buf: E.tensor_tensor(_b[:], _b[:], scl[:],
                                                           ALU.mult),
                         reads=[(role + "t",), ("scl",)], writes=[(role + "t",)])
                    p.op("vector",
                         lambda E, _b=buf: E.tensor_scalar_mul(
                             rot2[:].rearrange("p (h u d) -> p h u d", h=8, u=2)[:, :, 0, :],
                             _b[:].rearrange("p (h u d) -> p h u d", h=8, u=2)[:, :, 1, :],
                             -1.0),
                         reads=[(role + "t",)], writes=[("rot2",)])
                    p.op("vector",
                         lambda E, _b=buf: E.tensor_copy(
                             rot2[:].rearrange("p (h u d) -> p h u d", h=8, u=2)[:, :, 1, :],
                             _b[:].rearrange("p (h u d) -> p h u d", h=8, u=2)[:, :, 0, :]),
                         reads=[(role + "t",)], writes=[("rot2",)])
                    p.op("vector",
                         lambda E, _b=buf, _C=Ct, _c=tc * 64: E.tensor_tensor(
                             v3(t1_), v3(_b),
                             _C[:, _c:_c + 64].unsqueeze(1).broadcast_to((128, 8, 64)),
                             ALU.mult),
                         reads=[(role + "t",), ("rot",)], writes=[("t1_",)])
                    p.op("vector",
                         lambda E, _S=St, _c=tc * 64: E.tensor_tensor(
                             v3(rot2), v3(rot2),
                             _S[:, _c:_c + 64].unsqueeze(1).broadcast_to((128, 8, 64)),
                             ALU.mult),
                         reads=[("rot2",), ("rot",)], writes=[("rot2",)])
                    p.op("vector",
                         lambda E: E.tensor_add(rqb[:], t1_[:], rot2[:]),
                         reads=[("t1_",), ("rot2",)], writes=[("rqb",)])
                    for fc in range(4):
                        p.op("tensor",
                             lambda E, _fc=fc: E.transpose(
                                 ptr[:, _fc * 128:(_fc + 1) * 128],
                                 rqb[:, _fc * 128:(_fc + 1) * 128], identb[:]),
                             reads=[("rqb",), ("identb",)], writes=[("ptr",)])
                    for fc in range(4):
                        p.op("vector",
                             lambda E, _fc=fc, _d=dstf, _tc=tc: E.tensor_copy(
                                 _d[_fc][:, _tc * 128:(_tc + 1) * 128],
                                 ptr[:, _fc * 128:(_fc + 1) * 128]),
                             reads=[("ptr",)], writes=[(role + "f", fc)])

    def ffn_attn(stka, s, l, vt, qf, kf, of):
        if os.environ.get("KDBG") and s == 0 and l == 0:
            kb.dump("fqf0", qf[0][:], [("qf", 0)])
            kb.dump("fkf0", kf[0][:], [("kf", 0)])
            kb.dump("fvt0", vt[0][:], [("vt", 0)])
        asbf = [kb.sbuf(f"f_asb{i}", [128, 640], BF16, stack=stka) for i in range(4)]
        aTf = [[kb.sbuf(f"f_aT{h}_{k}", [128, 512], BF16, stack=stka)
                for k in range(8)] for h in range(2)]
        rsp = [kb.sbuf(f"f_rsp{i}", [128, 2], F32, stack=stka) for i in range(4)]
        sums = kb.sbuf("f_sums", [128, 8], F32, stack=stka)
        for w in range(4):
            for hp in range(4):
                for h2 in range(2):
                    head = 2 * hp + h2
                    hb = (head % 2) * 64
                    hc = head // 2
                    for m in range(4):
                        if w == 0:
                            kcols = 128 * (m + 1)
                            off = (m % 2) * 1024
                            nsl = (kcols + 511) // 512
                            slres = [("pscore", off // 512 + j) for j in range(nsl)]
                            for j in range(nsl):
                                wdt = min(512, kcols - j * 512)
                                p.op("tensor",
                                     lambda E, _j=j, _w=wdt, _hc=hc, _hb=hb,
                                            _m=m, _o=off:
                                     E.matmul(
                                         pscore[:, _o + _j * 512:_o + _j * 512 + _w],
                                         qf[_hc][_hb:_hb + 64,
                                                 _m * 128:(_m + 1) * 128],
                                         kf[_hc][_hb:_hb + 64, _j * 512:_j * 512 + _w],
                                         start=True, stop=True),
                                     reads=[("qf", hc), ("kf", hc)],
                                     writes=[("pscore", off // 512 + j)])
                            doff = off + kcols - 128
                            p.op("vector",
                                 lambda E, _do=doff: E.tensor_tensor(
                                     pscore[:, _do:_do + 128],
                                     pscore[:, _do:_do + 128], tri[:], ALU.add),
                                 reads=[("pscore", doff // 512), ("tri",)],
                                 writes=[("pscore", doff // 512)])
                        else:
                            kcols = 640
                            off = (m % 2) * 1024
                            kb0 = (w - 1) * 512 + m * 128
                            slres = [("pscore", off // 512), ("pscore", off // 512 + 1)]
                            if (os.environ.get("KDBG") and s == 0 and l == 0
                                    and w == 1 and hp == 0 and h2 == 0 and m == 0):
                                dsc4 = kb.sbuf("f_dsc4", [128, 640], F32, stack=stka)
                                p.op("vector",
                                     lambda E: E.tensor_copy(dsc4[:], pscore[:, 0:640]),
                                     reads=slres, writes=[("fdsc4",)])
                                kb.dump("fscw1prior", dsc4[:], [("fdsc4",)])
                            q0 = (m + 4 * w) * 128
                            p.op("tensor",
                                 lambda E, _hc=hc, _hb=hb, _q0=q0, _o=off, _k=kb0:
                                 E.matmul(pscore[:, _o:_o + 512],
                                          qf[_hc][_hb:_hb + 64, _q0:_q0 + 128],
                                          kf[_hc][_hb:_hb + 64, _k:_k + 512],
                                          start=True, stop=True),
                                 reads=[("qf", hc), ("kf", hc)], writes=[slres[0]])
                            p.op("tensor",
                                 lambda E, _hc=hc, _hb=hb, _q0=q0, _o=off, _k=kb0:
                                 E.matmul(pscore[:, _o + 512:_o + 640],
                                          qf[_hc][_hb:_hb + 64, _q0:_q0 + 128],
                                          kf[_hc][_hb:_hb + 64, _k + 512:_k + 640],
                                          start=True, stop=True),
                                 reads=[("qf", hc), ("kf", hc)], writes=[slres[1]])
                            if (os.environ.get("KDBG") and s == 0 and l == 0
                                    and w == 1 and hp == 0 and h2 == 0 and m == 0):
                                kb.dump("fqf0_at_w1", qf[0][:], [("qf", 0)])
                                kb.dump("fkf0_at_w1", kf[0][:], [("kf", 0)])
                                dsc2 = kb.sbuf("f_dsc2", [128, 640], F32, stack=stka)
                                p.op("vector",
                                     lambda E: E.tensor_copy(dsc2[:], pscore[:, 0:640]),
                                     reads=slres, writes=[("fdsc2",)])
                                kb.dump("fscw1pre", dsc2[:], [("fdsc2",)])
                            p.op("vector",
                                 lambda E, _o=off: E.tensor_tensor(
                                     pscore[:, _o:_o + 512], pscore[:, _o:_o + 512],
                                     bandm[:, 0:512], ALU.add),
                                 reads=[slres[0], ("bandm",)], writes=[slres[0]])
                            p.op("vector",
                                 lambda E, _o=off: E.tensor_tensor(
                                     pscore[:, _o + 512:_o + 640],
                                     pscore[:, _o + 512:_o + 640],
                                     bandm[:, 512:640], ALU.add),
                                 reads=[slres[1], ("bandm",)], writes=[slres[1]])
                            if (os.environ.get("KDBG") and s == 0 and l == 0
                                    and w == 1 and hp == 0 and h2 == 0 and m == 0):
                                dsc3 = kb.sbuf("f_dsc3", [128, 640], F32, stack=stka)
                                p.op("vector",
                                     lambda E: E.tensor_copy(dsc3[:], pscore[:, 0:640]),
                                     reads=slres, writes=[("fdsc3",)])
                                kb.dump("fscw1post", dsc3[:], [("fdsc3",)])
                        p.op("vector",
                             lambda E, _o=off, _n=kcols:
                             E.reduce_max(sums[:, 2:3], pscore[:, _o:_o + _n],
                                          axis=AX.X, negate=True),
                             reads=slres, writes=[("sums", 2)], small=[sums[:, 2:3]])
                        p.op("scalar",
                             lambda E, _o=off, _n=kcols, _m=m: E.activation(
                                 asbf[_m][:, 0:_n], pscore[:, _o:_o + _n], AF.Exp,
                                 bias=sums[:, 2:3], accum_out=sums[:, 4:5]),
                             reads=slres + [("sums", 2)],
                             writes=[("asbf", m), ("sums", 4)], small=[sums[:, 4:5]])
                        p.op("vector",
                             lambda E, _m=m, _h2=h2: E.reciprocal(
                                 rsp[_m][:, _h2:_h2 + 1], sums[:, 4:5]),
                             reads=[("sums", 4)], writes=[("frsp", m, h2)],
                             small=[rsp[m][:, h2:h2 + 1]])
                        if (os.environ.get("KDBG") and s == 0 and l == 0 and w == 0
                                and hp == 0 and h2 == 0 and m == 1):
                            kb.dump("fasb1", asbf[1][:, 0:256], [("asbf", 1)])
                            kb.dump("fsums1", sums[:], [("sums", 2), ("sums", 4)])
                            dsc = kb.sbuf("f_dsc", [128, 256], F32, stack=stka)
                            p.op("vector",
                                 lambda E: E.tensor_copy(dsc[:], pscore[:, 1024:1280]),
                                 reads=[("pscore", 2)], writes=[("fdsc",)])
                            kb.dump("fscm1", dsc[:], [("fdsc",)])
                        if (os.environ.get("KDBG") and s == 0 and l == 0 and w == 1
                                and hp == 0 and h2 == 0 and m == 0):
                            kb.dump("fasbw1", asbf[0][:, 0:640], [("asbf", 0)])
                    kcr_range = range(8) if w > 0 else range(4, 8)
                    for kcr in kcr_range:
                        half = kcr % 2
                        if w > 0:
                            ms = range(max(0, kcr - 4), min(3, kcr) + 1)
                        else:
                            kcg = kcr - 4
                            ms = range(kcg, 4)
                        ms = list(ms)
                        for m in ms:
                            j = (kcr - m) if w > 0 else (kcr - 4)
                            p.op("tensor",
                                 lambda E, _m=m, _j=j, _h=half: E.transpose(
                                     ptr[:, _h * 512 + _m * 128: _h * 512 + (_m + 1) * 128],
                                     asbf[_m][:, _j * 128:(_j + 1) * 128], identb[:]),
                                 reads=[("asbf", m), ("identb",)], writes=[("ptr",)])
                        f0, f1 = ms[0], ms[-1] + 1
                        if f0 > 0:
                            p.op("vector",
                                 lambda E, _h2=h2, _k=kcr, _f=f0: E.memset(
                                     aTf[_h2][_k][:, 0:_f * 128], 0.0),
                                 writes=[("aTf", h2, kcr)])
                        if f1 < 4:
                            p.op("vector",
                                 lambda E, _h2=h2, _k=kcr, _f=f1: E.memset(
                                     aTf[_h2][_k][:, _f * 128:512], 0.0),
                                 writes=[("aTf", h2, kcr)])
                        p.op("vector",
                             lambda E, _h2=h2, _k=kcr, _h=half, _a=f0, _b=f1:
                             E.tensor_copy(aTf[_h2][_k][:, _a * 128:_b * 128],
                                           ptr[:, _h * 512 + _a * 128:_h * 512 + _b * 128]),
                             reads=[("ptr",)], writes=[("aTf", h2, kcr)])
                for m in range(4):
                    p.op("tensor",
                         lambda E, _m=m: E.transpose(
                             pfa[0:2, _m * 128:(_m + 1) * 128], rsp[_m][:],
                             identf[:]),
                         reads=[("frsp", m, 0), ("frsp", m, 1), ("identf",)],
                         writes=[("pfa",)])
                p.op("vector", lambda E: E.tensor_copy(rs2_sb[:], pfa[0:2, :]),
                     reads=[("pfa",)], writes=[("rs2_sb",)], small=[rs2_sb[:]])
                p.op("tensor", lambda E: E.matmul(pfb[:], ones2[:], rs2_sb[:],
                                                  start=True, stop=True),
                     reads=[("rs2_sb",), ("ones2",)], writes=[("pfb",)])
                p.op("vector", lambda E: E.tensor_copy(rsbc_sb[:], pfb[:]),
                     reads=[("pfb",)], writes=[("rsbc_sb",)])
                kcr_list = list(range(8)) if w > 0 else list(range(4, 8))
                for h2, bank, bres in ((0, pfc, ("pfc",)), (1, pfa, ("pfa",))):
                    for i, kcr in enumerate(kcr_list):
                        tck = 4 * (w - 1) + kcr
                        p.op("tensor",
                             lambda E, _t=tck, _h2=h2, _hp=hp, _b=bank,
                                    _st=(i == 0), _sp=(i == len(kcr_list) - 1),
                                    _k=kcr:
                             E.matmul(_b[0:64, :],
                                      vt[_t][:, _hp * 128 + _h2 * 64:
                                             _hp * 128 + (_h2 + 1) * 64],
                                      aTf[_h2][_k][:], start=_st, stop=_sp),
                             reads=[("vt", tck), ("aTf", h2, kcr)], writes=[bres])
                p.op("vector",
                     lambda E, _hp=hp, _w=w: E.tensor_tensor(
                         of[_hp][0:64, _w * 512:(_w + 1) * 512], pfc[0:64, :],
                         rsbc_sb[0:64, :], ALU.mult),
                     reads=[("pfc",), ("rsbc_sb",)], writes=[("off", hp, w)])
                p.op("vector",
                     lambda E, _hp=hp, _w=w: E.tensor_tensor(
                         of[_hp][64:128, _w * 512:(_w + 1) * 512], pfa[0:64, :],
                         rsbc_sb[64:128, :], ALU.mult),
                     reads=[("pfa",), ("rsbc_sb",)], writes=[("off", hp, w)])

    def ffn_wo(stka, s, l, xi, of):
        if os.environ.get("KDBG") and s == 0 and l == 0:
            kb.dump("fof0", of[0][:], [("off", 0, w) for w in range(4)])
            kb.dump("fof3", of[3][:], [("off", 3, w) for w in range(4)])
        wob = kb.sbuf("f_wob", [128, 4 * 1024], BF16, stack=stka)
        wos = kb.sbuf("f_wos", [128, 1024], F32, stack=stka)
        for dc in range(4):
            p.dma(wos[:], dr["f_wo"][l, dc * 128:(dc + 1) * 128, :],
                  writes=[("fwos",)])
            p.op("scalar",
                 lambda E, _dc=dc: E.copy(wob[:, _dc * 1024:(_dc + 1) * 1024],
                                          wos[:]),
                 reads=[("fwos",)], writes=[("fwob", dc)])
        for oc in range(NCH):
            for tb in range(NTB):
                pj = (oc * NTB + tb) % 4
                for dc in range(4):
                    p.op("tensor",
                         lambda E, _oc=oc, _tb=tb, _dc=dc, _pj=pj:
                         E.matmul(pscore[:, _pj * 512:(_pj + 1) * 512],
                                  wob[:, _dc * 1024 + _oc * 128:
                                      _dc * 1024 + (_oc + 1) * 128],
                                  of[_dc][:, _tb * 512:(_tb + 1) * 512],
                                  start=(_dc == 0), stop=(_dc == 3)),
                         reads=[("fwob", dc), ("off", dc, tb)],
                         writes=[("pscore", pj)])
                p.op("vector",
                     lambda E, _oc=oc, _t0=tb * 512, _pj=pj: E.tensor_tensor(
                         xi[_oc][:, 1 + _t0:1 + _t0 + 512],
                         xi[_oc][:, 1 + _t0:1 + _t0 + 512],
                         pscore[:, _pj * 512:(_pj + 1) * 512], ALU.add),
                     reads=[("pscore", pj), ("xi", oc, tb)],
                     writes=[("xi", oc, tb)])

    def ffn_mlp_a(stkm, s, l, xi):
        xn2 = [kb.sbuf(f"f_xn2_{c}", [128, S], BF16, stack=stkm)
               for c in range(NCH)]
        xbt = [kb.sbuf(f"f_mxbt{c}", [128, 513], BF16, stack=stkm)
               for c in range(NCH)]
        sqb = [kb.sbuf(f"f_msq{i}", [128, 512], BF16, stack=stkm) for i in range(4)]
        lnt = kb.sbuf("f_lnt", [128, 512], F32, stack=stkm)
        lgt = load_col_table(stkm, "f_lgt", dr["f_lg"][l], NCH)
        lbt = load_col_table(stkm, "f_lbt", dr["f_lb"][l], NCH)
        for tb in range(NTB):
            t0 = tb * 512
            for dc in range(NCH):
                p.op("scalar",
                     lambda E, _dc=dc, _t0=t0: E.copy(
                         xbt[_dc][:], xi[_dc][:, _t0:_t0 + 513]),
                     reads=[("xi", dc, tb), ("xi", dc, tb - 1), ("xi", dc, "pad")],
                     writes=[("mxbt", dc)])
            for dc in range(NCH):
                p.op("tensor",
                     lambda E, _dc=dc, _co=(1 if dc < 4 else 0): E.matmul(
                         pfa[0:1, :], ones_col_b[:],
                         xbt[_dc][:, _co:_co + 512],
                         start=(_dc == 0), stop=(_dc == NCH - 1)),
                     reads=[("mxbt", dc), ("ones_col_b",)], writes=[("pfa",)])
            for dc in range(NCH):
                p.op("scalar",
                     lambda E, _dc=dc, _co=(1 if dc < 4 else 0): E.activation(
                         sqb[_dc % 4][:], xbt[_dc][:, _co:_co + 512], AF.Square),
                     reads=[("mxbt", dc)], writes=[("msqb", dc % 4)])
                p.op("tensor",
                     lambda E, _dc=dc: E.matmul(
                         pfb[0:1, :], ones_col_b[:], sqb[_dc % 4][:],
                         start=(_dc == 0), stop=(_dc == NCH - 1)),
                     reads=[("msqb", dc % 4), ("ones_col_b",)], writes=[("pfb",)])
            p.op("vector", lambda E: E.tensor_scalar_mul(row("mu"), pfa[0:1, :], 1.0 / D),
                 reads=[("pfa",)], writes=[rres("mu")])
            p.op("vector", lambda E: E.tensor_scalar_mul(row("var"), pfb[0:1, :], 1.0 / D),
                 reads=[("pfb",)], writes=[rres("var")])
            p.op("vector", lambda E: E.tensor_mul(row("A"), row("mu"), row("mu")),
                 reads=[rres("mu")], writes=[rres("A")])
            p.op("vector", lambda E: E.tensor_sub(row("var"), row("var"), row("A")),
                 reads=[rres("var"), rres("A")], writes=[rres("var")])
            p.op("vector", lambda E: E.tensor_scalar_add(row("var"), row("var"), EPS),
                 reads=[rres("var")], writes=[rres("var")])
            p.op("scalar", lambda E: E.activation(row("var"), row("var"), AF.Sqrt),
                 reads=[rres("var")], writes=[rres("var")], small=[row("var")])
            p.op("vector", lambda E: E.reciprocal(row("A"), row("var")),
                 reads=[rres("var")], writes=[rres("A")])
            p.op("vector", lambda E: E.tensor_mul(row("B"), row("mu"), row("A")),
                 reads=[rres("mu"), rres("A")], writes=[rres("B")])
            p.op("vector", lambda E: E.tensor_scalar_mul(row("B"), row("B"), -1.0),
                 reads=[rres("B")], writes=[rres("B")], small=[row("B"), row("A")])
            p.op("tensor", lambda E: E.matmul(pfc[:], ones_row_f[:], row("A"),
                                              start=True, stop=True),
                 reads=[rres("A"), ("ones_row_f",)], writes=[("pfc",)])
            p.op("tensor", lambda E: E.matmul(pfa[:], ones_row_f[:], row("B"),
                                              start=True, stop=True),
                 reads=[rres("B"), ("ones_row_f",)], writes=[("pfa",)])
            for dc in range(NCH):
                co = 1 if dc < 4 else 0
                p.op("vector",
                     lambda E, _dc=dc, _co=co: E.tensor_tensor(
                         lnt[:], xbt[_dc][:, _co:_co + 512], pfc[:], ALU.mult),
                     reads=[("mxbt", dc), ("pfc",)], writes=[("flnt",)])
                p.op("vector",
                     lambda E: E.tensor_tensor(lnt[:], lnt[:], pfa[:], ALU.add),
                     reads=[("flnt",), ("pfa",)], writes=[("flnt",)])
                p.op("scalar",
                     lambda E, _dc=dc, _t0=t0: E.activation(
                         xn2[_dc][:, _t0:_t0 + 512], lnt[:], AF.Identity,
                         scale=lgt[:, _dc:_dc + 1], bias=lbt[:, _dc:_dc + 1]),
                     reads=[("flnt",), ("f_lgt",), ("f_lbt",)],
                     writes=[("xn2", dc, tb)])
        def load_b1(stk2, name, base):
            stage = kb.sbuf(name + "_st", [22, 128], F32, stack=stk2)
            tbl = kb.sbuf(name, [128, 22], F32, stack=stk2)
            p.op("vector", lambda E: E.memset(stage[:], 0.0), writes=[(name, "st")])
            p.dma(stage[0:21, :],
                  dr["f_b1"][l, base:base + 2688].rearrange("(c p) -> c p", p=128),
                  writes=[(name, "st")])
            p.dma(stage[21:22, 0:42], dr["f_b1"][l, base + 2688:base + 2730]
                  .unsqueeze(0), writes=[(name, "st")], slow_ok=True)
            p.op("tensor",
                 lambda E: E.transpose(pfb[:, 0:22], stage[:], identf[0:22, 0:22]),
                 reads=[(name, "st"), ("identf",)], writes=[("pfb",)])
            p.op("vector", lambda E: E.tensor_copy(tbl[:], pfb[:, 0:22]),
                 reads=[("pfb",)], writes=[(name,)], small=[tbl[:]])
            return tbl
        if os.environ.get("KDBG") and s == 0 and l == 0:
            kb.dump("fxn2_0", xn2[0][:], [("xn2", 0, t) for t in range(NTB)])
            kb.dump("fxiw0", xi[0][:, 0:1024], [("xi", 0, 0), ("xi", 0, 1)])
        b1a = load_b1(stkm, "f_b1a", 0)
        b1g = load_b1(stkm, "f_b1g", 2730)
        build_w1c = l not in built_fw1c
        built_fw1c.add(l)
        w1s = [kb.sbuf(f"f_w1s{i}", [128, 1024], F32, stack=stkm) for i in range(2)]
        w1ab = [kb.sbuf(f"f_w1ab{i}", [128, 1024], BF16, stack=stkm)
                for i in range(2)]
        w1gb = [kb.sbuf(f"f_w1gb{i}", [128, 1024], BF16, stack=stkm)
                for i in range(2)]
        ggb = kb.sbuf("f_ggb", [128, 512], BF16, stack=stkm)
        atb = kb.sbuf("f_atb", [128, 512], F32, stack=stkm)
        h2st = [kb.sbuf(f"f_h2st{i}", [128, S], BF16, stack=stkm) for i in range(2)]
        for c in range(22):
            wc = 128 if c < 21 else 42
            pr = c % 2
            for pi, part in enumerate(("a", "g")):
                wbuf = (w1ab if part == "a" else w1gb)[pr]
                wres = ("w1" + part, pr)
                if build_w1c:
                    base = c * 128 if part == "a" else 2730 + c * 128
                    st_ = w1s[pi]
                    p.dma(st_[:, 0:8 * wc].rearrange("p (c n) -> p c n", c=8),
                          dr["f_w1"][l, :, base:base + wc]
                          .rearrange("(c p) n -> p c n", p=128),
                          writes=[("w1s", part)])
                    p.op("scalar",
                         lambda E, _w=wbuf, _s=st_, _n=8 * wc: E.copy(
                             _w[:, 0:_n], _s[:, 0:_n]),
                         reads=[("w1s", part)], writes=[wres])
                    p.dma(fw1c[l, pi, c, :, 0:8 * wc], wbuf[:, 0:8 * wc],
                          reads=[wres])
                else:
                    p.dma(wbuf[:, 0:8 * wc], fw1c[l, pi, c, :, 0:8 * wc],
                          writes=[wres])
            for tb in range(NTB):
                pja, pjg = (tb * 2) % 4, (tb * 2 + 1) % 4
                for dc in range(NCH):
                    p.op("tensor",
                         lambda E, _dc=dc, _tb=tb, _pj=pjg, _wc=wc,
                         _w=w1gb[pr]: E.matmul(
                             pscore[0:_wc, _pj * 512:(_pj + 1) * 512],
                             _w[:, _dc * _wc:(_dc + 1) * _wc],
                             xn2[_dc][:, _tb * 512:(_tb + 1) * 512],
                             start=(_dc == 0), stop=(_dc == NCH - 1)),
                         reads=[("w1g", pr), ("xn2", dc, tb)],
                         writes=[("pscore", pjg)])
                for dc in range(NCH):
                    p.op("tensor",
                         lambda E, _dc=dc, _tb=tb, _pj=pja, _wc=wc,
                         _w=w1ab[pr]: E.matmul(
                             pscore[0:_wc, _pj * 512:(_pj + 1) * 512],
                             _w[:, _dc * _wc:(_dc + 1) * _wc],
                             xn2[_dc][:, _tb * 512:(_tb + 1) * 512],
                             start=(_dc == 0), stop=(_dc == NCH - 1)),
                         reads=[("w1a", pr), ("xn2", dc, tb)],
                         writes=[("pscore", pja)])
                p.op("scalar",
                     lambda E, _pj=pjg, _wc=wc, _c=c: E.activation(
                         ggb[0:_wc, :], pscore[0:_wc, _pj * 512:(_pj + 1) * 512],
                         AF.Gelu, bias=b1g[0:_wc, _c:_c + 1]),
                     reads=[("pscore", pjg), ("f_b1g",)], writes=[("ggb",)])
                p.op("scalar",
                     lambda E, _pj=pja, _wc=wc, _c=c: E.activation(
                         atb[0:_wc, :], pscore[0:_wc, _pj * 512:(_pj + 1) * 512],
                         AF.Identity, bias=b1a[0:_wc, _c:_c + 1]),
                     reads=[("pscore", pja), ("f_b1a",)], writes=[("atb",)])
                p.op("vector",
                     lambda E, _c=c, _tb=tb, _wc=wc: E.tensor_tensor(
                         h2st[_c % 2][0:_wc, _tb * 512:(_tb + 1) * 512],
                         atb[0:_wc, :], ggb[0:_wc, :], ALU.mult),
                     reads=[("atb",), ("ggb",)], writes=[("h2st", c % 2)])
            p.dma(h2dram[c, 0:wc, :], h2st[c % 2][0:wc, :],
                  reads=[("h2st", c % 2)])

    def ffn_mlp_b(stkb, s, l, xi):
        dodump = os.environ.get("KDBG") and s == 0 and l == 0
        w2b = kb.sbuf("f_w2b", [128, 22 * 1024], BF16, stack=stkb)
        w2s = [kb.sbuf(f"f_w2s{i}", [128, 1024], F32, stack=stkb) for i in range(2)]
        b2t = load_col_table(stkb, "f_b2t", dr["f_b2"][l], NCH)
        h2l = [kb.sbuf(f"f_h2l{i}", [128, 22 * 512], BF16, stack=stkb)
               for i in range(2)]
        tmpo = kb.sbuf("f_tmpo", [128, 512], F32, stack=stkb)
        for c in range(22):
            wc = 128 if c < 21 else 42
            st_ = w2s[c % 2]
            p.dma(st_[0:wc, :], dr["f_w2"][l, c * 128:c * 128 + wc, :],
                  writes=[("w2s", c % 2)])
            p.op("scalar",
                 lambda E, _c=c, _s=st_, _wc=wc: E.copy(
                     w2b[0:_wc, _c * 1024:(_c + 1) * 1024], _s[0:_wc, :]),
                 reads=[("w2s", c % 2)], writes=[("w2b", c)])
        for tb in range(NTB):
            t0 = tb * 512
            hb = h2l[tb % 2]
            for c in range(22):
                wc = 128 if c < 21 else 42
                p.dma(hb[0:wc, c * 512:(c + 1) * 512],
                      h2dram[c, 0:wc, t0:t0 + 512], writes=[("h2l", tb % 2, c)])
            for oc in range(NCH):
                pj = (tb * 2 + oc) % 4
                for c in range(22):
                    wc = 128 if c < 21 else 42
                    p.op("tensor",
                         lambda E, _c=c, _oc=oc, _pj=pj, _wc=wc, _hb=hb: E.matmul(
                             pscore[:, _pj * 512:(_pj + 1) * 512],
                             w2b[0:_wc, _c * 1024 + _oc * 128:
                                 _c * 1024 + (_oc + 1) * 128],
                             _hb[0:_wc, _c * 512:(_c + 1) * 512],
                             start=(_c == 0), stop=(_c == 21)),
                         reads=[("w2b", c), ("h2l", tb % 2, c)],
                         writes=[("pscore", pj)])
                p.op("scalar",
                     lambda E, _pj=pj, _oc=oc: E.activation(
                         tmpo[:], pscore[:, _pj * 512:(_pj + 1) * 512],
                         AF.Identity, bias=b2t[:, _oc:_oc + 1]),
                     reads=[("pscore", pj), ("f_b2t",)], writes=[("ftmpo",)])
                p.op("vector",
                     lambda E, _oc=oc, _t0=t0: E.tensor_tensor(
                         xi[_oc][:, 1 + _t0:1 + _t0 + 512],
                         xi[_oc][:, 1 + _t0:1 + _t0 + 512], tmpo[:], ALU.add),
                     reads=[("ftmpo",), ("xi", oc, tb)], writes=[("xi", oc, tb)])
        if dodump:
            kb.dump("fxim0", xi[0][:, 0:1024], [("xi", 0, 0), ("xi", 0, 1)])
            kb.dump("fh2l0", h2l[1][:, 0:1024], [("h2l", 1, 0), ("h2l", 1, 1)])

    def phase_ffn(s, inv):
        with kb.scope() as stk:
            xi = [kb.sbuf(f"f_xi{c}", [128, S + 1], F32, stack=stk)
                  for c in range(NCH)]
            rott = [kb.sbuf(f"f_rot{i}", [128, NTC * 64], F32, stack=stk)
                    for i in range(4)]
            for i in range(4):
                p.dma(rott[i][:], rotdram[i, :, :], writes=[("rot",)])
            with kb.scope() as stk0:
                xnb = [kb.sbuf(f"f_xnb{c}", [128, S], BF16, stack=stk0)
                       for c in range(NCH)]
                lntmp = kb.sbuf("f_lntmp", [128, 512], F32, stack=stk0)
                for dc in range(NCH):
                    p.op("vector", lambda E, _dc=dc: E.memset(xi[_dc][:, 0:1], 0.0),
                         writes=[("xi", dc, "pad")])
                stage_ln(stk0, s, xnb, first=False, ln_out=xi, out_off=1,
                         tmpf=lntmp)
            p.barrier()
            for l in range(2):
                with kb.scope() as stko:
                    of = [kb.sbuf(f"f_of{i}", [128, S], BF16, stack=stko)
                          for i in range(4)]
                    vt = [kb.sbuf(f"f_vt{i}", [128, 512], BF16, stack=stko)
                          for i in range(NTC)]
                    qff = [kb.sbuf(f"f_qf{i}", [128, S], BF16, stack=stko)
                           for i in range(4)]
                    kff = [kb.sbuf(f"f_kf{i}", [128, S], BF16, stack=stko)
                           for i in range(4)]
                    with kb.scope() as stka:
                        ffn_qkv_rotary(stka, s, l, xi, vt, qff, kff, rott)
                    p.barrier()
                    with kb.scope() as stka:
                        ffn_attn(stka, s, l, vt, qff, kff, of)
                    p.barrier()
                    with kb.scope() as stka:
                        ffn_wo(stka, s, l, xi, of)
                    p.barrier()
                with kb.scope() as stkm:
                    ffn_mlp_a(stkm, s, l, xi)
                    p.barrier()
                with kb.scope() as stkb:
                    ffn_mlp_b(stkb, s, l, xi)
                p.barrier()
            with kb.scope() as stk9:
                xblk = [kb.sbuf(f"f_oxb{i}", [128, 512], F32, stack=stk9)
                        for i in range(2)]
                for tb in range(NTB):
                    for dc in range(NCH):
                        bx = xblk[dc % 2]
                        p.dma(bx[:], xdram[s, dc, :, tb * 512:(tb + 1) * 512],
                              writes=[("oxblk", dc % 2)])
                        p.op("vector",
                             lambda E, _b=bx, _dc=dc, _t0=tb * 512: E.tensor_tensor(
                                 _b[:], _b[:], xi[_dc][:, 1 + _t0:1 + _t0 + 512],
                                 ALU.add),
                             reads=[("oxblk", dc % 2), ("xi", dc, tb)],
                             writes=[("oxblk", dc % 2)])
                        p.dma(xdram[s, dc, :, tb * 512:(tb + 1) * 512], bx[:],
                              reads=[("oxblk", dc % 2)])
        p.barrier()

    def phase_attn_b(s):
        with kb.scope() as stk:
            xnf = [kb.sbuf(f"b_xnf{c}", [128, S], F32, stack=stk)
                   for c in range(NCH)]
            with kb.scope() as stk0:
                xnb0 = [kb.sbuf(f"b_xnb0_{c}", [128, S], BF16, stack=stk0)
                        for c in range(NCH)]
                lntmp = kb.sbuf("b_lntmp", [128, 512], F32, stack=stk0)
                stage_ln(stk0, s, xnb0, first=False, ln_out=xnf, out_off=0,
                         tmpf=lntmp)
            p.barrier()
            XNF = [("xi", dc, tb) for dc in range(NCH) for tb in range(NTB)]

            def xnfr(dc):
                return [("xi", dc, t) for t in range(NTB)]

            with kb.scope() as stke:
                xnb8 = [kb.sbuf(f"b_xnb8_{c}", [128, S], BF16, stack=stke)
                        for c in range(NCH)]
                for dc in range(NCH):
                    p.op("scalar",
                         lambda E, _dc=dc: E.copy(xnb8[_dc][:], xnf[_dc][:]),
                         reads=xnfr(dc), writes=[("bxnb", dc)])
                posc = kb.sbuf("b_posc", [128, S], BF16, stack=stke)
                rpos = kb.sbuf("b_rpos", [128, S], BF16, stack=stke)
                with kb.scope() as stkt:
                    tia = kb.sbuf("b_tia", [128, S], F32, stack=stkt)
                    tia2 = kb.sbuf("b_tia2", [128, S], F32, stack=stkt)
                    g(lambda E: E.iota(tia[:], [[1, S]], base=1,
                                       channel_multiplier=0,
                                       allow_small_or_imprecise_dtypes=True),
                      [("btia",)])
                    kb.gfence([(tia[:], ("btia",))])
                    p.op("vector", lambda E: E.reciprocal(tia2[:], tia[:]),
                         reads=[("btia",)], writes=[("btia2",)])
                    p.op("scalar", lambda E: E.copy(rpos[:], tia2[:]),
                         reads=[("btia2",)], writes=[("brpos",)])
                    p.op("scalar", lambda E: E.activation(tia[:], tia[:], AF.Sqrt),
                         reads=[("btia",)], writes=[("btia",)])
                    p.op("vector", lambda E: E.reciprocal(tia2[:], tia[:]),
                         reads=[("btia",)], writes=[("btia2",)])
                    p.op("scalar", lambda E: E.copy(posc[:], tia2[:]),
                         reads=[("btia2",)], writes=[("bposc",)])
                p.barrier()
                womb = kb.sbuf("b_womb", [128, NCH * 1024], BF16, stack=stke)
                woms = kb.sbuf("b_woms", [128, 1024], F32, stack=stke)
                for dc in range(NCH):
                    p.dma(woms[:], dr["b_womT"][dc * 128:(dc + 1) * 128, :],
                          writes=[("bwoms",)])
                    p.op("scalar",
                         lambda E, _dc=dc: E.copy(
                             womb[:, _dc * 1024:(_dc + 1) * 1024], woms[:]),
                         reads=[("bwoms",)], writes=[("bwomb", dc)])
                els_t = load_col_table(stke, "b_els", dr["b_ls"][:], NCH)
                bom_t = load_col_table(stke, "b_bom", dr["b_bom"][:], NCH)
                p.op("scalar", lambda E: E.activation(els_t[:], els_t[:], AF.Exp),
                     reads=[("b_els",)], writes=[("b_els",)], small=[els_t[:]])
                ebom_t = kb.sbuf("b_ebom", [128, NCH], F32, stack=stke)
                p.op("vector", lambda E: E.tensor_mul(ebom_t[:], els_t[:], bom_t[:]),
                     reads=[("b_els",), ("b_bom",)], writes=[("b_ebom",)],
                     small=[ebom_t[:]])
                og = kb.sbuf("b_og", [128, S], F32, stack=stke)
                u_ = kb.sbuf("b_u", [128, S], F32, stack=stke)
                ki_ = kb.sbuf("b_ki", [128, S], mybir.dt.int32, stack=stke)
                phi = kb.sbuf("b_phi", [128, S], F32, stack=stke)
                kf_ = phi
                sinb = kb.sbuf("b_sin", [128, S], BF16, stack=stke)
                cosb = kb.sbuf("b_cos", [128, S], BF16, stack=stke)
                cmr = kb.sbuf("b_cmr", [128, S], BF16, stack=stke)
                cmi = kb.sbuf("b_cmi", [128, S], BF16, stack=stke)
                memr = kb.sbuf("b_memr", [128, S], F32, stack=stke)
                memi = kb.sbuf("b_memi", [128, S], F32, stack=stke)
                ones1 = kb.sbuf("b_ones", [128, 1], BF16, stack=stke)
                p.op("vector", lambda E: E.memset(ones1[:], 1.0),
                     writes=[("bones",)], small=[ones1[:]])
                onesb_f = None
                for ec in range(NCH):
                    for tb in range(NTB):
                        pj = tb % 2
                        for dc in range(NCH):
                            p.op("tensor",
                                 lambda E, _ec=ec, _tb=tb, _dc=dc, _pj=pj:
                                 E.matmul(pscore[:, _pj * 512:(_pj + 1) * 512],
                                          womb[:, _dc * 1024 + _ec * 128:
                                               _dc * 1024 + (_ec + 1) * 128],
                                          xnb8[_dc][:, _tb * 512:(_tb + 1) * 512],
                                          start=(_dc == 0), stop=(_dc == NCH - 1)),
                                 reads=[("bwomb", dc), ("bxnb", dc)],
                                 writes=[("pscore", pj)])
                        p.op("scalar",
                             lambda E, _ec=ec, _tb=tb, _pj=pj: E.activation(
                                 og[:, _tb * 512:(_tb + 1) * 512],
                                 pscore[:, _pj * 512:(_pj + 1) * 512], AF.Identity,
                                 scale=els_t[:, _ec:_ec + 1],
                                 bias=ebom_t[:, _ec:_ec + 1]),
                             reads=[("pscore", pj), ("b_els",), ("b_ebom",)],
                             writes=[("bog",)])
                    p.op("vector",
                         lambda E: E.tensor_tensor(og[:], og[:], posc[:], ALU.mult),
                         reads=[("bog",), ("bposc",)], writes=[("bog",)])
                    p.op("vector",
                         lambda E: E.tensor_tensor_scan(phi[:], ones1[:].broadcast_to((128, S)), og[:],
                                                        0.0, ALU.mult, ALU.add),
                         reads=[("bog",), ("bones",)], writes=[("bphi",)])
                    p.op("vector",
                         lambda E: E.tensor_scalar_mul(u_[:], phi[:],
                                                       float(1.0 / TWO_PI)),
                         reads=[("bphi",)], writes=[("bu",)])
                    p.op("vector", lambda E: E.tensor_copy(ki_[:], u_[:]),
                         reads=[("bu",)], writes=[("bki",)])
                    p.op("vector", lambda E: E.tensor_copy(kf_[:], ki_[:]),
                         reads=[("bki",)], writes=[("bphi",)])
                    p.op("vector", lambda E: E.tensor_sub(u_[:], u_[:], kf_[:]),
                         reads=[("bu",), ("bphi",)], writes=[("bu",)])
                    p.op("scalar",
                         lambda E: E.activation(sinb[:], u_[:], AF.Sin,
                                                scale=TWO_PI),
                         reads=[("bu",)], writes=[("bsin",)])
                    p.op("vector", lambda E: E.tensor_scalar_add(u_[:], u_[:], 0.25),
                         reads=[("bu",)], writes=[("bu",)])
                    p.op("vector",
                         lambda E: E.tensor_scalar(kf_[:], u_[:], 0.5, 1.0,
                                                   ALU.is_gt, ALU.mult),
                         reads=[("bu",)], writes=[("bphi",)])
                    p.op("vector",
                         lambda E: E.scalar_tensor_tensor(u_[:], kf_[:], -1.0,
                                                          u_[:], ALU.mult, ALU.add),
                         reads=[("bphi",), ("bu",)], writes=[("bu",)])
                    p.op("scalar",
                         lambda E: E.activation(cosb[:], u_[:], AF.Sin,
                                                scale=TWO_PI),
                         reads=[("bu",)], writes=[("bcos",)])
                    p.op("vector",
                         lambda E, _ec=ec: E.tensor_tensor(
                             cmr[:], xnf[_ec][:], cosb[:], ALU.mult),
                         reads=xnfr(ec) + [("bcos",)], writes=[("bcmr",)])
                    p.op("vector",
                         lambda E, _ec=ec: E.tensor_tensor(
                             cmi[:], xnf[_ec][:], sinb[:], ALU.mult),
                         reads=xnfr(ec) + [("bsin",)], writes=[("bcmi",)])
                    p.op("vector",
                         lambda E: E.tensor_tensor_scan(memr[:], ones1[:].broadcast_to((128, S)),
                                                        cmr[:], 0.0, ALU.mult,
                                                        ALU.add),
                         reads=[("bcmr",), ("bones",)], writes=[("bmemr",)])
                    p.op("vector",
                         lambda E: E.tensor_tensor(memr[:], memr[:], rpos[:],
                                                   ALU.mult),
                         reads=[("bmemr",), ("brpos",)], writes=[("bmemr",)])
                    p.op("vector",
                         lambda E: E.tensor_tensor_scan(memi[:], ones1[:].broadcast_to((128, S)),
                                                        cmi[:], 0.0, ALU.mult,
                                                        ALU.add),
                         reads=[("bcmi",), ("bones",)], writes=[("bmemi",)])
                    p.op("vector",
                         lambda E: E.tensor_tensor(memi[:], memi[:], rpos[:],
                                                   ALU.mult),
                         reads=[("bmemi",), ("brpos",)], writes=[("bmemi",)])
                    if os.environ.get("KDBG") and s == 0 and ec == 0:
                        kb.dump("bog0", og[:], [("bog",)])
                        kb.dump("bphi0", phi[:], [("bphi",)])
                        kb.dump("bcos0", cosb[:], [("bcos",)])
                        kb.dump("bsin0", sinb[:], [("bsin",)])
                        kb.dump("bcmr0", cmr[:], [("bcmr",)])
                        kb.dump("bmemr0", memr[:], [("bmemr",)])
                    p.dma(ctxdram[ec, :, :], cmr[:], reads=[("bcmr",)])
                    p.dma(ctxdram[8 + ec, :, :], cmi[:], reads=[("bcmi",)])
                    p.op("vector",
                         lambda E: E.tensor_tensor(og[:], memr[:], cosb[:],
                                                   ALU.mult),
                         reads=[("bmemr",), ("bcos",)], writes=[("bog",)])
                    p.op("vector",
                         lambda E: E.tensor_tensor(u_[:], memi[:], sinb[:],
                                                   ALU.mult),
                         reads=[("bmemi",), ("bsin",)], writes=[("bu",)])
                    p.op("vector",
                         lambda E: E.tensor_tensor(cmr[:], og[:], u_[:], ALU.add),
                         reads=[("bog",), ("bu",)], writes=[("bcmr",)])
                    p.op("vector",
                         lambda E: E.tensor_tensor(og[:], memi[:], cosb[:],
                                                   ALU.mult),
                         reads=[("bmemi",), ("bcos",)], writes=[("bog",)])
                    p.op("vector",
                         lambda E: E.tensor_tensor(u_[:], memr[:], sinb[:],
                                                   ALU.mult),
                         reads=[("bmemr",), ("bsin",)], writes=[("bu",)])
                    p.op("vector",
                         lambda E: E.tensor_tensor(cmi[:], og[:], u_[:],
                                                   ALU.subtract),
                         reads=[("bog",), ("bu",)], writes=[("bcmi",)])
                    if os.environ.get("KDBG") and s == 0 and ec == 0:
                        kb.dump("bretr0", cmr[:], [("bcmr",)])
                    p.dma(ctxdram[16 + ec, :, :], cmr[:], reads=[("bcmr",)])
                    p.dma(ctxdram[24 + ec, :, :], cmi[:], reads=[("bcmi",)])
            p.barrier()

            with kb.scope() as stkl:
                rsum = kb.sbuf("b_rsum", [1, S], F32, stack=stkl)
                rsq = kb.sbuf("b_rsq", [1, S], F32, stack=stkl)
                cblk = [kb.sbuf(f"b_cblk{i}", [128, S], BF16, stack=stkl)
                        for i in range(2)]
                csq = kb.sbuf("b_csq", [128, S], BF16, stack=stkl)
                gam_t = load_col_table(stkl, "b_gam", dr["b_lg"][:], 32)
                bet_t = load_col_table(stkl, "b_bet", dr["b_lb"][:], 32)
                p.op("vector", lambda E: E.memset(rsum[:], 0.0), writes=[("brsum",)])
                p.op("vector", lambda E: E.memset(rsq[:], 0.0), writes=[("brsq",)])
                for cc in range(32):
                    b = cblk[cc % 2]
                    p.dma(b[:], ctxdram[cc, :, :], writes=[("bcblk", cc % 2)])
                    p.op("scalar",
                         lambda E, _b=b: E.activation(csq[:], _b[:], AF.Square),
                         reads=[("bcblk", cc % 2)], writes=[("bcsq",)])
                    for tb in range(NTB):
                        bank, bres = (pfa, ("pfa",)) if tb % 2 == 0 else (pfb, ("pfb",))
                        p.op("tensor",
                             lambda E, _tb=tb, _b=b, _bk=bank:
                             E.matmul(_bk[0:1, :], ones_col_b[:],
                                      _b[:, _tb * 512:(_tb + 1) * 512],
                                      start=True, stop=True),
                             reads=[("bcblk", cc % 2), ("ones_col_b",)],
                             writes=[bres])
                        p.op("vector",
                             lambda E, _tb=tb, _bk=bank: E.tensor_tensor(
                                 rsum[0:1, _tb * 512:(_tb + 1) * 512],
                                 rsum[0:1, _tb * 512:(_tb + 1) * 512],
                                 _bk[0:1, :], ALU.add),
                             reads=[bres, ("brsum",)], writes=[("brsum",)])
                    for tb in range(NTB):
                        bank, bres = (pfa, ("pfa",)) if tb % 2 == 0 else (pfb, ("pfb",))
                        p.op("tensor",
                             lambda E, _tb=tb, _bk=bank:
                             E.matmul(_bk[0:1, :], ones_col_b[:],
                                      csq[:, _tb * 512:(_tb + 1) * 512],
                                      start=True, stop=True),
                             reads=[("bcsq",), ("ones_col_b",)], writes=[bres])
                        p.op("vector",
                             lambda E, _tb=tb, _bk=bank: E.tensor_tensor(
                                 rsq[0:1, _tb * 512:(_tb + 1) * 512],
                                 rsq[0:1, _tb * 512:(_tb + 1) * 512],
                                 _bk[0:1, :], ALU.add),
                             reads=[bres, ("brsq",)], writes=[("brsq",)])
                rt = kb.sbuf("b_rt", [1, S], F32, stack=stkl)
                p.op("vector", lambda E: E.tensor_scalar_mul(rsum[:], rsum[:],
                                                             1.0 / 4096.0),
                     reads=[("brsum",)], writes=[("brsum",)])
                p.op("vector", lambda E: E.tensor_scalar_mul(rsq[:], rsq[:],
                                                             1.0 / 4096.0),
                     reads=[("brsq",)], writes=[("brsq",)])
                p.op("vector", lambda E: E.tensor_mul(rt[:], rsum[:], rsum[:]),
                     reads=[("brsum",)], writes=[("brt",)])
                p.op("vector", lambda E: E.tensor_sub(rsq[:], rsq[:], rt[:]),
                     reads=[("brsq",), ("brt",)], writes=[("brsq",)])
                p.op("vector", lambda E: E.tensor_scalar_add(rsq[:], rsq[:], EPS),
                     reads=[("brsq",)], writes=[("brsq",)])
                p.op("scalar", lambda E: E.activation(rsq[:], rsq[:], AF.Sqrt),
                     reads=[("brsq",)], writes=[("brsq",)], small=[rsq[:]])
                p.op("vector", lambda E: E.reciprocal(rt[:], rsq[:]),
                     reads=[("brsq",)], writes=[("brt",)])
                p.op("vector", lambda E: E.tensor_mul(rsum[:], rsum[:], rt[:]),
                     reads=[("brsum",), ("brt",)], writes=[("brsum",)])
                p.op("vector", lambda E: E.tensor_scalar_mul(rsum[:], rsum[:], -1.0),
                     reads=[("brsum",)], writes=[("brsum",)],
                     small=[rsum[:], rt[:]])
                if os.environ.get("KDBG") and s == 0:
                    kb.dump("brows_mu", rsum[:], [("brsum",)])
                    kb.dump("brows_rt", rt[:], [("brt",)])
                happ = kb.sbuf("b_happ", [128, 512], F32, stack=stkl)
                hblk2 = [kb.sbuf(f"b_hap{i}", [128, 512], BF16, stack=stkl)
                         for i in range(2)]
                for tb in range(NTB):
                    t0 = tb * 512
                    p.op("tensor",
                         lambda E, _t0=t0: E.matmul(pfc[:], ones_row_f[:],
                                                    rt[0:1, _t0:_t0 + 512],
                                                    start=True, stop=True),
                         reads=[("brt",), ("ones_row_f",)], writes=[("pfc",)])
                    p.op("tensor",
                         lambda E, _t0=t0: E.matmul(pscore[:, 1536:2048],
                                                    ones_row_f[:],
                                                    rsum[0:1, _t0:_t0 + 512],
                                                    start=True, stop=True),
                         reads=[("brsum",), ("ones_row_f",)],
                         writes=[("pscore", 3)])
                    for cc in range(32):
                        hb2 = hblk2[cc % 2]
                        p.dma(hb2[:], ctxdram[cc, :, t0:t0 + 512],
                              writes=[("bhap", cc % 2)])
                        p.op("vector",
                             lambda E, _b=hb2: E.tensor_tensor(
                                 happ[:], _b[:], pfc[:], ALU.mult),
                             reads=[("bhap", cc % 2), ("pfc",)],
                             writes=[("bhapp",)])
                        p.op("vector",
                             lambda E: E.tensor_tensor(
                                 happ[:], happ[:], pscore[:, 1536:2048], ALU.add),
                             reads=[("bhapp",), ("pscore", 3)], writes=[("bhapp",)])
                        p.op("scalar",
                             lambda E, _cc=cc, _b=hb2: E.activation(
                                 _b[:], happ[:], AF.Identity,
                                 scale=gam_t[:, _cc:_cc + 1],
                                 bias=bet_t[:, _cc:_cc + 1]),
                             reads=[("bhapp",), ("b_gam",), ("b_bet",)],
                             writes=[("bhap", cc % 2)])
                        if (os.environ.get("KDBG") and s == 0 and tb == 0
                                and cc == 0):
                            kb.dump("bh0", hb2[:], [("bhap", 0)])
                        p.dma(ctxdram[cc, :, t0:t0 + 512], hb2[:],
                              reads=[("bhap", cc % 2)])
            p.barrier()

            if s == 0:
                with kb.scope() as stkc:
                    cw1s = [kb.sbuf(f"b_cw1s{i}", [128, 32 * 128], F32,
                                    stack=stkc) for i in range(2)]
                    cw1b = [kb.sbuf(f"b_cw1b{i}", [128, 32 * 128], BF16,
                                    stack=stkc) for i in range(2)]
                    for oc in range(16):
                        pr = oc % 2
                        p.dma(cw1s[pr][:].rearrange("p (c n) -> p c n", c=32),
                              dr["b_w1T"][:, oc * 128:(oc + 1) * 128]
                              .rearrange("(c p) n -> p c n", p=128),
                              writes=[("bcw1s", pr)])
                        p.op("scalar",
                             lambda E, _p=pr: E.copy(cw1b[_p][:], cw1s[_p][:]),
                             reads=[("bcw1s", pr)], writes=[("bcw1b", pr)])
                        p.dma(bw1c[oc], cw1b[pr][:], reads=[("bcw1b", pr)])
                p.barrier()

            with kb.scope() as stkw:
                w2b = kb.sbuf("b_w2b", [128, 16 * 1024], BF16, stack=stkw)
                w2s = [kb.sbuf(f"b_w2s{i}", [128, 1024], F32, stack=stkw)
                       for i in range(2)]
                for c2 in range(16):
                    st_ = w2s[c2 % 2]
                    p.dma(st_[:], dr["b_w2T"][c2 * 128:(c2 + 1) * 128, :],
                          writes=[("bw2s", c2 % 2)])
                    p.op("scalar",
                         lambda E, _c=c2, _s=st_: E.copy(
                             w2b[:, _c * 1024:(_c + 1) * 1024], _s[:]),
                         reads=[("bw2s", c2 % 2)], writes=[("bw2b", c2)])
                b1t = load_col_table(stkw, "b_b1t", dr["b_b1"][:], 16)
                b2t = load_col_table(stkw, "b_b2t", dr["b_b2"][:], NCH)
                w1b = [kb.sbuf(f"b_w1b{i}", [128, 32 * 128], BF16, stack=stkw)
                       for i in range(2)]
                hblk = [kb.sbuf(f"b_hblk{c}", [128, 512], BF16, stack=stkw)
                        for c in range(32)]
                o1g = [kb.sbuf(f"b_o1g{c}", [128, 512], BF16, stack=stkw)
                       for c in range(16)]
                tmpo = kb.sbuf("b_tmpo", [128, 512], F32, stack=stkw)
                xblk = [kb.sbuf(f"b_xblk{i}", [128, 512], F32, stack=stkw)
                        for i in range(2)]
                for tb in range(NTB):
                    t0 = tb * 512
                    for cc in range(32):
                        p.dma(hblk[cc][:], ctxdram[cc, :, t0:t0 + 512],
                              writes=[("bhblk", cc)])
                    for oc in range(16):
                        wb = w1b[oc % 2]
                        p.dma(wb[:], bw1c[oc], writes=[("bw1b", oc % 2)])
                        pj = oc % 2
                        for cc in range(32):
                            p.op("tensor",
                                 lambda E, _cc=cc, _pj=pj, _w=wb: E.matmul(
                                     pscore[:, _pj * 512:(_pj + 1) * 512],
                                     _w[:, _cc * 128:(_cc + 1) * 128],
                                     hblk[_cc][:],
                                     start=(_cc == 0), stop=(_cc == 31)),
                                 reads=[("bw1b", oc % 2), ("bhblk", cc)],
                                 writes=[("pscore", pj)])
                        p.op("scalar",
                             lambda E, _oc=oc, _pj=pj: E.activation(
                                 o1g[_oc][:], pscore[:, _pj * 512:(_pj + 1) * 512],
                                 AF.Gelu, bias=b1t[:, _oc:_oc + 1]),
                             reads=[("pscore", pj), ("b_b1t",)],
                             writes=[("bo1g", oc)])
                        if os.environ.get("KDBG") and s == 0 and tb == 0:
                            kb.dump(f"bo1g{oc}", o1g[oc][:], [("bo1g", oc)])
                    for oc2 in range(NCH):
                        pj = 2 + oc2 % 2
                        for c2 in range(16):
                            p.op("tensor",
                                 lambda E, _c=c2, _oc=oc2, _pj=pj: E.matmul(
                                     pscore[:, _pj * 512:(_pj + 1) * 512],
                                     w2b[:, _c * 1024 + _oc * 128:
                                         _c * 1024 + (_oc + 1) * 128],
                                     o1g[_c][:],
                                     start=(_c == 0), stop=(_c == 15)),
                                 reads=[("bw2b", c2), ("bo1g", c2)],
                                 writes=[("pscore", pj)])
                        p.op("scalar",
                             lambda E, _oc=oc2, _pj=pj: E.activation(
                                 tmpo[:], pscore[:, _pj * 512:(_pj + 1) * 512],
                                 AF.Identity, bias=b2t[:, _oc:_oc + 1]),
                             reads=[("pscore", pj), ("b_b2t",)],
                             writes=[("btmpo",)])
                        if (os.environ.get("KDBG") and s == 0 and tb == 0
                                and oc2 == 0):
                            kb.dump("bout2_0", tmpo[:], [("btmpo",)])
                        p.op("vector",
                             lambda E, _oc=oc2, _t0=t0: E.tensor_tensor(
                                 tmpo[:], tmpo[:], xnf[_oc][:, _t0:_t0 + 512],
                                 ALU.add),
                             reads=[("btmpo",), ("xi", oc2, tb)],
                             writes=[("btmpo",)])
                        bx = xblk[oc2 % 2]
                        p.dma(bx[:], xdram[s, oc2, :, t0:t0 + 512],
                              writes=[("bxblk", oc2 % 2)])
                        p.op("vector",
                             lambda E, _b=bx: E.tensor_tensor(
                                 _b[:], _b[:], tmpo[:], ALU.add),
                             reads=[("bxblk", oc2 % 2), ("btmpo",)],
                             writes=[("bxblk", oc2 % 2)])
                        p.dma(xdram[s, oc2, :, t0:t0 + 512], bx[:],
                              reads=[("bxblk", oc2 % 2)])
        p.barrier()

    def phase_output(s):
        with kb.scope() as stk:
            xfm = [kb.sbuf(f"o_xfm{i}", [128, 512], F32, stack=stk) for i in range(NCH)]
            otm = kb.sbuf("o_otm", [128, D], F32, stack=stk)
            for tb in range(NTB):
                t0 = tb * 512
                for dc in range(NCH):
                    p.dma(xfm[dc][:], xdram[s, dc, :, t0:t0 + 512],
                          writes=[("oxfm", dc)])
                for t4 in range(4):
                    pj = t4 % 2
                    for dc in range(NCH):
                        p.op("tensor",
                             lambda E, _t4=t4, _dc=dc, _pj=pj:
                             E.transpose(
                                 pscore[:, _pj * 1024 + _dc * 128: _pj * 1024 + (_dc + 1) * 128],
                                 xfm[_dc][:, _t4 * 128:(_t4 + 1) * 128],
                                 identf[:]),
                             reads=[("oxfm", dc), ("identf",)],
                             writes=[("pscore", pj * 2), ("pscore", pj * 2 + 1)])
                    p.op("vector",
                         lambda E, _pj=pj: E.tensor_copy(
                             otm[:], pscore[:, _pj * 1024:(_pj + 1) * 1024]),
                         reads=[("pscore", pj * 2), ("pscore", pj * 2 + 1)],
                         writes=[("otm",)])
                    p.dma(out[s, t0 + t4 * 128: t0 + (t4 + 1) * 128, :], otm[:],
                          reads=[("otm",)])
        p.barrier()

    for s in range(NSEQ):
        if "attn_a" in SUBLAYERS:
            phase_attn_a(s, first=True)
    if "ffn1" in SUBLAYERS or "ffn2" in SUBLAYERS:
        build_rotary()
    if "ffn1" in SUBLAYERS:
        for s in range(NSEQ):
            phase_ffn(s, 1)
    if "attn_b" in SUBLAYERS:
        for s in range(NSEQ):
            phase_attn_b(s)
    if "ffn2" in SUBLAYERS:
        for s in range(NSEQ):
            phase_ffn(s, 2)
    for s in range(NSEQ):
        phase_output(s)


_CACHE = {}


def _get_built():
    if "nc" not in _CACHE:
        kb = KB()
        kb.stack.__enter__()
        try:
            build(kb)
            kb.p.finalize_and_emit(kb.stack)
        finally:
            pass
        _CACHE["kb"] = kb
        _CACHE["nc"] = kb.nc
        _CACHE["stats"] = (dict(kb.p.cum), kb.p.n_waits)
    return _CACHE["nc"]


def kernel(L0_attn_a_x, L0_attn_a_w_qkv, L0_attn_a_b_qkv, L0_attn_a_w_out,
           L0_attn_a_b_out, L0_attn_b_W_omega, L0_attn_b_b_omega,
           L0_attn_b_log_scale, L0_attn_b_ln_gamma, L0_attn_b_ln_beta,
           L0_attn_b_W1, L0_attn_b_b1, L0_attn_b_W2, L0_attn_b_b2,
           L0_ffn_wqkv, L0_ffn_q_scale, L0_ffn_k_scale, L0_ffn_wo,
           L0_ffn_ln_g, L0_ffn_ln_b, L0_ffn_w1, L0_ffn_b1, L0_ffn_w2,
           L0_ffn_b2):
    nc = _get_built()
    x = np.ascontiguousarray(np.asarray(L0_attn_a_x, dtype=np.float32))
    com = {
        "aw_qkv": np.ascontiguousarray(L0_attn_a_w_qkv, dtype=np.float32),
        "ab_qkv": np.ascontiguousarray(L0_attn_a_b_qkv, dtype=np.float32),
        "aw_out": np.ascontiguousarray(L0_attn_a_w_out, dtype=np.float32),
        "ab_out": np.ascontiguousarray(L0_attn_a_b_out, dtype=np.float32),
        "f_wqkv": np.ascontiguousarray(L0_ffn_wqkv, dtype=np.float32),
        "f_qs": np.ascontiguousarray(L0_ffn_q_scale, dtype=np.float32),
        "f_ks": np.ascontiguousarray(L0_ffn_k_scale, dtype=np.float32),
        "f_wo": np.ascontiguousarray(L0_ffn_wo, dtype=np.float32),
        "f_lg": np.ascontiguousarray(L0_ffn_ln_g, dtype=np.float32),
        "f_lb": np.ascontiguousarray(L0_ffn_ln_b, dtype=np.float32),
        "f_w1": np.ascontiguousarray(L0_ffn_w1, dtype=np.float32),
        "f_b1": np.ascontiguousarray(L0_ffn_b1, dtype=np.float32),
        "f_w2": np.ascontiguousarray(L0_ffn_w2, dtype=np.float32),
        "f_b2": np.ascontiguousarray(L0_ffn_b2, dtype=np.float32),
        "b_womT": np.ascontiguousarray(np.asarray(L0_attn_b_W_omega).T,
                                       dtype=np.float32),
        "b_bom": np.ascontiguousarray(L0_attn_b_b_omega, dtype=np.float32),
        "b_ls": np.ascontiguousarray(L0_attn_b_log_scale, dtype=np.float32),
        "b_lg": np.ascontiguousarray(L0_attn_b_ln_gamma, dtype=np.float32),
        "b_lb": np.ascontiguousarray(L0_attn_b_ln_beta, dtype=np.float32),
        "b_w1T": np.ascontiguousarray(np.asarray(L0_attn_b_W1).T,
                                      dtype=np.float32),
        "b_b1": np.ascontiguousarray(L0_attn_b_b1, dtype=np.float32),
        "b_w2T": np.ascontiguousarray(np.asarray(L0_attn_b_W2).T,
                                      dtype=np.float32),
        "b_b2": np.ascontiguousarray(L0_attn_b_b2, dtype=np.float32),
    }
    in_maps = []
    for c in range(8):
        m = dict(com)
        m["x_in"] = x[2 * c:2 * c + 2]
        in_maps.append(m)
    res = run_bass_kernel_spmd(nc, in_maps, core_ids=list(range(8)))
    return np.concatenate([r["out"] for r in res.results], axis=0)
